# version 13
# baseline (speedup 1.0000x reference)
import numpy as np
import ml_dtypes
from collections import defaultdict

import concourse.bass as bass
import concourse.mybir as mybir
from concourse.bass_core_rust import IndirectOffsetOnAxis
from concourse.bass_utils import run_bass_kernel_spmd

F32 = mybir.dt.float32
BF16 = mybir.dt.bfloat16
I32 = mybir.dt.int32
U32 = mybir.dt.uint32
AF = mybir.ActivationFunctionType
ALU = mybir.AluOpType
AX = mybir.AxisListType
BF16NP = ml_dtypes.bfloat16


class CFG:
    def __init__(self, S=8192, FH=3072, MH=4096, layers=(0, 1), debug_taps=()):
        self.S, self.FH, self.MH = S, FH, MH
        self.D, self.H, self.DH, self.E, self.B, self.NC = 1024, 16, 64, 8, 2, 8
        self.layers = tuple(layers)
        self.debug_taps = tuple(debug_taps)
        self.TPC = self.S * self.B // self.NC
        self.CB = min(512, self.TPC)
        self.NCB = self.TPC // self.CB
        self.QW = self.CB
        self.NQB = self.TPC // self.QW
        self.NTB = self.TPC // 128
        self.TP = min(1024, self.TPC)
        self.NTP = self.TPC // self.TP
        self.YW = min(512, self.TP)
        self.NYW = self.TP // self.YW
        self.FCH = self.D // 128
        self.NHB_F = self.FH // 128
        self.NHB_M = self.MH // 128
        self.KTPR = self.TPC // 128
        self.sparse_moe = True
        cap = int(np.ceil(self.TPC * 0.375 / 128.0)) * 128
        self.CAP = max(128, cap)
        self.CAPC = self.CAP // 128
        self.CAPW = [512] * (self.CAP // 512) + ([self.CAP % 512] if self.CAP % 512 else [])
        assert self.FH % 256 == 0 and self.MH % 512 == 0 and self.TPC % 128 == 0


ENGINES = ("tensor", "vector", "scalar", "sync", "gpsimd")


def tmax(*toks):
    best = {}
    for t in toks:
        if t is None:
            continue
        s, v = t
        if v > best.get(s, -1):
            best[s] = v
    return [(s, v) for s, v in best.items()]


class Prog:
    def __init__(self):
        self.ev = {e: [] for e in ENGINES}
        self.cnt = defaultdict(int)
        self.sems = set()
        self.bank_r = {}

    def emit(self, eng, fn, waits=(), inc=None, amt=1):
        w = [t for t in waits if t is not None]
        tok = None
        if inc is not None:
            self.sems.add(inc)
            self.cnt[inc] += amt
            tok = (inc, self.cnt[inc])
        self.ev[eng].append((fn, w, inc, amt))
        return tok

    def replay_into(self, sems):
        waited = {e: defaultdict(int) for e in ENGINES}

        def mk(eng_name):
            def run(eng):
                for fn, waits, inc, amt in self.ev[eng_name]:
                    for s, v in waits:
                        if v > waited[eng_name][s]:
                            eng.wait_ge(sems[s], v)
                            waited[eng_name][s] = v
                    inst = fn(eng)
                    if inc is not None:
                        inst.then_inc(sems[inc], amt)
            return run
        return {e: mk(e) for e in ENGINES}


class Stager:

    def __init__(self, P, tag, bufs, bufs_i32, cap):
        self.P, self.tag, self.cap = P, tag, cap
        self.bufs, self.bufs_i32 = bufs, bufs_i32
        self.free = [[] for _ in bufs]
        self.i = 0

    def gate(self, toks):
        self.free = [list(toks) for _ in self.bufs]

    def xfer(self, src, slab, gsz, n_g, waits, sqs, i32=False):
        maxg = max(1, self.cap // gsz)
        slab2 = slab.ap().rearrange("p g c -> p (g c)") if n_g > 1 else slab[:]
        tok = None
        g0 = 0
        while g0 < n_g:
            g1 = min(n_g, g0 + maxg)
            s = self.i % len(self.bufs)
            self.i += 1
            stg = (self.bufs_i32 if i32 else self.bufs)[s]
            n = (g1 - g0) * gsz
            if n_g > 1:
                dst = stg[:, 0:n].rearrange("p (g c) -> p g c", c=gsz)
                sp = src[:, g0:g1, :]
            else:
                dst = stg[:, 0:n]
                sp = src
            q = sqs[s % len(sqs)]
            d = self.P.emit(q,
                            lambda e, o=dst, i_=sp: e.dma_start(out=o, in_=i_),
                            waits=self.free[s], inc=f"{self.tag}{q[0]}{s}",
                            amt=16)
            c = self.P.emit("vector",
                            lambda e, o=slab2[:, g0 * gsz:g0 * gsz + n],
                            i_=stg[:, 0:n]: e.tensor_copy(o, i_),
                            waits=[d] + (list(waits) if g0 == 0 else []),
                            inc="DV")
            self.free[s] = [c]
            tok = c
            g0 = g1
        return tok


class Stream:

    def __init__(self, P, q, tag, bufs, stager=None, gsz=None, n_g=1,
                 sqs=("sync", "sync"), i32=False):
        self.P, self.q, self.bufs = P, q, bufs
        self.tag = tag
        self.i = 0
        self.free = [[] for _ in bufs]
        self.ld = [None] * len(bufs)
        self.stager, self.gsz, self.n_g = stager, gsz, n_g
        self.sqs, self.i32 = sqs, i32

    def load(self, src_ap):
        s = self.i % len(self.bufs)
        self.i += 1
        buf = self.bufs[s]
        if self.stager is not None:
            self.ld[s] = self.stager.xfer(src_ap, buf, self.gsz, self.n_g,
                                          self.free[s], self.sqs, self.i32)
        else:
            self.ld[s] = self.P.emit(
                self.q, lambda e, d=buf, src=src_ap: e.dma_start(out=d[:], in_=src),
                waits=self.free[s], inc=f"{self.tag}{s}", amt=16)
        return buf, s

    def release(self, s, tok):
        self.free[s] = [tok]

    def gate(self, tok):
        self.free = [[tok] for _ in self.bufs]



def slabize(wT, nsl):
    G = wT.shape[0] // 128
    W = wT.shape[1] // nsl
    return np.ascontiguousarray(
        wT.reshape(G, 128, nsl, W).transpose(2, 1, 0, 3).reshape(nsl * 128, G * W))


def host_prepare(cfg, inp):
    D, H, DH, E, S = cfg.D, cfg.H, cfg.DH, cfg.E, cfg.S
    sh = {}
    if 0 in cfg.layers:
        sh["l0_wq"] = slabize(
            np.asarray(inp["L0_attn_W_Q"]).transpose(2, 0, 1).reshape(D, H * DH), 8)
        sh["l0_wk"] = slabize(
            np.asarray(inp["L0_attn_W_K"]).transpose(2, 0, 1).reshape(D, H * DH), 8)
        sh["l0_wv"] = slabize(
            np.asarray(inp["L0_attn_W_V"]).transpose(2, 0, 1).reshape(D, H * DH), 2)
        sh["l0_wo"] = np.ascontiguousarray(np.asarray(inp["L0_attn_W_O"]).T)
        sh["l0_wg"] = slabize(np.asarray(inp["L0_ffn_w_gate"]).T, cfg.NHB_F)
        sh["l0_wu"] = slabize(np.asarray(inp["L0_ffn_w_up"]).T, cfg.NHB_F)
        sh["l0_wd"] = slabize(np.asarray(inp["L0_ffn_w_down"]).T, cfg.FCH)
    if 1 in cfg.layers:
        wqkv = np.asarray(inp["L1_attn_Wqkv"]).reshape(H, 3 * DH, D)
        bqkv = np.asarray(inp["L1_attn_bqkv"]).reshape(H, 3 * DH)
        sh["l1_wq"] = slabize(
            wqkv[:, 0:DH, :].transpose(2, 0, 1).reshape(D, H * DH), 8)
        sh["l1_wk"] = slabize(
            wqkv[:, DH:2 * DH, :].transpose(2, 0, 1).reshape(D, H * DH), 8)
        sh["l1_wv"] = slabize(
            wqkv[:, 2 * DH:, :].transpose(2, 0, 1).reshape(D, H * DH), 2)
        sh["l1_bq"] = np.ascontiguousarray(bqkv[:, 0:DH].reshape(8, 128).T)
        sh["l1_bk"] = np.ascontiguousarray(bqkv[:, DH:2 * DH].reshape(8, 128).T)
        sh["l1_bv"] = np.ascontiguousarray(bqkv[:, 2 * DH:].reshape(1, H * DH))
        sh["l1_wo"] = np.ascontiguousarray(np.asarray(inp["L1_attn_Wo"]).T)
        sh["l1_bo"] = np.ascontiguousarray(
            np.asarray(inp["L1_attn_bo"]).reshape(8, 128).T)
        sh["l1_rw"] = np.ascontiguousarray(np.asarray(inp["L1_ffn_router_w"]).T)
        w1 = np.asarray(inp["L1_ffn_w1"])
        b1 = np.asarray(inp["L1_ffn_b1"])
        w2 = np.asarray(inp["L1_ffn_w2"])
        w1aug = np.zeros((E, D + 128, cfg.MH), np.float32)
        for e in range(E):
            w1aug[e, :D, :] = w1[e].T
            w1aug[e, D, :] = b1[e]
        sh["l1_w1"] = np.concatenate(
            [slabize(w1aug[e], cfg.NHB_M) for e in range(E)], axis=0)
        sh["l1_b1c"] = np.ascontiguousarray(
            b1.reshape(E, cfg.MH // 128, 128).transpose(2, 0, 1).reshape(128, -1))
        NH2 = cfg.NHB_M // 2
        w2T = w2.transpose(0, 2, 1)
        sh["l1_w2"] = np.ascontiguousarray(
            w2T.reshape(E, NH2, 2, 128, 2, 512).transpose(0, 4, 1, 3, 2, 5)
            .reshape(E * 2 * NH2 * 128, 2 * 512))
        sh["l1_b2"] = np.ascontiguousarray(np.asarray(inp["L1_ffn_b2"]))

    sh["fconst"] = np.broadcast_to(
        np.array([0.0, 1.0, 1e-5, -0.5, 0.125, 0.5, -1.0, 1.0 / 127.0],
                 np.float32), (128, 8)).copy()
    sh["ones_col"] = np.ones((128, 1), np.float32)
    sh["ones_r1"] = np.ones((1, 128), np.float32)
    sh["ones_r1_bf"] = np.ones((1, 128), BF16NP)
    sh["ones_row_bf"] = np.ones((1, 512), BF16NP)
    sh["iota_e"] = np.broadcast_to(np.arange(8, dtype=np.float32), (128, 8)).copy()
    sh["iota_col"] = np.arange(128, dtype=np.float32).reshape(128, 1)
    sh["ecap1"] = (np.arange(8, dtype=np.float32) * cfg.CAP - 1.0).reshape(8, 1)
    sh["ecap2"] = (np.arange(8, dtype=np.float32) * cfg.CAP + cfg.CAP).reshape(8, 1)
    sel8 = np.zeros((8, 8 * 128), np.float32)
    for e in range(8):
        sel8[e, e * 128:(e + 1) * 128] = 1.0
    sh["sel8"] = sel8
    sh["ident"] = np.eye(128, dtype=np.float32)

    x = np.asarray(inp["L0_attn_x"])
    in_maps, idxs = [], []
    kk = np.arange(128)[:, None]
    qq = np.arange(cfg.QW)[None, :]
    for c in range(8):
        b, j = divmod(c, 4)
        idx = j + 4 * np.arange(cfg.TPC)
        idxs.append((b, idx))
        m = dict(sh)
        m["x_fm"] = np.ascontiguousarray(x[b, idx, :].T)
        masks = np.zeros((128, 4 * cfg.QW), BF16NP)
        for r in range(4):
            d = 0 if r <= j else 1
            masks[:, r * cfg.QW:(r + 1) * cfg.QW] = (kk <= qq - d)
        m["masks"] = masks
        in_maps.append(m)
    return in_maps, idxs


def host_assemble(cfg, results, idxs, key="out_fm"):
    out = np.zeros((cfg.B, cfg.S, cfg.D), np.float32)
    for c in range(8):
        b, idx = idxs[c]
        out[b, idx, :] = results[c][key].T
    return out


OFF_END = 212832


def build_program(cfg, layers):
    nc = bass.Bass("TRN2", target_bir_lowering=False, debug=False)
    D, H, DH, E, TPC = cfg.D, cfg.H, cfg.DH, cfg.E, cfg.TPC
    FCH, CB, NCB, QW, NQB = cfg.FCH, cfg.CB, cfg.NCB, cfg.QW, cfg.NQB
    TP, NTP, YW, NYW, KTPR = cfg.TP, cfg.NTP, cfg.YW, cfg.NYW, cfg.KTPR
    NTB = cfg.NTB

    OFF_A = 0
    OFF_B = OFF_A + FCH * TPC * 4
    OFF_C = OFF_B + FCH * TPC * 2
    OFF_D = OFF_C + FCH * TPC * 2
    OFF_SM = OFF_END - 19456

    din = {}
    def dram_in(name, shape, dt):
        din[name] = nc.dram_tensor(name, list(shape), dt, kind="ExternalInput").ap()

    dram_in("x_fm", (D, TPC), F32)
    dram_in("fconst", (128, 8), F32)
    for n, s, dt in [("ones_col", (128, 1), F32), ("ones_r1", (1, 128), F32),
                     ("ones_r1_bf", (1, 128), BF16), ("ones_row_bf", (1, 512), BF16),
                     ("iota_e", (128, 8), F32), ("ident", (128, 128), F32),
                     ("masks", (128, 4 * QW), BF16), ("sel8", (8, 1024), F32),
                     ("iota_col", (128, 1), F32), ("ecap1", (8, 1), F32),
                     ("ecap2", (8, 1), F32)]:
        dram_in(n, s, dt)
    if 0 in layers:
        for n, s, dt in [("l0_wq", (D, D), F32), ("l0_wk", (D, D), F32),
                         ("l0_wv", (2 * 128, FCH * 512), F32),
                         ("l0_wo", (D, D), F32),
                         ("l0_wg", (cfg.NHB_F * 128, FCH * 128), I32),
                         ("l0_wu", (cfg.NHB_F * 128, FCH * 128), I32),
                         ("l0_wd", (FCH * 128, cfg.NHB_F * 128), I32)]:
            dram_in(n, s, dt)
    if 1 in layers:
        for n, s, dt in [("l1_wq", (D, D), F32), ("l1_wk", (D, D), F32),
                         ("l1_wv", (2 * 128, FCH * 512), F32),
                         ("l1_bq", (128, 8), F32),
                         ("l1_bk", (128, 8), F32), ("l1_bv", (1, D), F32),
                         ("l1_wo", (D, D), F32), ("l1_bo", (128, 8), F32),
                         ("l1_rw", (D, 8), F32),
                         ("l1_w1", (E * cfg.NHB_M * 128, 9 * 128), F32),
                         ("l1_b1c", (128, E * cfg.MH // 128), F32),
                         ("l1_w2", (E * cfg.NHB_M * 128, 2 * 512), F32),
                         ("l1_b2", (E, D), F32)]:
            dram_in(n, s, dt)

    out_fm = nc.dram_tensor("out_fm", [D, TPC], F32, kind="ExternalOutput").ap()
    gate_tap = (nc.dram_tensor("tap_gates", [8, TPC], F32, kind="ExternalOutput").ap()
                if "gates" in cfg.debug_taps else None)
    h_tap = (nc.dram_tensor("tap_h", [cfg.MH, TP], F32, kind="ExternalOutput").ap()
             if "moe_h" in cfg.debug_taps else None)
    taps = {t: nc.dram_tensor("tap_" + t, [D, TPC], F32, kind="ExternalOutput").ap()
            for t in cfg.debug_taps if t != "gates"}

    k_contrib = nc.dram_tensor("k_contrib", [D + 1, TPC], BF16, kind="Internal").ap()
    k_gath = nc.dram_tensor("k_gath", [4 * (D + 1), TPC], BF16, kind="Internal").ap()
    v_contrib = nc.dram_tensor("v_contrib", [4 * TPC, 260], BF16, kind="Internal").ap()
    v_gath = nc.dram_tensor("v_gath", [16 * TPC, 260], BF16, kind="Internal").ap()
    q_park = nc.dram_tensor("q_park", [D, TPC], BF16, kind="Internal").ap()
    CAP, CAPC = cfg.CAP, cfg.CAPC
    NCH = E * CAPC
    if cfg.sparse_moe and 1 in layers:
        xn_tm = nc.dram_tensor("xn_tm", [TPC, D], BF16, kind="Internal").ap()
        cT_dram = nc.dram_tensor("cT_dram", [E * TPC, 1], F32, kind="Internal").ap()
        idx_dram = nc.dram_tensor("idx_dram", [NCH * 128, 1], F32, kind="Internal").ap()
        idxA_dram = nc.dram_tensor("idxA_dram", [NCH * 128, 1], F32, kind="Internal").ap()
        idxB_dram = nc.dram_tensor("idxB_dram", [NCH * 128, 1], F32, kind="Internal").ap()
        y1_dram = nc.dram_tensor("y1_dram", [TPC, D], F32, kind="Internal").ap()
        y2_dram = nc.dram_tensor("y2_dram", [TPC, D], F32, kind="Internal").ap()

    SBUF_BASE = 16512

    def sb(name, shape, dt, off):
        nbytes = int(np.prod(shape[1:])) * mybir.dt.size(dt)
        assert off >= 0 and off + nbytes <= OFF_END, (name, off, nbytes)
        return nc.alloc_sbuf_tensor_at(name, list(shape), dt,
                                       offset=SBUF_BASE + off)

    xacc = [sb(f"xacc{i}", (128, TPC), F32, OFF_A + i * TPC * 4) for i in range(FCH)]
    xn = [sb(f"xn{i}", (128, TPC), BF16, OFF_B + i * TPC * 2) for i in range(FCH)]
    kwin = [sb(f"kw{i}", (128, TPC), BF16, OFF_B + i * TPC * 2) for i in range(8)]
    q2 = [sb(f"q2_{i}", (128, TPC), BF16, OFF_C + i * TPC * 2) for i in range(8)]

    sm = [OFF_SM]
    def sbs(name, shape, dt):
        h = sb(name, shape, dt, sm[0])
        sm[0] += (int(np.prod(shape[1:])) * mybir.dt.size(dt) + 31) & ~31
        assert sm[0] <= OFF_END, name
        return h

    fconst_sb = sbs("c_fconst", (128, 8), F32)
    FCONSTS = [0.0, 1.0, 1e-5, -0.5, 0.125, 0.5, -1.0, 1.0 / 127.0]
    for i, v in enumerate(FCONSTS):
        nc.const_aps.register(F32, v, fconst_sb[:, i:i + 1])
    ones_col = sbs("c_ones", (128, 1), F32)
    ones_r1 = sbs("c_or1", (1, 128), F32)
    ones_r1_bf = sbs("c_or1b", (1, 128), BF16)
    ones_row_bf = sbs("c_orow", (1, 512), BF16)
    iota_e = sbs("c_iota", (128, 8), F32)
    sel8_sb = sbs("c_sel8", (8, 1024), F32)
    iota_col = sbs("c_iotac", (128, 1), F32)
    ecap1_sb = sbs("c_ecap1", (8, 1), F32)
    ecap2_sb = sbs("c_ecap2", (8, 1), F32)
    ident = sbs("c_ident", (128, 128), F32)
    masks_sb = sbs("c_masks", (128, 4 * QW), BF16)
    bq_sb = sbs("bq", (128, 8), F32)
    bk_sb = sbs("bk", (128, 8), F32)
    bo_sb = sbs("bo", (128, 8), F32)
    bv_sb = sbs("bv", (1, D), BF16)
    rw_sb = sbs("rw", (128, 64), BF16)
    b1c_sb = sbs("b1c", (128, E * cfg.MH // 128), F32)
    b2_sb = sbs("b2", (8, D), BF16)
    qn_sb = sbs("qn", (1, 8), F32)
    kn_sb = sbs("kn", (1, 8), F32)
    kn4_sb = sbs("kn4", (1, 32), F32)
    ntmp = sbs("ntmp", (1, 16), F32)
    msc = sbs("msc", (1, 8), F32)
    negM1 = sbs("negM1", (1, 1), F32)
    negM = sbs("negM", (128, 1), F32)
    recrow = sbs("recrow", (1, 512), F32)

    AL = lambda x: (x + 31) & ~31
    xsq = sb("xsq", (128, TPC), F32, OFF_D)
    mean_bc = sb("mean_bc", (128, TPC), F32, OFF_D + TPC * 4)
    rstd_bc = sb("rstd_bc", (128, TPC), F32, OFF_D + 2 * TPC * 4)
    mrow = sb("mrow", (1, TPC), F32, OFF_D + 3 * TPC * 4)
    vrow = sb("vrow", (1, TPC), F32, OFF_D + 3 * TPC * 4 + AL(TPC * 4))
    srow = sb("srow", (1, TPC), F32, OFF_D + 3 * TPC * 4 + 2 * AL(TPC * 4))
    o1 = OFF_D + AL(TPC * 4)
    wslab = [sb(f"wsl{i}", (128, FCH, 128), BF16, o1 + i * FCH * 256) for i in range(3)]
    o2 = o1 + 3 * FCH * 256
    vslab = [sb(f"vsl{i}", (128, FCH, 512), BF16, o2 + i * FCH * 1024) for i in range(2)]
    o3 = o2 + 2 * FCH * 1024
    k2tmp = [sb(f"k2t{i}", (128, TPC), BF16, o3 + i * TPC * 2) for i in range(2)]
    o4 = o3 + 2 * TPC * 2
    vstage = [sb(f"vst{i}", (128, 4, 260), BF16, o4 + i * 2080) for i in range(2)]

    HCAP = (8 * TPC) // TP

    def h_tiles(n, tag):
        out = []
        for i in range(n):
            off = OFF_C + i * TP * 2 if i < HCAP else OFF_D + (i - HCAP) * TP * 2
            out.append(sb(f"{tag}{i}", (128, TP), BF16, off))
        return out

    if 0 in layers:
        h_l0 = h_tiles(cfg.NHB_F, "hf")
        f1 = OFF_D + max(0, cfg.NHB_F - HCAP) * TP * 2
        gslab = [sb(f"gsl{i}", (128, FCH, 128), BF16, f1 + i * FCH * 256)
                 for i in range(3)]
        uslab = [sb(f"usl{i}", (128, FCH, 128), BF16, f1 + 3 * FCH * 256 + i * FCH * 256)
                 for i in range(3)]
        g_sb = [sb(f"g_s{i}", (128, CB), BF16, f1 + 6 * FCH * 256 + i * CB * 2)
                for i in range(2)]
        dslab = [sb(f"dsl{i}", (128, cfg.NHB_F, 128), BF16,
                    f1 + 6 * FCH * 256 + 2 * CB * 2 + i * cfg.NHB_F * 256)
                 for i in range(3)]
    if 1 in layers:
        h_moe = h_tiles(cfg.NHB_M, "hm")
        m0 = OFF_D + max(0, cfg.NHB_M - HCAP) * TP * 2
        m1slab = [sb(f"m1s{i}", (128, 9, 128), BF16, m0 + i * 9 * 256)
                  for i in range(3)]
        m1e = m0 + 3 * 9 * 256
        m2slab = [sb(f"m2s{i}", (128, 2, 512), BF16, m1e + i * 2 * 1024)
                  for i in range(2)]
        m2e = m1e + 2 * 2 * 1024
        gb_sb = [sb("gb0", (128, TP), F32, m2e)]
        gbe = m2e + TP * 4
        cT_sb = sb("cT", (8, TPC), F32, gbe)
        cTb_sb = sb("cTb", (8, TPC), BF16, gbe + TPC * 4)
        rl_buf = sb("rl0", (128, CB), BF16, gbe + TPC * 4 + TPC * 2)
        r0 = gbe + TPC * 4 + TPC * 2 + CB * 2
        if cfg.sparse_moe:
            pA_sb = sb("pA", (8, TPC), F32, OFF_C)
            pB_sb = sb("pB", (8, TPC), F32, OFF_C + TPC * 4)
            mA_sb = sb("mA", (8, TPC), F32, OFF_C + 2 * TPC * 4)
            mB_sb = sb("mB", (8, TPC), F32, OFF_C + 3 * TPC * 4)
            HCAP_S = (8 * TPC * 2) // (CAP * 2)
            h_sp = []
            for i in range(cfg.NHB_M):
                off = OFF_C + i * CAP * 2 if i < HCAP_S \
                    else OFF_D + (i - HCAP_S) * CAP * 2
                h_sp.append(sb(f"hs{i}", (128, CAP), BF16, off))
            sp0 = OFF_D + max(0, cfg.NHB_M - HCAP_S) * CAP * 2
            m1slab_s = [sb(f"m1ss{i}", (128, 9, 128), BF16, sp0 + i * 9 * 256)
                        for i in range(3)]
            sp1 = sp0 + 3 * 9 * 256
            m2slab_s = [sb(f"m2ss{i}", (128, 2, 512), BF16, sp1 + i * 2 * 1024)
                        for i in range(2)]
            sp2 = sp1 + 2 * 2 * 1024
            xe_g = [sb(f"xeg{i}", (128, D), BF16, sp2 + i * D * 2) for i in range(2)]
            sp3 = sp2 + 2 * D * 2
            xe_f = sb("xef", (128, D), F32, sp3)
            sp4 = sp3 + D * 4
            gbs_sb = sb("gbs", (128, CAP), F32, sp4)
            sp5 = sp4 + CAP * 4
            grow = sb("grow", (1, CAP), F32, sp5)
            sp6 = sp5 + ((CAP * 4 + 31) & ~31)
            growb = sb("growb", (1, CAP), BF16, sp6)
            sp7 = sp6 + ((CAP * 2 + 31) & ~31)
            gsc = sb("gsc", (128, 128), F32, sp7)
            sp8 = sp7 + 128 * 4
            b2row = [sb(f"b2r{i}", (1, D), BF16, sp8 + i * D * 2) for i in range(2)]
            sp9 = sp8 + 2 * D * 2
            y_fm = [sb(f"yfm{i}", (128, CAP), BF16, sp9 + i * CAP * 2)
                    for i in range(4)]
            spA = sp9 + 4 * CAP * 2
            ytm = [sb(f"ytm{i}", (128, 512), BF16, spA + i * 1024) for i in range(2)]
            spB = spA + 2 * 1024
            cnv = sb("cnv", (128, 128), F32, spB)
            spC = spB + 128 * 4
            zscan = sb("zscan", (8, CB), F32, spC)
            spD = spC + CB * 4
            carry = sb("carry", (8, 1), F32, spD)
            slotc = [sb(f"slc{i}", (128, 1), F32, spD + 32 + i * 32) for i in range(2)]
            slotu = [sb(f"slu{i}", (128, 1), U32, spD + 96 + i * 32) for i in range(2)]
            tokid = sb("tokid", (128, 1), F32, spD + 160)
            gcol_u = sb("gcol_u", (128, 1), U32, spD + 192)
            spE = spD + 224
            oobfill = sb("oobf", (128, NCH), F32, spE)
            spF = spE + ((NCH * 4 + 31) & ~31)
            idx_ld = sb("idx_ld", (NCH, 128), F32, spF)
            spG = spF + 512
            idxu = sb("idxu", (128, NCH), U32, spG)
            idxAu = sb("idxAu", (128, NCH), U32, spG + ((NCH * 4 + 31) & ~31))
            idxBu = sb("idxBu", (128, NCH), U32, spG + 2 * ((NCH * 4 + 31) & ~31))
            idxA2 = sb("idxA2", (128, NCH), U32, spG + 3 * ((NCH * 4 + 31) & ~31))
            idxB2 = sb("idxB2", (128, NCH), U32, spG + 4 * ((NCH * 4 + 31) & ~31))
            spH = spG + 5 * ((NCH * 4 + 31) & ~31)
            xe_need = FCH * CAP * 2
            if xe_need <= TPC * 6:
                xe_fm = [sb(f"xefm{i}", (128, CAP), BF16, gbe + i * CAP * 2)
                         for i in range(FCH)]
            else:
                xe_fm = [sb(f"xefm{i}", (128, CAP), BF16, spH + i * CAP * 2)
                         for i in range(FCH)]
                spH += xe_need
            tm_st = [sb(f"tmst{i}", (128, D), BF16, o4 + 2 * 2080 + i * D * 2)
                     for i in range(2)]
            assert spH <= OFF_SM, (spH, OFF_SM)
            sq_lo = o4 + 2 * 2080 + 2 * D * 2
            if 0 in layers:
                sq_lo = max(sq_lo, f1 + 6 * FCH * 256 + 2 * CB * 2
                            + 3 * cfg.NHB_F * 256)
            sq_lo = AL(sq_lo)
            sq_cap = (((OFF_SM - sq_lo) // 2) // 4) & ~7
            assert sq_cap >= 640, (sq_lo, OFF_SM)
            stga_bufs = [sb(f"stga{i}", (128, sq_cap), F32,
                            sq_lo + i * sq_cap * 4) for i in range(2)]
            stga_bufs_i = [sb(f"stgai{i}", (128, sq_cap), I32,
                             sq_lo + i * sq_cap * 4) for i in range(2)]
            sm_cap = ((FCH * TPC * 2 // 2) // 4) & ~7
            stgm_bufs = [sb(f"stgm{i}", (128, sm_cap), F32,
                            OFF_B + i * sm_cap * 4) for i in range(2)]
        lg_sb = sb("lg", (128, 8), F32, r0)
        r_m8 = sb("r_m8", (128, 8), F32, r0 + 32)
        r_i8 = sb("r_i8", (128, 8), U32, r0 + 64)
        r_if = sb("r_if", (128, 2), F32, r0 + 96)
        r_oh1 = sb("r_oh1", (128, 8), F32, r0 + 128)
        r_oh2 = sb("r_oh2", (128, 8), F32, r0 + 160)
        r_neg = sb("r_neg", (128, 1), F32, r0 + 192)
        r_p = sb("r_p", (128, 8), F32, r0 + 224)
        r_s = sb("r_s", (128, 1), F32, r0 + 256)
        r_t = sb("r_t", (128, 1), F32, r0 + 288)
        r_cf = sb("r_cf", (128, 8), F32, r0 + 320)

    vwstride = (KTPR * 520 + 31) & ~31
    vw = [sb(f"vw{i}", (128, KTPR, 260), BF16, OFF_D + i * vwstride)
          for i in range(4)]
    a0 = OFF_D + 4 * vwstride
    z_sb = [sb(f"z{i}", (128, TPC), BF16, a0 + i * TPC * 2) for i in range(2)]
    a1 = a0 + 2 * TPC * 2
    p_buf = [sb(f"pbf{i}", (128, 2 * QW), BF16, a1 + i * 2 * QW * 2)
             for i in range(2)]
    a3 = a1 + 5 * QW * 2
    oslab = [sb(f"osl{i}", (128, D), BF16, a3 + i * D * 2) for i in range(4)]
    a4 = a3 + 4 * D * 2
    recb = sb("recb", (64, 512), F32, a4)
    assert a4 + 2048 <= OFF_SM, (a4, OFF_SM)

    pq = [nc.alloc_psum_tensor(f"pq{i}", [128, 1024], F32) for i in range(2)]
    pb_s = [nc.alloc_psum_tensor(f"bank{i}", [128, 512], F32) for i in range(4, 8)]
    pbk = [pq[0][:, 0:512], pq[0][:, 512:1024],
           pq[1][:, 0:512], pq[1][:, 512:1024]] + [h[:] for h in pb_s]

    P = Prog()

    BNDREG = {}

    def bndreg(e, v):
        if v not in BNDREG:
            r = e.alloc_register(f"bnd{v}")
            e.reg_mov(r, v)
            BNDREG[v] = r
        return BNDREG[v]

    def dma(q, dst, src, waits=(), sem="MISC"):
        return P.emit(q, lambda e, d=dst, s=src: e.dma_start(out=d, in_=s),
                      waits=waits, inc=sem, amt=16)

    def mm(out, lhsT, rhs, start, stop, waits=(), inc=None):
        return P.emit("tensor",
                      lambda e, o=out, l=lhsT, r=rhs, a=start, b=stop:
                      e.matmul(o, l, r, start=a, stop=b),
                      waits=waits, inc=inc)

    def bk_acq(i, waits=()):
        t = P.bank_r.get(i)
        return (list(waits) + [t]) if t else list(waits)

    def bk_read(i, tok):
        P.bank_r[i] = tok

    def vec(fn, waits=(), inc="DV"):
        return P.emit("vector", fn, waits=waits, inc=inc)

    def act(fn, waits=(), inc="AC"):
        return P.emit("scalar", fn, waits=list(waits) + [cst], inc=inc)

    dma("sync", fconst_sb[:], din["fconst"], sem="CST")
    for h, name in [(ones_col, "ones_col"), (ones_r1, "ones_r1"),
                    (iota_e, "iota_e"), (ident, "ident"),
                    (ones_r1_bf, "ones_r1_bf"), (ones_row_bf, "ones_row_bf"),
                    (masks_sb, "masks"), (sel8_sb, "sel8"),
                    (iota_col, "iota_col"), (ecap1_sb, "ecap1"),
                    (ecap2_sb, "ecap2")]:
        dma("sync", h[:], din[name], sem="CST")
    if 1 in layers:
        for h, name in [(bq_sb, "l1_bq"), (bk_sb, "l1_bk"), (bo_sb, "l1_bo")]:
            dma("sync", h[:], din[name], sem="CST")
        for h, name in [(bv_sb, "l1_bv"), (b2_sb, "l1_b2")]:
            dma("gpsimd", h[:], din[name], sem="CSG")
        dma("sync", b1c_sb[:], din["l1_b1c"], sem="CST")
        dma("gpsimd", rw_sb.ap().rearrange("p (fc e) -> p fc e", e=8),
            din["l1_rw"].rearrange("(fc p) e -> p fc e", p=128), sem="CSG")
    cst = ("CST", P.cnt["CST"])
    cstg = ("CSG", P.cnt["CSG"]) if P.cnt["CSG"] else None

    for fc in range(FCH):
        dma("sync", xacc[fc][:], din["x_fm"][fc * 128:(fc + 1) * 128, :], sem="XL")
    xacc_tok = [("XL", P.cnt["XL"])] * FCH

    zt = P.emit("vector", lambda e: e.memset(mrow[0:1, :], 0.0), inc="DV")
    norm_zero = dma("gpsimd", k_contrib[D:D + 1, :], mrow[0:1, :], waits=[zt],
                    sem="KN")

    st = {
        "xacc": xacc_tok,
        "B_free": [None] * 8,
        "q2_free": [None] * 8,
        "h_free": None,
        "arena_free": None,
        "kgath_free": None,
        "vgath_free": None,
        "qpark_free": None,
        "contrib_free": None,
        "k2tmp_free": [None, None],
        "vstage_free": [None, None],
        "xsq_gate": None,
        "wstream": Stream(P, "gpsimd", "sW", wslab),
        "vstream": Stream(P, "gpsimd", "sV", vslab),
        "ostream": Stream(P, "gpsimd", "sO", oslab),
    }
    stagerA = stagerM = None
    if 1 in layers and cfg.sparse_moe:
        stagerA = Stager(P, "qg", stga_bufs, stga_bufs_i, sq_cap)
        stagerM = Stager(P, "mg", stgm_bufs, None, sm_cap)
        st["wstream"] = Stream(P, "gpsimd", "sW", wslab, stager=stagerA,
                               gsz=128, n_g=FCH, sqs=("gpsimd", "gpsimd"))
        st["vstream"] = Stream(P, "gpsimd", "sV", vslab, stager=stagerA,
                               gsz=512, n_g=FCH, sqs=("gpsimd", "gpsimd"))
    st["stagerQ"], st["stagerM"] = stagerA, stagerM
    if 0 in layers:
        st["gs"] = Stream(P, "gpsimd", "sG", gslab, stager=stagerA,
                          gsz=128, n_g=FCH, sqs=("sync", "sync"), i32=True)
        st["us"] = Stream(P, "gpsimd", "sU", uslab, stager=stagerA,
                          gsz=128, n_g=FCH, sqs=("sync", "sync"), i32=True)
        st["ds"] = Stream(P, "gpsimd", "sD", dslab, stager=stagerA,
                          gsz=128, n_g=cfg.NHB_F, sqs=("sync", "sync"),
                          i32=True)
    if 1 in layers:
        st["m1s"] = Stream(P, "gpsimd", "s1", m1slab)
        st["m2s"] = Stream(P, "gpsimd", "s2", m2slab)
        if cfg.sparse_moe:
            st["m1s_s"] = Stream(P, "gpsimd", "t1", m1slab_s,
                                 stager=stagerM, gsz=128, n_g=9)
            st["m2s_s"] = Stream(P, "gpsimd", "t2", m2slab_s,
                                 stager=stagerM, gsz=512, n_g=2)

    def emit_ln(lid):
        arena_w = [st["arena_free"]] if st["arena_free"] else []
        sum_tok = [None] * NCB
        sq_tok_f = [None] * NCB
        last_sq = st["xsq_gate"]
        for fc in range(FCH):
            for tb in range(NCB):
                w = [st["xacc"][fc], cst]
                if fc == 0:
                    w = bk_acq(tb, w)
                t = mm(pbk[tb][0:1, 0:CB], ones_col[:],
                       xacc[fc][:, tb * CB:(tb + 1) * CB],
                       start=(fc == 0), stop=(fc == FCH - 1),
                       waits=w, inc="TE" if fc == FCH - 1 else None)
                if fc == FCH - 1:
                    sum_tok[tb] = t
            w = [st["xacc"][fc], last_sq] + (arena_w if fc == 0 else [])
            s = act(lambda e, fc=fc: e.activation(xsq[:], xacc[fc][:], AF.Square),
                    waits=w)
            for tb in range(NCB):
                w = [s]
                if fc == 0:
                    w = bk_acq(4 + tb, w)
                t = mm(pbk[4 + tb][0:1, 0:CB], ones_col[:],
                       xsq[:, tb * CB:(tb + 1) * CB],
                       start=(fc == 0), stop=(fc == FCH - 1),
                       waits=w, inc="TE")
                if fc == FCH - 1:
                    sq_tok_f[tb] = t
                last_sq = t
        m_toks = []
        for tb in range(NCB):
            t = vec(lambda e, tb=tb: e.tensor_scalar_mul(
                mrow[0:1, tb * CB:(tb + 1) * CB], pbk[tb][0:1, 0:CB], 1.0 / D),
                waits=[sum_tok[tb], norm_zero] + arena_w)
            bk_read(tb, t)
            m_toks.append(t)
        msq = vec(lambda e: e.tensor_mul(srow[0:1, :], mrow[0:1, :], mrow[0:1, :]),
                  waits=m_toks)
        v_toks = []
        for tb in range(NCB):
            t = vec(lambda e, tb=tb: e.scalar_tensor_tensor(
                vrow[0:1, tb * CB:(tb + 1) * CB], pbk[4 + tb][0:1, 0:CB],
                1.0 / D, srow[0:1, tb * CB:(tb + 1) * CB], ALU.mult, ALU.subtract),
                waits=[sq_tok_f[tb], msq])
            bk_read(4 + tb, t)
            v_toks.append(t)
        t_ln = act(lambda e: e.activation(srow[0:1, :], vrow[0:1, :], AF.Ln,
                                          bias=1e-5), waits=v_toks + [msq])
        t_rs = act(lambda e: e.activation(srow[0:1, :], srow[0:1, :], AF.Exp,
                                          scale=-0.5), waits=[t_ln])
        bc = []
        for tb in range(NCB):
            w = bk_acq(tb, [m_toks[tb], cst])
            t1 = mm(pbk[tb][:, 0:CB], ones_r1[0:1, :],
                    mrow[0:1, tb * CB:(tb + 1) * CB], True, True, w, inc="TE")
            w = bk_acq(4 + tb, [t_rs])
            t2 = mm(pbk[4 + tb][:, 0:CB], ones_r1[0:1, :],
                    srow[0:1, tb * CB:(tb + 1) * CB], True, True, w, inc="TE")
            c1 = vec(lambda e, tb=tb: e.tensor_copy(
                mean_bc[:, tb * CB:(tb + 1) * CB], pbk[tb][:, 0:CB]), waits=[t1])
            c2 = vec(lambda e, tb=tb: e.tensor_copy(
                rstd_bc[:, tb * CB:(tb + 1) * CB], pbk[4 + tb][:, 0:CB]), waits=[t2])
            bk_read(tb, c1)
            bk_read(4 + tb, c2)
            bc += [c1, c2]
        xn_tok = []
        for fc in range(FCH):
            t1 = vec(lambda e, fc=fc: e.tensor_sub(xsq[:], xacc[fc][:], mean_bc[:]),
                     waits=bc + [st["xacc"][fc], last_sq])
            w2 = [t1] + ([st["B_free"][fc]] if st["B_free"][fc] else [])
            t2 = vec(lambda e, fc=fc: e.tensor_mul(xn[fc][:], xsq[:], rstd_bc[:]),
                     waits=w2)
            xn_tok.append(t2)
            last_sq = t2
        st["xsq_gate"] = last_sq
        return xn_tok

    def emit_qkv(lid, xn_tok):
        l0 = (lid == 0)
        pre = f"l{lid}_"
        ws = st["wstream"]
        g_ = xn_tok[-1]
        ws.gate(g_)
        st["vstream"].gate(g_)
        if st["stagerQ"] is not None:
            st["stagerQ"].gate([g_])
        st["k2tmp_free"] = [g_] * 2
        st["vstage_free"] = [g_] * 2

        def proj_qk(which):
            contrib_toks = []
            norm_done = None
            for hp in range(8):
                buf, slot = ws.load(
                    din[pre + ("wq" if which == "q" else "wk")]
                    [hp * 128:(hp + 1) * 128, :].rearrange("p (g c) -> p g c", c=128))
                ld = ws.ld[slot]
                if which == "q":
                    dest, dw = q2[hp], st["q2_free"][hp]
                else:
                    dest = k2tmp[hp % 2]
                    dw = contrib_toks[hp - 2] if hp >= 2 else st["k2tmp_free"][hp % 2]
                evac = []
                for tb in range(NCB):
                    bi = tb % 2
                    w = bk_acq(bi, [ld] + xn_tok)
                    t = None
                    for fc in range(FCH):
                        t = mm(pbk[bi][:, 0:CB], buf[:, fc, :],
                               xn[fc][:, tb * CB:(tb + 1) * CB],
                               start=(fc == 0), stop=(fc == FCH - 1),
                               waits=w if fc == 0 else (),
                               inc="TE" if fc == FCH - 1 else None)
                    ew = [t] + ([dw] if dw and tb == 0 else [])
                    if not l0:
                        b = bq_sb if which == "q" else bk_sb
                        et = vec(lambda e, d=dest, tb=tb, bi=bi, b=b, hp=hp:
                                 e.tensor_scalar_add(d[:, tb * CB:(tb + 1) * CB],
                                                     pbk[bi][:, 0:CB],
                                                     b[:, hp:hp + 1]),
                                 waits=ew + [cst])
                    else:
                        et = vec(lambda e, d=dest, tb=tb, bi=bi:
                                 e.tensor_copy(d[:, tb * CB:(tb + 1) * CB],
                                               pbk[bi][:, 0:CB]), waits=ew)
                    bk_read(bi, et)
                    evac.append(et)
                ws.release(slot, evac[-1])
                s = act(lambda e, d=dest: e.activation(xsq[:], d[:], AF.Square),
                        waits=evac + [st["xsq_gate"]])
                red = []
                for tb in range(NCB):
                    w = bk_acq(6, [s])
                    t = mm(pbk[6][0:1, 0:CB], ones_col[:],
                           xsq[:, tb * CB:(tb + 1) * CB], True, True, w, inc="TE")
                    rt = vec(lambda e, tb=tb: e.reduce_max(
                        ntmp[0:1, tb:tb + 1], pbk[6][0:1, 0:CB], AX.X),
                        waits=[t, norm_done])
                    bk_read(6, rt)
                    red.append(rt)
                dstrow = qn_sb if which == "q" else kn_sb
                norm_done = vec(lambda e, d=dstrow, hp=hp: e.reduce_max(
                    d[0:1, hp:hp + 1], ntmp[0:1, 0:NCB], AX.X), waits=red)
                st["xsq_gate"] = s
                st["xsq_gate"] = red[-1]
                if which == "q":
                    pk = dma("sync", q_park[hp * 128:(hp + 1) * 128, :], q2[hp][:],
                             waits=evac + ([st["qpark_free"]] if st["qpark_free"]
                                           else []), sem=f"pk{hp}")
                    st["q2_free"][hp] = pk
                else:
                    cw = [st["contrib_free"]] if st["contrib_free"] else []
                    kc = dma("sync", k_contrib[hp * 128:(hp + 1) * 128, :],
                             k2tmp[hp % 2][:], waits=evac + cw, sem=f"kc{hp % 2}")
                    contrib_toks.append(kc)
            return contrib_toks, norm_done

        _, qn_done = proj_qk("q")
        k_contrib_toks, kn_done = proj_qk("k")
        st["k2tmp_free"] = [k_contrib_toks[-2], k_contrib_toks[-1]]
        kc_norm = dma("gpsimd", k_contrib[D:D + 1, 0:8], kn_sb[0:1, 0:8],
                      waits=[kn_done], sem="KN")
        k_contrib_toks.append(kc_norm)

        vs = st["vstream"]
        ones_done = [None, None]
        for i in range(2):
            prev = st["vstage_free"][i]
            for q4 in range(4):
                prev = vec(lambda e, i=i, q4=q4: e.memset(
                    vstage[i].ap()[:, q4, :].rearrange("p (h c) -> p h c", c=65)
                    [:, :, 64:65], 1.0), waits=[prev])
            ones_done[i] = prev
        v_contrib_toks = []
        ev_last = None
        for vh in range(2):
            buf, slot = vs.load(din[pre + "wv"][vh * 128:(vh + 1) * 128, :]
                                .rearrange("p (g c) -> p g c", c=512))
            ld = vs.ld[slot]
            for tb in range(NTB):
                bi = 2 + tb % 2
                w = bk_acq(bi, [ld] + xn_tok)
                t = None
                for fc in range(FCH):
                    last = (fc == FCH - 1) and l0
                    t = mm(pbk[bi][:, 0:512], xn[fc][:, tb * 128:(tb + 1) * 128],
                           buf[:, fc, :], start=(fc == 0), stop=last,
                           waits=w if fc == 0 else (), inc="TE" if last else None)
                if not l0:
                    t = mm(pbk[bi][:, 0:512], ones_r1_bf[0:1, :],
                           bv_sb[0:1, vh * 512:(vh + 1) * 512], False, True,
                           waits=[cstg], inc="TE")
                si = tb % 2
                sw = [t, ones_done[si]]
                if len(v_contrib_toks) >= 2:
                    sw.append(v_contrib_toks[-2])
                ev = vec(lambda e, si=si, bi=bi, vh=vh: e.tensor_copy(
                    vstage[si].ap()[:, 2 * vh:2 * vh + 2, :]
                    .rearrange("p q (h c) -> p (q h) c", c=65)[:, :, 0:64],
                    pbk[bi][:, 0:512]), waits=sw)
                bk_read(bi, ev)
                cw = [st["contrib_free"]] if st["contrib_free"] else []
                vc = dma("sync",
                         v_contrib.rearrange("(q t p) c -> p q t c", q=4, p=128)
                         [:, 2 * vh:2 * vh + 2, tb, :],
                         vstage[si][:, 2 * vh:2 * vh + 2, :], waits=[ev] + cw,
                         sem=f"vc{si}")
                v_contrib_toks.append(vc)
                ev_last = ev
            vs.release(slot, ev_last)
        st["vstage_free"] = [v_contrib_toks[-2], v_contrib_toks[-1]]

        coll_w = k_contrib_toks + [v_contrib_toks[-1], v_contrib_toks[-2]]
        st["contrib_done"] = tmax(*(k_contrib_toks + v_contrib_toks))
        if st["kgath_free"]:
            coll_w.append(st["kgath_free"])
        ck = P.emit("gpsimd", lambda e: e.collective_compute(
            "AllGather", ALU.bypass, ins=[k_contrib.opt()], outs=[k_gath.opt()],
            replica_groups=[[0, 1, 2, 3], [4, 5, 6, 7]]), waits=coll_w, inc="CCK")
        cv_w = [v_contrib_toks[-1]]
        if st["vgath_free"]:
            cv_w.append(st["vgath_free"])
        cv = P.emit("gpsimd", lambda e: e.collective_compute(
            "AllGather", ALU.bypass, ins=[v_contrib.opt()], outs=[v_gath.opt()],
            replica_groups=[[0, 1, 2, 3], [4, 5, 6, 7]]), waits=cv_w, inc="CCV")
        st["contrib_free"] = cv
        return ck, cv

    def emit_ffn_l0(xn_tok, qkv_gate):
        gs, us, ds = st["gs"], st["us"], st["ds"]
        for _s in (gs, us, ds):
            for i in range(len(_s.free)):
                _s.free[i] = _s.free[i] + qkv_gate if _s.free[i] else list(qkv_gate)
        NHB = cfg.NHB_F
        h_tok = [None] * NHB
        g_free = [None, None]
        for tp in range(NTP):
            for hb in range(NHB):
                b1, s1 = gs.load(din["l0_wg"][hb * 128:(hb + 1) * 128, :]
                                 .rearrange("p (g c) -> p g c", c=128))
                b2, s2 = us.load(din["l0_wu"][hb * 128:(hb + 1) * 128, :]
                                 .rearrange("p (g c) -> p g c", c=128))
                for tbl in range(TP // CB):
                    col0 = tp * TP + tbl * CB
                    bg, bu = tbl % 2, 2 + tbl % 2
                    w = bk_acq(bg, [gs.ld[s1]] + xn_tok)
                    tg = None
                    for fc in range(FCH):
                        tg = mm(pbk[bg][:, 0:CB], b1[:, fc, :],
                                xn[fc][:, col0:col0 + CB],
                                start=(fc == 0), stop=(fc == FCH - 1),
                                waits=w if fc == 0 else (),
                                inc="TE" if fc == FCH - 1 else None)
                    w = bk_acq(bu, [us.ld[s2]])
                    tu = None
                    for fc in range(FCH):
                        tu = mm(pbk[bu][:, 0:CB], b2[:, fc, :],
                                xn[fc][:, col0:col0 + CB],
                                start=(fc == 0), stop=(fc == FCH - 1),
                                waits=w if fc == 0 else (),
                                inc="TE" if fc == FCH - 1 else None)
                    gi = tbl % 2
                    ga = act(lambda e, gi=gi, bg=bg: e.activation(
                        g_sb[gi][:, 0:CB], pbk[bg][:, 0:CB], AF.Silu,
                        scale=1.0 / 127.0), waits=[tg, g_free[gi]])
                    bk_read(bg, ga)
                    hw = [tu, ga]
                    if tp == 0 and tbl == 0:
                        hw += qkv_gate
                        if hb < HCAP:
                            hw.append(st["q2_free"][hb * TP // TPC])
                        elif st["arena_free"]:
                            hw.append(st["arena_free"])
                    ht = vec(lambda e, hb=hb, tbl=tbl, bu=bu, gi=gi:
                             e.scalar_tensor_tensor(
                                 h_l0[hb][:, tbl * CB:(tbl + 1) * CB],
                                 pbk[bu][:, 0:CB], 1.0 / 127.0, g_sb[gi][:, 0:CB],
                                 ALU.mult, ALU.mult), waits=hw)
                    bk_read(bu, ht)
                    g_free[gi] = ht
                    h_tok[hb] = ht
                gs.release(s1, h_tok[hb])
                us.release(s2, h_tok[hb])
            for ob in range(FCH):
                b3, s3 = ds.load(din["l0_wd"][ob * 128:(ob + 1) * 128, :]
                                 .rearrange("p (g c) -> p g c", c=128))
                last_at = None
                for tbl in range(TP // CB):
                    col0 = tp * TP + tbl * CB
                    bd = 4 + tbl % 2
                    w = bk_acq(bd, [ds.ld[s3]] + h_tok)
                    t = None
                    for hb in range(NHB):
                        t = mm(pbk[bd][:, 0:CB], b3[:, hb, :],
                               h_l0[hb][:, tbl * CB:(tbl + 1) * CB],
                               start=(hb == 0), stop=(hb == NHB - 1),
                               waits=w if hb == 0 else (),
                               inc="TE" if hb == NHB - 1 else None)
                    at = vec(lambda e, ob=ob, col0=col0, bd=bd:
                             e.scalar_tensor_tensor(
                                 xacc[ob][:, col0:col0 + CB], pbk[bd][:, 0:CB],
                                 1.0 / 127.0, xacc[ob][:, col0:col0 + CB],
                                 ALU.mult, ALU.add),
                             waits=[t, st["xacc"][ob]])
                    bk_read(bd, at)
                    st["xacc"][ob] = at
                    last_at = at
                ds.release(s3, last_at)
        return

    def emit_moe(xn_tok, qkv_gate):
        NHB = cfg.NHB_M
        for _s in (st["m1s"], st["m2s"]):
            for i in range(len(_s.free)):
                _s.free[i] = _s.free[i] + qkv_gate if _s.free[i] else list(qkv_gate)
        arena_w = [st["arena_free"]] if st["arena_free"] else []
        cT_toks = []
        rprev = None
        for tb in range(NTB):
            w = bk_acq(7, xn_tok + [cstg] + arena_w)
            t = None
            for fc in range(FCH):
                t = mm(pbk[7][:, 0:8], xn[fc][:, tb * 128:(tb + 1) * 128],
                       rw_sb[:, fc * 8:(fc + 1) * 8],
                       start=(fc == 0), stop=(fc == FCH - 1),
                       waits=w if fc == 0 else (), inc="TE" if fc == FCH - 1 else None)
            c0 = vec(lambda e: e.tensor_copy(lg_sb[:], pbk[7][:, 0:8]),
                     waits=[t, rprev])
            bk_read(7, c0)
            t1 = vec(lambda e: e.max(r_m8[:], lg_sb[:]), waits=[c0])
            t2 = vec(lambda e: e.max_index(r_i8[:], r_m8[:], lg_sb[:]), waits=[t1])
            t2b = vec(lambda e: e.tensor_copy(r_if[:], r_i8[:, 0:2]), waits=[t2])
            t3 = vec(lambda e: e.tensor_single_scalar(r_oh1[:], iota_e[:],
                                                      r_if[:, 0:1], ALU.is_equal),
                     waits=[t2b, cst])
            t7 = vec(lambda e: e.tensor_single_scalar(r_oh2[:], iota_e[:],
                                                      r_if[:, 1:2], ALU.is_equal),
                     waits=[t3])
            t8 = vec(lambda e: e.tensor_scalar_mul(r_neg[:], r_m8[:, 0:1], -1.0),
                     waits=[t1])
            t9 = act(lambda e: e.activation(r_p[:], lg_sb[:], AF.Exp,
                                            bias=r_neg[:, 0:1]), waits=[t8, t2])
            ta = vec(lambda e: e.reduce_sum(r_s[:], r_p[:], AX.X), waits=[t9])
            tb_ = vec(lambda e: e.reciprocal(r_t[:], r_s[:]), waits=[ta])
            tc = vec(lambda e: e.tensor_scalar_mul(r_p[:], r_p[:], r_t[:, 0:1]),
                     waits=[tb_])
            td = vec(lambda e: e.tensor_add(r_oh2[:], r_oh1[:], r_oh2[:]),
                     waits=[t7, tc])
            te_ = vec(lambda e: e.tensor_mul(r_p[:], r_p[:], r_oh2[:]), waits=[td])
            tf = vec(lambda e: e.reduce_sum(r_s[:], r_p[:], AX.X), waits=[te_])
            tg = vec(lambda e: e.tensor_scalar_add(r_s[:], r_s[:], 1e-9), waits=[tf])
            th = vec(lambda e: e.reciprocal(r_t[:], r_s[:]), waits=[tg])
            ti = vec(lambda e: e.tensor_scalar_mul(r_cf[:], r_p[:], r_t[:, 0:1]),
                     waits=[th])
            w = bk_acq(6, [ti, cst])
            tj = P.emit("tensor", lambda e: e.transpose(pbk[6][0:8, 0:128],
                                                        r_cf[:], ident[:]),
                        waits=w, inc="TE")
            tk = vec(lambda e, tb=tb: e.tensor_copy(
                cT_sb[0:8, tb * 128:(tb + 1) * 128], pbk[6][0:8, 0:128]),
                waits=[tj] + arena_w)
            bk_read(6, tk)
            tl = vec(lambda e, tb=tb: e.tensor_copy(
                cTb_sb[0:8, tb * 128:(tb + 1) * 128],
                cT_sb[0:8, tb * 128:(tb + 1) * 128]), waits=[tk])
            cT_toks.append(tl)
            rprev = ti

        if gate_tap is not None:
            dma("sync", gate_tap, cT_sb[0:8, :], waits=cT_toks, sem="OUT")
        m1s, m2s = st["m1s"], st["m2s"]
        h_tok = [None] * NHB
        rl_free = [arena_w[0] if arena_w else None]
        gb_tok = [None]
        for e_i in range(E):
            for tp in range(NTP):
                gi = 0
                gt = None
                for yw in range(TP // YW):
                    w = bk_acq(6, cT_toks + ([gb_tok[gi]] if gb_tok[gi] else [])
                               + (arena_w if e_i == 0 else []))
                    t = mm(pbk[6][:, 0:YW],
                           sel8_sb[0:8, e_i * 128:(e_i + 1) * 128],
                           cT_sb[0:8, tp * TP + yw * YW:tp * TP + (yw + 1) * YW],
                           True, True, waits=w, inc="TE")
                    gt = vec(lambda e, gi=gi, yw=yw: e.tensor_copy(
                        gb_sb[gi][:, yw * YW:(yw + 1) * YW], pbk[6][:, 0:YW]),
                        waits=[t])
                    bk_read(6, gt)
                gb_tok[gi] = gt
                for hb in range(NHB):
                    b1, s1 = m1s.load(
                        din["l1_w1"][(e_i * NHB + hb) * 128:
                                     (e_i * NHB + hb + 1) * 128, :]
                        .rearrange("p (g c) -> p g c", c=128))
                    for tbl in range(TP // CB):
                        col0 = tp * TP + tbl * CB
                        bi = tbl % 2
                        w = bk_acq(bi, [m1s.ld[s1]] + xn_tok)
                        t = None
                        for fc in range(FCH):
                            t = mm(pbk[bi][:, 0:CB], b1[:, fc, :],
                                   xn[fc][:, col0:col0 + CB],
                                   start=(fc == 0), stop=(fc == FCH - 1),
                                   waits=w if fc == 0 else (),
                                   inc="TE" if fc == FCH - 1 else None)
                        rt = act(lambda e, bi=bi, e_i=e_i, hb=hb: e.activation(
                            rl_buf[:, 0:CB], pbk[bi][:, 0:CB], AF.Relu,
                            bias=b1c_sb[:, e_i * NHB + hb:e_i * NHB + hb + 1]),
                            waits=[t, rl_free[0]])
                        bk_read(bi, rt)
                        hw = [rt, gb_tok[gi]]
                        if e_i == 0 and tp == 0 and tbl == 0:
                            hw += qkv_gate
                            if hb < HCAP:
                                hw.append(st["q2_free"][hb * TP // TPC])
                            elif st["arena_free"]:
                                hw.append(st["arena_free"])
                        ht = vec(lambda e, hb=hb, tbl=tbl, gi=gi:
                                 e.tensor_mul(
                                     h_moe[hb][:, tbl * CB:(tbl + 1) * CB],
                                     rl_buf[:, 0:CB],
                                     gb_sb[gi][:, tbl * CB:(tbl + 1) * CB]),
                                 waits=hw)
                        rl_free[0] = ht
                        h_tok[hb] = ht
                    m1s.release(s1, h_tok[hb])
                if h_tap is not None and e_i == 3 and tp == 0:
                    for hb in range(NHB):
                        dma("gpsimd", h_tap[hb * 128:(hb + 1) * 128, :],
                            h_moe[hb][:], waits=[h_tok[hb]], sem="HT")
                    htt = ("HT", P.cnt["HT"])
                    h_tok = [htt] * NHB
                ytiles = [(cb, th) for cb in range(4) for th in range(NYW)]
                for ch in range(2):
                    if e_i == 0:
                        for (cb, th) in ytiles:
                            bi = cb * NYW + th
                            w = bk_acq(bi, cT_toks + [cstg])
                            mm(pbk[bi][:, 0:YW],
                               b2_sb[0:8, ch * 512 + cb * 128:ch * 512 + (cb + 1) * 128],
                               cTb_sb[0:8, tp * TP + th * YW:tp * TP + (th + 1) * YW],
                               True, False, waits=w)
                    last = {}
                    for hbg in range(NHB // 2):
                        b2w, s2 = m2s.load(
                            din["l1_w2"]
                            [((e_i * 2 + ch) * (NHB // 2) + hbg) * 128:
                             ((e_i * 2 + ch) * (NHB // 2) + hbg + 1) * 128, :]
                            .rearrange("p (g c) -> p g c", c=512))
                        slab_last = None
                        for g4 in range(2):
                            hb = hbg * 2 + g4
                            for (cb, th) in ytiles:
                                bi = cb * NYW + th
                                isend = (hb == NHB - 1)
                                isfirst = (e_i > 0 and hb == 0)
                                w = [m2s.ld[s2], h_tok[hb]] if (cb, th) == ytiles[0] else ()
                                if isfirst:
                                    w = bk_acq(bi, list(w) if w else [])
                                t = mm(pbk[bi][:, 0:YW],
                                       b2w[:, g4, cb * 128:(cb + 1) * 128],
                                       h_moe[hb][:, th * YW:(th + 1) * YW],
                                       isfirst, isend, waits=w,
                                       inc="TE" if isend or ((g4 == 1) and
                                                             (cb, th) == ytiles[-1])
                                       else None)
                                if isend:
                                    last[(cb, th)] = t
                                if (g4 == 1) and (cb, th) == ytiles[-1]:
                                    slab_last = t
                        m2s.release(s2, slab_last)
                    for (cb, th) in ytiles:
                        bi = cb * NYW + th
                        ob = ch * 4 + cb
                        col0 = tp * TP + th * YW
                        at = vec(lambda e, ob=ob, col0=col0, bi=bi:
                                 e.tensor_add(xacc[ob][:, col0:col0 + YW],
                                              pbk[bi][:, 0:YW],
                                              xacc[ob][:, col0:col0 + YW]),
                                 waits=[last[(cb, th)], st["xacc"][ob]])
                        bk_read(bi, at)
                        st["xacc"][ob] = at
        return

    def emit_moe_sparse(xn_tok, qkv_gate, xn_tm_toks):
        NHB = cfg.NHB_M
        CAPW = cfg.CAPW
        arena_w = [st["arena_free"]] if st["arena_free"] else []
        IOA = IndirectOffsetOnAxis

        cT_toks = []
        mask_toks = []
        rprev = None
        slot_gate = [st["q2_free"][i] for i in range(8) if st["q2_free"][i]]
        for tb in range(NTB):
            w = bk_acq(7, xn_tok + [cstg] + arena_w)
            t = None
            for fc in range(FCH):
                t = mm(pbk[7][:, 0:8], xn[fc][:, tb * 128:(tb + 1) * 128],
                       rw_sb[:, fc * 8:(fc + 1) * 8],
                       start=(fc == 0), stop=(fc == FCH - 1),
                       waits=w if fc == 0 else (), inc="TE" if fc == FCH - 1 else None)
            c0 = vec(lambda e: e.tensor_copy(lg_sb[:], pbk[7][:, 0:8]),
                     waits=[t, rprev])
            bk_read(7, c0)
            t1 = vec(lambda e: e.max(r_m8[:], lg_sb[:]), waits=[c0])
            t2 = vec(lambda e: e.max_index(r_i8[:], r_m8[:], lg_sb[:]), waits=[t1])
            t2b = vec(lambda e: e.tensor_copy(r_if[:], r_i8[:, 0:2]), waits=[t2])
            t3 = vec(lambda e: e.tensor_single_scalar(r_oh1[:], iota_e[:],
                                                      r_if[:, 0:1], ALU.is_equal),
                     waits=[t2b, cst])
            t7 = vec(lambda e: e.tensor_single_scalar(r_oh2[:], iota_e[:],
                                                      r_if[:, 1:2], ALU.is_equal),
                     waits=[t3])
            t8 = vec(lambda e: e.tensor_scalar_mul(r_neg[:], r_m8[:, 0:1], -1.0),
                     waits=[t1])
            t9 = act(lambda e: e.activation(r_p[:], lg_sb[:], AF.Exp,
                                            bias=r_neg[:, 0:1]), waits=[t8, t2])
            ta = vec(lambda e: e.reduce_sum(r_s[:], r_p[:], AX.X), waits=[t9])
            tb_ = vec(lambda e: e.reciprocal(r_t[:], r_s[:]), waits=[ta])
            tc = vec(lambda e: e.tensor_scalar_mul(r_p[:], r_p[:], r_t[:, 0:1]),
                     waits=[tb_])
            w = bk_acq(6, [t7, cst] + (slot_gate if tb == 0 else []))
            tt1 = P.emit("tensor", lambda e: e.transpose(pbk[6][0:8, 0:128],
                                                         r_oh1[:], ident[:]),
                         waits=w, inc="TE")
            mc1 = vec(lambda e, tb=tb: e.tensor_copy(
                mA_sb[0:8, tb * 128:(tb + 1) * 128], pbk[6][0:8, 0:128]),
                waits=[tt1])
            bk_read(6, mc1)
            w = bk_acq(6, [t7])
            tt2 = P.emit("tensor", lambda e: e.transpose(pbk[6][0:8, 0:128],
                                                         r_oh2[:], ident[:]),
                         waits=w, inc="TE")
            mc2 = vec(lambda e, tb=tb: e.tensor_copy(
                mB_sb[0:8, tb * 128:(tb + 1) * 128], pbk[6][0:8, 0:128]),
                waits=[tt2])
            bk_read(6, mc2)
            mask_toks += [mc1, mc2]
            td = vec(lambda e: e.tensor_add(r_oh2[:], r_oh1[:], r_oh2[:]),
                     waits=[t7, tc, tt1, tt2])
            te_ = vec(lambda e: e.tensor_mul(r_p[:], r_p[:], r_oh2[:]), waits=[td])
            tf = vec(lambda e: e.reduce_sum(r_s[:], r_p[:], AX.X), waits=[te_])
            tg = vec(lambda e: e.tensor_scalar_add(r_s[:], r_s[:], 1e-9), waits=[tf])
            th = vec(lambda e: e.reciprocal(r_t[:], r_s[:]), waits=[tg])
            ti = vec(lambda e: e.tensor_scalar_mul(r_cf[:], r_p[:], r_t[:, 0:1]),
                     waits=[th])
            w = bk_acq(6, [ti, cst])
            tj = P.emit("tensor", lambda e: e.transpose(pbk[6][0:8, 0:128],
                                                        r_cf[:], ident[:]),
                        waits=w, inc="TE")
            tk = vec(lambda e, tb=tb: e.tensor_copy(
                cT_sb[0:8, tb * 128:(tb + 1) * 128], pbk[6][0:8, 0:128]),
                waits=[tj] + arena_w)
            bk_read(6, tk)
            cT_toks.append(tk)
            rprev = ti
        ct_w = dma("sync", cT_dram.rearrange("(a b) c -> a (b c)", a=E),
                   cT_sb[0:8, :], waits=cT_toks, sem="ctd")
        st["stagerM"].gate([ct_w] + list(xn_tm_toks))

        z0 = vec(lambda e: e.memset(zscan[:], 0.0),
                 waits=arena_w + [cstg] + xn_tm_toks)
        z1 = vec(lambda e: e.memset(carry[:], 0.0), waits=[z0])
        z2 = vec(lambda e: e.memset(gsc[:], 0.0), waits=arena_w + xn_tm_toks)
        scan_toks = {}
        carry_tok = z1
        for nm, msk, dst in (("A", mA_sb, pA_sb), ("B", mB_sb, pB_sb)):
            prev = carry_tok
            for c in range(TPC // CB):
                c0_, c1_ = c * CB, (c + 1) * CB
                t = vec(lambda e, msk=msk, dst=dst, c0_=c0_, c1_=c1_:
                        e.tensor_tensor_scan(dst[0:8, c0_:c1_], msk[0:8, c0_:c1_],
                                             zscan[0:8, 0:c1_ - c0_],
                                             carry[0:8, 0:1],
                                             ALU.add, ALU.add),
                        waits=[prev, z0] + mask_toks + slot_gate)
                prev = vec(lambda e, dst=dst, c1_=c1_: e.tensor_copy(
                    carry[0:8, 0:1], dst[0:8, c1_ - 1:c1_]), waits=[t])
            rz = vec(lambda e: e.memset(carry[:], 0.0), waits=[prev])
            carry_tok = rz
            scan_toks[nm] = prev
        import os as _os
        if _os.environ.get("DBG_SCANTAP"):
            dbg_scan = nc.dram_tensor("dbg_scan", [24, TPC], F32,
                                      kind="ExternalOutput").ap()
            dma("sync", dbg_scan[0:8, :], pA_sb[0:8, :],
                waits=[scan_toks["A"]], sem="OUT")
            dma("sync", dbg_scan[8:16, :], mA_sb[0:8, :],
                waits=[scan_toks["A"]], sem="OUT")
            dma("sync", dbg_scan[16:24, :], pB_sb[0:8, :],
                waits=[scan_toks["B"]], sem="OUT")
            _sc_tap = ("OUT", P.cnt["OUT"])
        if _os.environ.get("DBG_NOSLOT"):
            sA4 = scan_toks["A"]
            sB5 = scan_toks["B"]
        else:
            _dummy = None
        if not _os.environ.get("DBG_NOSLOT"):
            _tapw = ([("OUT", P.cnt["OUT"])]
                     if _os.environ.get("DBG_SCANTAP") else [])
            sA1 = vec(lambda e: e.tensor_scalar_add(pA_sb[0:8, :], pA_sb[0:8, :],
                                                ecap1_sb[0:8, 0:1]),
                  waits=[scan_toks["A"], cst] + _tapw)
            import os as _o4
            if _o4.environ.get("DBG_S1TAP"):
                dbg_s1 = nc.dram_tensor("dbg_s1", [8, TPC], F32,
                                        kind="ExternalOutput").ap()
                _t1 = dma("sync", dbg_s1[0:8, :], pA_sb[0:8, :], waits=[sA1],
                          sem="OUT")
                sA1 = _t1
            sA2 = vec(lambda e: e.tensor_mul(pA_sb[0:8, :], pA_sb[0:8, :], mA_sb[0:8, :]),
                      waits=[sA1])
            sA3 = vec(lambda e: e.scalar_tensor_tensor(pA_sb[0:8, :], mA_sb[0:8, :],
                                                       -65536.0, pA_sb[0:8, :],
                                                       ALU.mult, ALU.add), waits=[sA2])
            sA4 = vec(lambda e: e.tensor_scalar_add(pA_sb[0:8, :], pA_sb[0:8, :],
                                                    65536.0),
                      waits=[sA3])
            sB1 = vec(lambda e: e.tensor_scalar_mul(pB_sb[0:8, :], pB_sb[0:8, :], -1.0),
                      waits=[scan_toks["B"]] + _tapw)
            sB2 = vec(lambda e: e.tensor_scalar_add(pB_sb[0:8, :], pB_sb[0:8, :],
                                                    ecap2_sb[0:8, 0:1]), waits=[sB1, cst])
            sB3 = vec(lambda e: e.tensor_mul(pB_sb[0:8, :], pB_sb[0:8, :], mB_sb[0:8, :]),
                      waits=[sB2])
            sB4 = vec(lambda e: e.scalar_tensor_tensor(pB_sb[0:8, :], mB_sb[0:8, :],
                                                       -65536.0, pB_sb[0:8, :],
                                                       ALU.mult, ALU.add), waits=[sB3])
            sB5 = vec(lambda e: e.tensor_scalar_add(pB_sb[0:8, :], pB_sb[0:8, :],
                                                    65536.0),
                      waits=[sB4])

        import os as _o6
        if _o6.environ.get("DBG_SLFIN"):
            dbg_slf = nc.dram_tensor("dbg_slf", [16, TPC], F32,
                                     kind="ExternalOutput").ap()
            _w1 = dma("sync", dbg_slf[0:8, :], pA_sb[0:8, :], waits=[sA4],
                      sem="OUT")
            _w2 = dma("sync", dbg_slf[8:16, :], pB_sb[0:8, :], waits=[sB5],
                      sem="OUT")
            sA4 = _w1
            sB5 = _w2
        ofill = vec(lambda e: e.memset(oobfill[:], 65536.0),
                    waits=arena_w + xn_tm_toks)
        init_toks = {}
        for nm, arr in (("I", idx_dram), ("A", idxA_dram), ("B", idxB_dram)):
            init_toks[nm] = dma("gpsimd",
                                arr.rearrange("(p c) a -> p (c a)", p=128),
                                oobfill[:], waits=[ofill], sem=f"ix{nm}")
        tok_prev = None
        sc_done = {"I": init_toks["I"], "A": init_toks["A"], "B": init_toks["B"]}
        for tb in range(NTB):
            tkt = vec(lambda e, tb=tb: e.tensor_single_scalar(
                tokid[:], iota_col[:], float(tb * 128), ALU.add),
                waits=[cst] + (tok_prev or []) + (xn_tm_toks if tb == 0 else []))
            sl_toks = {}
            for j, (nm, slots, s_last) in enumerate(
                    (("A", pA_sb, sA4), ("B", pB_sb, sB5))):
                w = bk_acq(6, [s_last])
                tt = P.emit("tensor", lambda e, slots=slots, tb=tb: e.transpose(
                    pbk[6][:, 0:8], slots[0:8, tb * 128:(tb + 1) * 128],
                    ident[0:8, 0:8]), waits=w, inc="TE")
                rd = vec(lambda e, j=j: e.tensor_reduce(
                    slotc[j][:], pbk[6][:, 0:8], AX.X, ALU.min), waits=[tt])
                bk_read(6, rd)
                import os as _o5
                if _o5.environ.get("DBG_SCTAP"):
                    if "dbg_sc" not in taps:
                        taps["dbg_sc"] = nc.dram_tensor(
                            "dbg_sc", [128, 2 * NTB], F32,
                            kind="ExternalOutput").ap()
                    rd = P.emit("sync", lambda e, tb=tb, j=j: e.dma_start(
                        out=taps["dbg_sc"][:, 2 * tb + j:2 * tb + j + 1],
                        in_=slotc[j][:], allow_slow_non_contiguous=True),
                        waits=[rd], inc="OUT", amt=16)
                cu = vec(lambda e, j=j: e.tensor_copy(slotu[j][:], slotc[j][:]),
                         waits=[rd] + (tok_prev or []))
                sl_toks[nm] = cu
            this_sc = []
            for nm, j in (("A", 0), ("B", 1)):
                arr = idxA_dram if nm == "A" else idxB_dram
                d1 = P.emit("gpsimd", lambda e, j=j, arr=arr: e.indirect_dma_start(
                    arr, IOA(slotu[j][:], 0), tokid[:], None,
                    bounds_check=bndreg(e, NCH * 128 - 1), oob_is_err=False),
                    waits=[sl_toks[nm], tkt, sc_done[nm]], inc=f"ix{nm}", amt=16)
                sc_done[nm] = d1
                d2 = P.emit("gpsimd", lambda e, j=j: e.indirect_dma_start(
                    idx_dram, IOA(slotu[j][:], 0), tokid[:], None,
                    bounds_check=bndreg(e, NCH * 128 - 1), oob_is_err=False),
                    waits=[sl_toks[nm], tkt, sc_done["I"]], inc="ixI", amt=16)
                sc_done["I"] = d2
                this_sc += [d1, d2]
            tok_prev = this_sc

        import os as _os3
        if _os3.environ.get("DBG_IXTAP"):
            dbg_ix = nc.dram_tensor("dbg_ix", [3, NCH * 128], F32,
                                    kind="ExternalOutput").ap()
            for _k, (_nm, _arr) in enumerate((("I", idx_dram), ("A", idxA_dram),
                                              ("B", idxB_dram))):
                dma("sync", dbg_ix[_k:_k + 1, :],
                    _arr.rearrange("(a) c -> c (a)", a=NCH * 128)
                    if False else _arr.rearrange("a c -> c a"),
                    waits=[sc_done[_nm]], sem="OUT")
        idx_toks = {}
        ld_prev = None
        for nm, arr, dstu in (("I", idx_dram, idxu), ("A", idxA_dram, idxAu),
                              ("B", idxB_dram, idxBu)):
            l1 = dma("gpsimd", idx_ld[:],
                     arr.rearrange("(c p) a -> c (p a)", c=NCH),
                     waits=[sc_done[nm], ld_prev] + xn_tm_toks, sem="ixL")
            w = bk_acq(6, [l1])
            t1 = P.emit("tensor", lambda e: e.transpose(
                pbk[6][:, 0:NCH], idx_ld[0:NCH, :], ident[0:NCH, 0:NCH]),
                waits=w, inc="TE")
            cu = vec(lambda e, dstu=dstu: e.tensor_copy(dstu[:], pbk[6][:, 0:NCH]),
                     waits=[t1])
            bk_read(6, cu)
            idx_toks[nm] = cu
            ld_prev = cu

        m1s, m2s = st["m1s_s"], st["m2s_s"]
        for _s in (m1s, m2s):
            for i in range(len(_s.free)):
                _s.free[i] = _s.free[i] + qkv_gate if _s.free[i] else list(qkv_gate)
        h_tok = [None] * NHB
        rl_free = [None]
        gsc_free = z2
        xe_free = [qkv_gate, qkv_gate]
        xef_free = qkv_gate
        xefm_free = [cT_toks[-1]] + arena_w + [ct_w]
        gbs_free = arena_w
        grow_free = arena_w + xn_tm_toks
        yfm_free = [arena_w + xn_tm_toks] * 4
        ytm_free = [arena_w, arena_w]
        b2r_free = [tmax(*(arena_w + xn_tm_toks))[0] if (arena_w + xn_tm_toks)
                    else None] * 2
        b2r_free = [None, None]
        if xn_tm_toks:
            b2r_free = [xn_tm_toks[-1], xn_tm_toks[-1]]
        iax_free = [None, None]
        gg_prev = [None]
        ysc_tok = None
        for e_i in range(E):
            b2i = e_i % 2
            b2t = dma("gpsimd", b2row[b2i][:], din["l1_b2"][e_i:e_i + 1, :],
                      waits=[b2r_free[b2i]], sem=f"b2r{b2i}")
            g_prev = gsc_free
            grow_toks = []
            xe_toks = [None] * FCH
            ck_tok = [None] * CAPC
            for sc in range(CAPC):
                col = e_i * CAPC + sc
                P.emit("gpsimd", lambda e, col=col, e_i=e_i, sc=sc:
                       e.indirect_dma_start(
                           gsc[:, sc:sc + 1], None, cT_dram,
                           IOA(idxu[:, col:col + 1], 0),
                           element_offset=e_i * TPC,
                           bounds_check=bndreg(e, TPC - 1), oob_is_err=False),
                       waits=([idx_toks["I"], g_prev, ct_w] if sc == 0 else ()),
                       inc="gcl", amt=16)
            gg_all = ("gcl", P.cnt["gcl"])
            w = bk_acq(6, [gg_all])
            tt = P.emit("tensor", lambda e: e.transpose(
                pbk[6][:, 0:128], gsc[:], ident[:]), waits=w, inc="TE")
            g_prev = tt
            stg_g = vec(lambda e: e.tensor_copy(zscan[0:CAPC, 0:128],
                                                pbk[6][0:CAPC, 0:128]),
                        waits=[tt])
            bk_read(6, stg_g)
            for sc in range(CAPC):
                col = e_i * CAPC + sc
                xi = sc % 2
                xg = P.emit("gpsimd", lambda e, xi=xi, col=col: e.indirect_dma_start(
                    xe_g[xi][:], None, xn_tm, IOA(idxu[:, col:col + 1], 0),
                    bounds_check=bndreg(e, TPC - 1), oob_is_err=False),
                    waits=[idx_toks["I"]] + xn_tm_toks +
                          (xe_free[xi] if isinstance(xe_free[xi], list)
                           else [xe_free[xi]]),
                    inc=f"xg{xi}", amt=16)
                cf_ = vec(lambda e, xi=xi: e.tensor_copy(xe_f[:], xe_g[xi][:]),
                          waits=[xg, xef_free] if not isinstance(xef_free, list)
                          else [xg] + xef_free)
                xe_free[xi] = cf_
                last_fc = None
                for fc in range(FCH):
                    w = bk_acq(7, [cf_])
                    t1 = P.emit("tensor", lambda e, fc=fc: e.transpose(
                        pbk[7][:, 0:128], xe_f[:, fc * 128:(fc + 1) * 128],
                        ident[:]), waits=w, inc="TE")
                    xw = [t1] + (xefm_free if (e_i == 0 and sc == 0) else [])
                    if xe_toks[fc]:
                        xw.append(xe_toks[fc])
                    c2 = vec(lambda e, fc=fc, sc=sc: e.tensor_copy(
                        xe_fm[fc][:, sc * 128:(sc + 1) * 128], pbk[7][:, 0:128]),
                        waits=xw)
                    bk_read(7, c2)
                    xe_toks[fc] = c2
                    last_fc = c2
                xef_free = last_fc
                ck_tok[sc] = last_fc
            gsc_free = g_prev
            w01 = bk_acq(0, [stg_g, cst]) + bk_acq(1, [])
            t = None
            for sc in range(CAPC):
                t = mm(pq[0][:, sc * 128:(sc + 1) * 128],
                       sel8_sb[0:CAPC, sc * 128:sc * 128 + 128],
                       zscan[0:CAPC, 0:128], True, True,
                       waits=w01 if sc == 0 else (),
                       inc="TE" if sc == CAPC - 1 else None)
            gc_ = vec(lambda e: e.tensor_copy(gbs_sb[:, 0:CAP],
                                              pq[0][:, 0:CAP]),
                      waits=[t] + (gbs_free if e_i == 0 else []))
            bk_read(0, gc_)
            bk_read(1, gc_)
            gbc = vec(lambda e: e.tensor_copy(growb[0:1, :], gbs_sb[0:1, :]),
                      waits=[gc_] + (grow_free if e_i == 0 else []))
            grow_free = [gbc]
            gb_toks = [gc_]
            gbs_free = gb_toks
            for hb in range(NHB):
                b1, s1 = m1s.load(
                    din["l1_w1"][(e_i * NHB + hb) * 128:
                                 (e_i * NHB + hb + 1) * 128, :]
                    .rearrange("p (g c) -> p g c", c=128))
                off = 0
                for ti, w_ in enumerate(CAPW):
                    bi = ti % 2
                    w = bk_acq(bi, [m1s.ld[s1], ck_tok[(off + w_) // 128 - 1]])
                    t = None
                    for fc in range(FCH):
                        t = mm(pbk[bi][:, 0:w_], b1[:, fc, :],
                               xe_fm[fc][:, off:off + w_],
                               start=(fc == 0), stop=(fc == FCH - 1),
                               waits=w if fc == 0 else (),
                               inc="TE" if fc == FCH - 1 else None)
                    rt = act(lambda e, bi=bi, e_i=e_i, hb=hb, w_=w_: e.activation(
                        rl_buf[:, 0:w_], pbk[bi][:, 0:w_], AF.Relu,
                        bias=b1c_sb[:, e_i * NHB + hb:e_i * NHB + hb + 1]),
                        waits=[t, rl_free[0]])
                    bk_read(bi, rt)
                    hw = [rt] + gb_toks
                    if e_i == 0 and ti == 0:
                        hw += qkv_gate + slot_gate + [sA4, sB5, idx_toks["B"]]
                    ht = vec(lambda e, hb=hb, off=off, w_=w_: e.tensor_mul(
                        h_sp[hb][:, off:off + w_], rl_buf[:, 0:w_],
                        gbs_sb[:, off:off + w_]), waits=hw)
                    rl_free[0] = ht
                    h_tok[hb] = ht
                    off += w_
                m1s.release(s1, h_tok[hb])
            for ch in range(2):
                ytiles = []
                off = 0
                for w_ in CAPW:
                    for cb in range(4):
                        ytiles.append((cb, off, w_))
                    off += w_
                for k_, (cb, off, w_) in enumerate(ytiles):
                    bi = k_ % 8
                    w = bk_acq(bi, [gbc, b2t, cstg])
                    mm(pbk[bi][:, 0:w_],
                       b2row[b2i][0:1, ch * 512 + cb * 128:ch * 512 + (cb + 1) * 128],
                       growb[0:1, off:off + w_], True, False, waits=w)
                last = {}
                for hbg in range(NHB // 2):
                    b2w, s2 = m2s.load(
                        din["l1_w2"]
                        [((e_i * 2 + ch) * (NHB // 2) + hbg) * 128:
                         ((e_i * 2 + ch) * (NHB // 2) + hbg + 1) * 128, :]
                        .rearrange("p (g c) -> p g c", c=512))
                    slab_last = None
                    for g2 in range(2):
                        hb = hbg * 2 + g2
                        for k_, (cb, off, w_) in enumerate(ytiles):
                            bi = k_ % 8
                            isend = (hb == NHB - 1)
                            wv = [m2s.ld[s2], h_tok[hb]] if k_ == 0 else ()
                            t = mm(pbk[bi][:, 0:w_],
                                   b2w[:, g2, cb * 128:(cb + 1) * 128],
                                   h_sp[hb][:, off:off + w_], False, isend,
                                   waits=wv,
                                   inc="TE" if isend or (g2 == 1 and
                                                         k_ == len(ytiles) - 1)
                                   else None)
                            if isend:
                                last[k_] = t
                            if g2 == 1 and k_ == len(ytiles) - 1:
                                slab_last = t
                    m2s.release(s2, slab_last)
                yev = {}
                for k_, (cb, off, w_) in enumerate(ytiles):
                    bi = k_ % 8
                    ev = vec(lambda e, cb=cb, off=off, w_=w_, bi=bi: e.tensor_copy(
                        y_fm[cb][:, off:off + w_], pbk[bi][:, 0:w_]),
                        waits=[last[k_]] + (yfm_free[cb]
                                            if isinstance(yfm_free[cb], list)
                                            else [yfm_free[cb]]))
                    bk_read(bi, ev)
                    yev.setdefault(cb, []).append(ev)
                iA = vec(lambda e, ch=ch: e.tensor_scalar(
                    idxA2[:], idxAu[:], 2.0, float(ch), ALU.mult, ALU.add),
                    waits=[idx_toks["A"], iax_free[0]])
                iB = vec(lambda e, ch=ch: e.tensor_scalar(
                    idxB2[:], idxBu[:], 2.0, float(ch), ALU.mult, ALU.add),
                    waits=[idx_toks["B"], iax_free[1]])
                cnv_t = None
                for sc in range(CAPC):
                    yi = sc % 2
                    tcp = []
                    for cb in range(4):
                        cc = vec(lambda e, cb=cb, sc=sc: e.tensor_copy(
                            cnv[:], y_fm[cb][:, sc * 128:(sc + 1) * 128]),
                            waits=yev[cb] + [cnv_t])
                        w = bk_acq(7, [cc])
                        t1 = P.emit("tensor", lambda e: e.transpose(
                            pbk[7][:, 0:128], cnv[:], ident[:]), waits=w, inc="TE")
                        cnv_t = t1
                        c2 = vec(lambda e, cb=cb, yi=yi: e.tensor_copy(
                            ytm[yi][:, cb * 128:(cb + 1) * 128], pbk[7][:, 0:128]),
                            waits=([t1] + ytm_free[yi])
                            if isinstance(ytm_free[yi], list)
                            else [t1, ytm_free[yi]])
                        bk_read(7, c2)
                        tcp.append(c2)
                    col = e_i * CAPC + sc
                    s1_ = P.emit("gpsimd", lambda e, yi=yi, col=col, ch=ch:
                                 e.indirect_dma_start(
                                     y1_dram.rearrange("t (h c) -> (t h) c", h=2),
                                     IOA(idxA2[:, col:col + 1], 0),
                                     ytm[yi][:], None,
                                     bounds_check=bndreg(e, 2 * TPC - 1), oob_is_err=False),
                                 waits=tcp + [iA, ysc_tok], inc="yS", amt=16)
                    s2_ = P.emit("gpsimd", lambda e, yi=yi, col=col, ch=ch:
                                 e.indirect_dma_start(
                                     y2_dram.rearrange("t (h c) -> (t h) c", h=2),
                                     IOA(idxB2[:, col:col + 1], 0),
                                     ytm[yi][:], None,
                                     bounds_check=bndreg(e, 2 * TPC - 1), oob_is_err=False),
                                 waits=[iB, s1_], inc="yS", amt=16)
                    ysc_tok = s2_
                    ytm_free[yi] = [s1_, s2_]
                    iax_free = [s1_, s2_]
                for cb in range(4):
                    yfm_free[cb] = yev[cb][-1] if False else yev[cb]
                for cb in range(4):
                    yfm_free[cb] = [ysc_tok]
            b2r_free[b2i] = ("TE", P.cnt["TE"])
        ys_all = ("yS", P.cnt["yS"])
        cmb_prev = [None, None]
        cnv_t = None
        for tbl in range(NTB):
            xi = tbl % 2
            xw = [ys_all] + ([cmb_prev[xi]] if cmb_prev[xi] else [])
            if not isinstance(xe_free[xi], list):
                xw.append(xe_free[xi])
            l1 = dma("gpsimd", xe_g[xi][:], y1_dram[tbl * 128:(tbl + 1) * 128, :],
                     waits=xw, sem=f"yc{xi}")
            l2 = dma("gpsimd", tm_st[xi][:], y2_dram[tbl * 128:(tbl + 1) * 128, :],
                     waits=[ys_all] + ([cmb_prev[xi]] if cmb_prev[xi] else []),
                     sem=f"yd{xi}")
            for fc in range(FCH):
                c1 = vec(lambda e, xi=xi, fc=fc: e.tensor_copy(
                    cnv[:], xe_g[xi][:, fc * 128:(fc + 1) * 128]),
                    waits=[l1, cnv_t])
                w = bk_acq(4 + (fc % 2), [c1])
                t1 = P.emit("tensor", lambda e, fc=fc: e.transpose(
                    pbk[4 + (fc % 2)][:, 0:128], cnv[:], ident[:]),
                    waits=w, inc="TE")
                c1b = vec(lambda e, xi=xi, fc=fc: e.tensor_copy(
                    cnv[:], tm_st[xi][:, fc * 128:(fc + 1) * 128]),
                    waits=[l2, t1])
                cnv_t = c1b
                w2 = bk_acq(6 + (fc % 2), [c1b])
                t2 = P.emit("tensor", lambda e, fc=fc: e.transpose(
                    pbk[6 + (fc % 2)][:, 0:128], cnv[:], ident[:]),
                    waits=w2, inc="TE")
                a1 = vec(lambda e, fc=fc, tbl=tbl: e.tensor_add(
                    xacc[fc][:, tbl * 128:(tbl + 1) * 128],
                    pbk[4 + (fc % 2)][:, 0:128],
                    xacc[fc][:, tbl * 128:(tbl + 1) * 128]),
                    waits=[t1, st["xacc"][fc]])
                bk_read(4 + (fc % 2), a1)
                a2 = vec(lambda e, fc=fc, tbl=tbl: e.tensor_add(
                    xacc[fc][:, tbl * 128:(tbl + 1) * 128],
                    pbk[6 + (fc % 2)][:, 0:128],
                    xacc[fc][:, tbl * 128:(tbl + 1) * 128]),
                    waits=[t2, a1])
                bk_read(6 + (fc % 2), a2)
                st["xacc"][fc] = a2
                cmb_prev[xi] = a2
        return

    def emit_attn(lid, ck, cv, xn_dead):
        l0 = (lid == 0)
        pre = f"l{lid}_"
        for r in range(4):
            dma("gpsimd", kn4_sb[0:1, r * 8:(r + 1) * 8],
                k_gath[r * (D + 1) + D:r * (D + 1) + D + 1, 0:8],
                waits=[ck], sem="K4")
        kn_toks = [("K4", P.cnt["K4"])]
        t1 = vec(lambda e: e.reduce_max(
            msc[0:1, 0:8],
            kn4_sb.ap()[0:1, 0:32].rearrange("a (r h) -> a h r", r=4), AX.X),
            waits=kn_toks)
        t2 = vec(lambda e: e.tensor_mul(msc[0:1, 0:8], msc[0:1, 0:8],
                                        qn_sb[0:1, 0:8]), waits=[t1])
        t3 = vec(lambda e: e.reduce_max(msc[0:1, 0:1], msc[0:1, 0:8], AX.X),
                 waits=[t2])
        t4 = act(lambda e: e.activation(msc[0:1, 0:1], msc[0:1, 0:1], AF.Ln),
                 waits=[t3])
        t5 = act(lambda e: e.activation(msc[0:1, 0:1], msc[0:1, 0:1], AF.Exp,
                                        scale=0.5), waits=[t4])
        t6 = vec(lambda e: e.tensor_scalar(negM1[0:1, 0:1], msc[0:1, 0:1],
                                           -0.1275, -1.0, ALU.mult, ALU.add),
                 waits=[t5])
        w = bk_acq(6, [t6, cst])
        t7 = mm(pbk[6][:, 0:1], ones_r1[0:1, :], negM1[0:1, 0:1], True, True, w,
                inc="TE")
        negM_tok = vec(lambda e: e.tensor_copy(negM[:], pbk[6][:, 0:1]), waits=[t7])
        bk_read(6, negM_tok)

        unpark = []
        for hp in range(8):
            w = [st["q2_free"][hp], st["h_free"]]
            unpark.append(dma("sync", q2[hp][:], q_park[hp * 128:(hp + 1) * 128, :],
                              waits=w, sem=f"up{hp}"))
        st["qpark_free"] = unpark[-1]

        def jmm(n=1):
            for _ in range(n):
                mm(pbk[5][:, 0:QW], masks_sb[:, 0:128], masks_sb[:, 0:QW],
                   True, True, waits=[cst])

        os_ = st["ostream"]
        gate = st["h_free"]
        os_.gate(gate)
        o_last = None
        kg_last = None
        vg_last = None
        rec_chain = gate
        recb_chain = gate
        attn_gate = gate
        slot_exp = [gate, gate]
        slot_av = [gate, gate]
        for quad in range(4):
            k_toks = {}
            for hpi in range(2):
                hp = quad * 2 + hpi
                for r in range(4):
                    ki = hpi * 4 + r
                    w = [ck] + ([st["B_free"][ki]] if st["B_free"][ki] else [])
                    if quad == 0:
                        w += xn_dead
                    t = dma("sync", kwin[ki][:],
                            k_gath[r * (D + 1) + hp * 128:
                                   r * (D + 1) + (hp + 1) * 128, :], waits=w,
                            sem=f"kw{ki}")
                    k_toks[(hpi, r)] = t
                    kg_last = t
            v_toks = {}
            for r in range(4):
                w = [cv, attn_gate]
                t = dma("sync", vw[r][:],
                        v_gath.rearrange("(r q t p) c -> p r q t c", r=4, q=4, p=128)
                        [:, r, quad, :, :], waits=w, sem=f"vq{r}")
                v_toks[r] = t
                vg_last = t
            oz_toks = []
            z_free = [attn_gate, attn_gate]
            for hpi in range(2):
                hp = quad * 2 + hpi
                hp_qk_last = None
                for hi in range(2):
                    hloc = 2 * hpi + hi
                    rlo, rhi = hi * 64, (hi + 1) * 64
                    for qb in range(NQB):
                        diag_kb = qb * QW // 128
                        if l0:
                            kl = [(r, kb) for r in range(4)
                                  for kb in range(diag_kb + 1)]
                        else:
                            kl = [(r, kb) for r in range(4) for kb in range(KTPR)]
                        pairs = [kl[i:i + 2] for i in range(0, len(kl), 2)]
                        np_ = len(pairs)
                        qk_tok = [None] * np_

                        def emit_qk(p_i):
                            pr = pairs[p_i]
                            slot = p_i % 2
                            for j, (r, kb) in enumerate(pr):
                                ww = [k_toks[(hpi, r)]]
                                if j == 0:
                                    ww += [slot_exp[slot], unpark[hp]]
                                qk_tok[p_i] = mm(
                                    pq[slot][:, j * QW:(j + 1) * QW],
                                    kwin[hpi * 4 + r][rlo:rhi,
                                                      kb * 128:(kb + 1) * 128],
                                    q2[hp][rlo:rhi, qb * QW:(qb + 1) * QW],
                                    True, True, waits=ww, inc="TE")

                        emit_qk(0)
                        zlast_t = None
                        for i in range(np_):
                            jmm(1)
                            if i + 1 < np_:
                                emit_qk(i + 1)
                            pr = pairs[i]
                            slot = i % 2
                            W = len(pr) * QW
                            et = act(lambda e, slot=slot, W=W: e.activation(
                                p_buf[slot][:, 0:W], pq[slot][:, 0:W], AF.Exp,
                                bias=negM[:, 0:1], scale=0.125),
                                waits=[qk_tok[i], negM_tok, slot_av[slot]])
                            slot_exp[slot] = et
                            half_tok = [et] * len(pr)
                            for j, (r, kb) in enumerate(pr):
                                if l0 and kb == diag_kb:
                                    half_tok[j] = vec(
                                        lambda e, slot=slot, j=j, r=r:
                                        e.tensor_mul(
                                            p_buf[slot][:, j * QW:(j + 1) * QW],
                                            p_buf[slot][:, j * QW:(j + 1) * QW],
                                            masks_sb[:, r * QW:(r + 1) * QW]),
                                        waits=[et, cst])
                            for j, (r, kb) in enumerate(pr):
                                first = (i == 0 and j == 0)
                                last = (i == np_ - 1 and j == len(pr) - 1)
                                w2 = [half_tok[j], v_toks[r]]
                                if first:
                                    w2 = bk_acq(4, w2)
                                zt = mm(pbk[4][0:65, 0:QW],
                                        vw[r][:, kb, hloc * 65:(hloc + 1) * 65],
                                        p_buf[slot][:, j * QW:(j + 1) * QW],
                                        start=first, stop=last,
                                        waits=w2, inc="TE")
                                slot_av[slot] = zt
                                if last:
                                    zlast_t = zt
                        hp_qk_last = qk_tok[np_ - 1]
                        jmm(6)
                        e1 = act(lambda e: e.activation(
                            recrow[0:1, 0:QW], pbk[4][64:65, 0:QW], AF.Ln),
                            waits=[zlast_t, rec_chain])
                        e2 = act(lambda e: e.activation(
                            recrow[0:1, 0:QW], recrow[0:1, 0:QW], AF.Exp,
                            scale=-1.0), waits=[e1])
                        w = bk_acq(6, [e2])
                        e3 = mm(pbk[6][0:64, 0:QW], ones_r1[0:1, 0:64],
                                recrow[0:1, 0:QW], True, True, w, inc="TE")
                        rec_chain = e3
                        e4 = vec(lambda e: e.tensor_copy(recb[:, 0:QW],
                                                         pbk[6][0:64, 0:QW]),
                                 waits=[e3, recb_chain])
                        bk_read(6, e4)
                        e5 = vec(lambda e, hpi=hpi, hi=hi, qb=qb: e.tensor_mul(
                            z_sb[hpi][hi * 64:(hi + 1) * 64,
                                      qb * QW:(qb + 1) * QW],
                            pbk[4][0:64, 0:QW], recb[:, 0:QW]),
                            waits=[e4, zlast_t, z_free[hpi]])
                        bk_read(4, e5)
                        recb_chain = e5
                        oz_toks.append(e5)
                st["q2_free"][hp] = hp_qk_last
            buf0, s0 = os_.load(din[pre + "wo"]
                                [(quad * 2) * 128:(quad * 2 + 1) * 128, :])
            buf1, s1 = os_.load(din[pre + "wo"]
                                [(quad * 2 + 1) * 128:(quad * 2 + 2) * 128, :])
            for ob in range(FCH):
                for tbl in range(NCB):
                    bo_i = 6 + tbl % 2
                    col0 = tbl * CB
                    w = bk_acq(bo_i, [os_.ld[s0], os_.ld[s1]] + oz_toks)
                    mm(pbk[bo_i][:, 0:CB], buf0[:, ob * 128:(ob + 1) * 128],
                       z_sb[0][:, col0:col0 + CB], True, False, w)
                    t = mm(pbk[bo_i][:, 0:CB], buf1[:, ob * 128:(ob + 1) * 128],
                           z_sb[1][:, col0:col0 + CB], False, True, (), inc="TE")
                    if (not l0) and quad == 0:
                        at = vec(lambda e, ob=ob, col0=col0, bo_i=bo_i:
                                 e.scalar_tensor_tensor(
                                     xacc[ob][:, col0:col0 + CB],
                                     pbk[bo_i][:, 0:CB], bo_sb[:, ob:ob + 1],
                                     xacc[ob][:, col0:col0 + CB],
                                     ALU.add, ALU.add),
                                 waits=[t, st["xacc"][ob], cst])
                    else:
                        at = vec(lambda e, ob=ob, col0=col0, bo_i=bo_i:
                                 e.tensor_add(xacc[ob][:, col0:col0 + CB],
                                              pbk[bo_i][:, 0:CB],
                                              xacc[ob][:, col0:col0 + CB]),
                                 waits=[t, st["xacc"][ob]])
                    bk_read(bo_i, at)
                    st["xacc"][ob] = at
                    o_last = at
            os_.release(s0, o_last)
            os_.release(s1, o_last)
            for ki in range(8):
                st["B_free"][ki] = o_last
            attn_gate = o_last
        for s_i in range(2):
            P.bank_r[2 * s_i] = slot_exp[s_i]
            P.bank_r[2 * s_i + 1] = slot_exp[s_i]
        st["kgath_free"] = kg_last
        st["vgath_free"] = vg_last
        return o_last

    for lid in layers:
        xn_tok = emit_ln(lid)
        ck, cv = emit_qkv(lid, xn_tok)
        xn_tm_toks = []
        if lid == 1 and cfg.sparse_moe:
            stga_last = [t for fl in st["stagerQ"].free for t in fl] \
                if st["stagerQ"] else []
            cnv_tok = ([st["arena_free"]] if st["arena_free"] else []) \
                + st["contrib_done"] + tmax(*stga_last)
            tm_free = [None, None]
            for tb in range(NTB):
                si = tb % 2
                cp_toks = []
                for fc in range(FCH):
                    c1 = vec(lambda e, fc=fc, tb=tb: e.tensor_copy(
                        cnv[:], xn[fc][:, tb * 128:(tb + 1) * 128]),
                        waits=xn_tok + (cnv_tok if isinstance(cnv_tok, list)
                                        else [cnv_tok]))
                    w = bk_acq(7, [c1])
                    t1 = P.emit("tensor", lambda e: e.transpose(
                        pbk[7][:, 0:128], cnv[:], ident[:]), waits=w, inc="TE")
                    cnv_tok = [t1]
                    c2 = vec(lambda e, fc=fc, si=si: e.tensor_copy(
                        tm_st[si][:, fc * 128:(fc + 1) * 128], pbk[7][:, 0:128]),
                        waits=[t1, tm_free[si]] if tm_free[si] else [t1])
                    bk_read(7, c2)
                    cp_toks.append(c2)
                dt_ = dma("sync", xn_tm[tb * 128:(tb + 1) * 128, :], tm_st[si][:],
                          waits=cp_toks, sem=f"tm{si}")
                tm_free[si] = dt_
                xn_tm_toks.append(dt_)
        qkv_gate = [ck, cv]
        if lid == 0:
            emit_ffn_l0(xn_tok, st["contrib_done"])
        elif cfg.sparse_moe:
            emit_moe_sparse(xn_tok, st["contrib_done"], xn_tm_toks)
        else:
            emit_moe(xn_tok, qkv_gate)
        lastacc = max(st["xacc"], key=lambda t: t[1])
        st["h_free"] = lastacc
        if f"mid{lid}" in taps:
            for fc in range(FCH):
                dma("sync", taps[f"mid{lid}"][fc * 128:(fc + 1) * 128, :],
                    xacc[fc][:], waits=[st["xacc"][fc]], sem="OUT")
            ttok = ("OUT", P.cnt["OUT"])
            st["xacc"] = [ttok] * FCH
        xn_dead = [lastacc]
        o_last = emit_attn(lid, ck, cv, xn_dead)
        st["arena_free"] = o_last
        if f"x{lid}" in taps:
            for fc in range(FCH):
                dma("sync", taps[f"x{lid}"][fc * 128:(fc + 1) * 128, :], xacc[fc][:],
                    waits=[st["xacc"][fc]], sem="OUT")

    for fc in range(FCH):
        dma("sync", out_fm[fc * 128:(fc + 1) * 128, :], xacc[fc][:],
            waits=[st["xacc"][fc]], sem="OUT")
    final_v = P.cnt["OUT"]
    P.emit("sync", lambda e: e.nop(), waits=[("OUT", final_v)])

    import contextlib
    ctx = contextlib.ExitStack()
    sems = {s: ctx.enter_context(nc.semaphore(s)) for s in sorted(P.sems)}
    runners = P.replay_into(sems)
    with nc.Block() as blk:
        blk.tensor(lambda e: runners["tensor"](e))
        blk.vector(lambda e: runners["vector"](e))
        blk.scalar(lambda e: runners["scalar"](e))
        blk.sync(lambda e: runners["sync"](e))
        blk.gpsimd(lambda e: runners["gpsimd"](e))
    ctx.close()
    return nc


_BUILD_CACHE = {}


def kernel(**inputs):
    cfg = CFG()
    in_maps, idxs = host_prepare(cfg, inputs)
    key = (cfg.S, cfg.FH, cfg.MH, cfg.layers)
    if key not in _BUILD_CACHE:
        _BUILD_CACHE[key] = build_program(cfg, cfg.layers)
    nc = _BUILD_CACHE[key]
    res = run_bass_kernel_spmd(nc, in_maps, core_ids=list(range(8)))
    return host_assemble(cfg, res.results, idxs)


# version 14
# speedup vs baseline: 1.0039x; 1.0039x over previous
import numpy as np
import ml_dtypes
from collections import defaultdict

import concourse.bass as bass
import concourse.mybir as mybir
from concourse.bass_core_rust import IndirectOffsetOnAxis
from concourse.bass_utils import run_bass_kernel_spmd

F32 = mybir.dt.float32
BF16 = mybir.dt.bfloat16
I32 = mybir.dt.int32
U32 = mybir.dt.uint32
AF = mybir.ActivationFunctionType
ALU = mybir.AluOpType
AX = mybir.AxisListType
BF16NP = ml_dtypes.bfloat16


class CFG:
    def __init__(self, S=8192, FH=3072, MH=4096, layers=(0, 1), debug_taps=()):
        self.S, self.FH, self.MH = S, FH, MH
        self.D, self.H, self.DH, self.E, self.B, self.NC = 1024, 16, 64, 8, 2, 8
        self.layers = tuple(layers)
        self.debug_taps = tuple(debug_taps)
        self.TPC = self.S * self.B // self.NC
        self.CB = min(512, self.TPC)
        self.NCB = self.TPC // self.CB
        self.QW = self.CB
        self.NQB = self.TPC // self.QW
        self.NTB = self.TPC // 128
        self.TP = min(1024, self.TPC)
        self.NTP = self.TPC // self.TP
        self.YW = min(512, self.TP)
        self.NYW = self.TP // self.YW
        self.FCH = self.D // 128
        self.NHB_F = self.FH // 128
        self.NHB_M = self.MH // 128
        self.KTPR = self.TPC // 128
        self.sparse_moe = True
        cap = int(np.ceil(self.TPC * 0.375 / 128.0)) * 128
        self.CAP = max(128, cap)
        self.CAPC = self.CAP // 128
        self.CAPW = [512] * (self.CAP // 512) + ([self.CAP % 512] if self.CAP % 512 else [])
        assert self.FH % 256 == 0 and self.MH % 512 == 0 and self.TPC % 128 == 0


ENGINES = ("tensor", "vector", "scalar", "sync", "gpsimd")


def tmax(*toks):
    best = {}
    for t in toks:
        if t is None:
            continue
        s, v = t
        if v > best.get(s, -1):
            best[s] = v
    return [(s, v) for s, v in best.items()]


class Prog:
    def __init__(self):
        self.ev = {e: [] for e in ENGINES}
        self.cnt = defaultdict(int)
        self.sems = set()
        self.bank_r = {}

    def emit(self, eng, fn, waits=(), inc=None, amt=1):
        w = [t for t in waits if t is not None]
        tok = None
        if inc is not None:
            self.sems.add(inc)
            self.cnt[inc] += amt
            tok = (inc, self.cnt[inc])
        self.ev[eng].append((fn, w, inc, amt))
        return tok

    def replay_into(self, sems):
        waited = {e: defaultdict(int) for e in ENGINES}

        def mk(eng_name):
            def run(eng):
                for fn, waits, inc, amt in self.ev[eng_name]:
                    for s, v in waits:
                        if v > waited[eng_name][s]:
                            eng.wait_ge(sems[s], v)
                            waited[eng_name][s] = v
                    inst = fn(eng)
                    if inc is not None:
                        inst.then_inc(sems[inc], amt)
            return run
        return {e: mk(e) for e in ENGINES}


class Stager:

    def __init__(self, P, tag, bufs, bufs_i32, cap):
        self.P, self.tag, self.cap = P, tag, cap
        self.bufs, self.bufs_i32 = bufs, bufs_i32
        self.free = [[] for _ in bufs]
        self.i = 0

    def gate(self, toks):
        self.free = [list(toks) for _ in self.bufs]

    def xfer(self, src, slab, gsz, n_g, waits, sqs, i32=False):
        maxg = max(1, self.cap // gsz)
        slab2 = slab.ap().rearrange("p g c -> p (g c)") if n_g > 1 else slab[:]
        tok = None
        g0 = 0
        while g0 < n_g:
            g1 = min(n_g, g0 + maxg)
            s = self.i % len(self.bufs)
            self.i += 1
            stg = (self.bufs_i32 if i32 else self.bufs)[s]
            n = (g1 - g0) * gsz
            if n_g > 1:
                dst = stg[:, 0:n].rearrange("p (g c) -> p g c", c=gsz)
                sp = src[:, g0:g1, :]
            else:
                dst = stg[:, 0:n]
                sp = src
            q = sqs[s % len(sqs)]
            d = self.P.emit(q,
                            lambda e, o=dst, i_=sp: e.dma_start(out=o, in_=i_),
                            waits=self.free[s], inc=f"{self.tag}{q[0]}{s}",
                            amt=16)
            c = self.P.emit("vector",
                            lambda e, o=slab2[:, g0 * gsz:g0 * gsz + n],
                            i_=stg[:, 0:n]: e.tensor_copy(o, i_),
                            waits=[d] + (list(waits) if g0 == 0 else []),
                            inc="DV")
            self.free[s] = [c]
            tok = c
            g0 = g1
        return tok


class Stream:

    def __init__(self, P, q, tag, bufs, stager=None, gsz=None, n_g=1,
                 sqs=("sync", "sync"), i32=False):
        self.P, self.q, self.bufs = P, q, bufs
        self.tag = tag
        self.i = 0
        self.free = [[] for _ in bufs]
        self.ld = [None] * len(bufs)
        self.stager, self.gsz, self.n_g = stager, gsz, n_g
        self.sqs, self.i32 = sqs, i32

    def load(self, src_ap):
        s = self.i % len(self.bufs)
        self.i += 1
        buf = self.bufs[s]
        if self.stager is not None:
            self.ld[s] = self.stager.xfer(src_ap, buf, self.gsz, self.n_g,
                                          self.free[s], self.sqs, self.i32)
        else:
            self.ld[s] = self.P.emit(
                self.q, lambda e, d=buf, src=src_ap: e.dma_start(out=d[:], in_=src),
                waits=self.free[s], inc=f"{self.tag}{s}", amt=16)
        return buf, s

    def release(self, s, tok):
        self.free[s] = [tok]

    def gate(self, tok):
        self.free = [[tok] for _ in self.bufs]



def slabize(wT, nsl):
    G = wT.shape[0] // 128
    W = wT.shape[1] // nsl
    return np.ascontiguousarray(
        wT.reshape(G, 128, nsl, W).transpose(2, 1, 0, 3).reshape(nsl * 128, G * W))


def host_prepare(cfg, inp):
    D, H, DH, E, S = cfg.D, cfg.H, cfg.DH, cfg.E, cfg.S
    sh = {}
    if 0 in cfg.layers:
        sh["l0_wq"] = slabize(
            np.asarray(inp["L0_attn_W_Q"]).transpose(2, 0, 1).reshape(D, H * DH), 8)
        sh["l0_wk"] = slabize(
            np.asarray(inp["L0_attn_W_K"]).transpose(2, 0, 1).reshape(D, H * DH), 8)
        sh["l0_wv"] = slabize(
            np.asarray(inp["L0_attn_W_V"]).transpose(2, 0, 1).reshape(D, H * DH), 2)
        sh["l0_wo"] = np.ascontiguousarray(np.asarray(inp["L0_attn_W_O"]).T)
        sh["l0_wg"] = slabize(np.asarray(inp["L0_ffn_w_gate"]).T, cfg.NHB_F)
        sh["l0_wu"] = slabize(np.asarray(inp["L0_ffn_w_up"]).T, cfg.NHB_F)
        sh["l0_wd"] = slabize(np.asarray(inp["L0_ffn_w_down"]).T, cfg.FCH)
    if 1 in cfg.layers:
        wqkv = np.asarray(inp["L1_attn_Wqkv"]).reshape(H, 3 * DH, D)
        bqkv = np.asarray(inp["L1_attn_bqkv"]).reshape(H, 3 * DH)
        sh["l1_wq"] = slabize(
            wqkv[:, 0:DH, :].transpose(2, 0, 1).reshape(D, H * DH), 8)
        sh["l1_wk"] = slabize(
            wqkv[:, DH:2 * DH, :].transpose(2, 0, 1).reshape(D, H * DH), 8)
        sh["l1_wv"] = slabize(
            wqkv[:, 2 * DH:, :].transpose(2, 0, 1).reshape(D, H * DH), 2)
        sh["l1_bq"] = np.ascontiguousarray(bqkv[:, 0:DH].reshape(8, 128).T)
        sh["l1_bk"] = np.ascontiguousarray(bqkv[:, DH:2 * DH].reshape(8, 128).T)
        sh["l1_bv"] = np.ascontiguousarray(bqkv[:, 2 * DH:].reshape(1, H * DH))
        sh["l1_wo"] = np.ascontiguousarray(np.asarray(inp["L1_attn_Wo"]).T)
        sh["l1_bo"] = np.ascontiguousarray(
            np.asarray(inp["L1_attn_bo"]).reshape(8, 128).T)
        sh["l1_rw"] = np.ascontiguousarray(np.asarray(inp["L1_ffn_router_w"]).T)
        w1 = np.asarray(inp["L1_ffn_w1"])
        b1 = np.asarray(inp["L1_ffn_b1"])
        w2 = np.asarray(inp["L1_ffn_w2"])
        w1aug = np.zeros((E, D + 128, cfg.MH), np.float32)
        for e in range(E):
            w1aug[e, :D, :] = w1[e].T
            w1aug[e, D, :] = b1[e]
        sh["l1_w1"] = np.concatenate(
            [slabize(w1aug[e], cfg.NHB_M) for e in range(E)], axis=0)
        sh["l1_b1c"] = np.ascontiguousarray(
            b1.reshape(E, cfg.MH // 128, 128).transpose(2, 0, 1).reshape(128, -1))
        NH2 = cfg.NHB_M // 2
        w2T = w2.transpose(0, 2, 1)
        sh["l1_w2"] = np.ascontiguousarray(
            w2T.reshape(E, NH2, 2, 128, 2, 512).transpose(0, 4, 1, 3, 2, 5)
            .reshape(E * 2 * NH2 * 128, 2 * 512))
        sh["l1_b2"] = np.ascontiguousarray(np.asarray(inp["L1_ffn_b2"]))

    sh["fconst"] = np.broadcast_to(
        np.array([0.0, 1.0, 1e-5, -0.5, 0.125, 0.5, -1.0, 1.0 / 127.0],
                 np.float32), (128, 8)).copy()
    sh["ones_col"] = np.ones((128, 1), np.float32)
    sh["ones_r1"] = np.ones((1, 128), np.float32)
    sh["ones_r1_bf"] = np.ones((1, 128), BF16NP)
    sh["ones_row_bf"] = np.ones((1, 512), BF16NP)
    sh["iota_e"] = np.broadcast_to(np.arange(8, dtype=np.float32), (128, 8)).copy()
    sh["iota_col"] = np.arange(128, dtype=np.float32).reshape(128, 1)
    sh["ecap1"] = (np.arange(8, dtype=np.float32) * cfg.CAP - 1.0).reshape(8, 1)
    sh["ecap2"] = (np.arange(8, dtype=np.float32) * cfg.CAP + cfg.CAP).reshape(8, 1)
    sel8 = np.zeros((8, 8 * 128), np.float32)
    for e in range(8):
        sel8[e, e * 128:(e + 1) * 128] = 1.0
    sh["sel8"] = sel8
    sh["ident"] = np.eye(128, dtype=np.float32)

    x = np.asarray(inp["L0_attn_x"])
    in_maps, idxs = [], []
    kk = np.arange(128)[:, None]
    qq = np.arange(cfg.QW)[None, :]
    for c in range(8):
        b, j = divmod(c, 4)
        idx = j + 4 * np.arange(cfg.TPC)
        idxs.append((b, idx))
        m = dict(sh)
        m["x_fm"] = np.ascontiguousarray(x[b, idx, :].T)
        masks = np.zeros((128, 4 * cfg.QW), BF16NP)
        for r in range(4):
            d = 0 if r <= j else 1
            masks[:, r * cfg.QW:(r + 1) * cfg.QW] = (kk <= qq - d)
        m["masks"] = masks
        in_maps.append(m)
    return in_maps, idxs


def host_assemble(cfg, results, idxs, key="out_fm"):
    out = np.zeros((cfg.B, cfg.S, cfg.D), np.float32)
    for c in range(8):
        b, idx = idxs[c]
        out[b, idx, :] = results[c][key].T
    return out


OFF_END = 212832


def build_program(cfg, layers):
    nc = bass.Bass("TRN2", target_bir_lowering=False, debug=False)
    D, H, DH, E, TPC = cfg.D, cfg.H, cfg.DH, cfg.E, cfg.TPC
    FCH, CB, NCB, QW, NQB = cfg.FCH, cfg.CB, cfg.NCB, cfg.QW, cfg.NQB
    TP, NTP, YW, NYW, KTPR = cfg.TP, cfg.NTP, cfg.YW, cfg.NYW, cfg.KTPR
    NTB = cfg.NTB

    OFF_A = 0
    OFF_B = OFF_A + FCH * TPC * 4
    OFF_C = OFF_B + FCH * TPC * 2
    OFF_D = OFF_C + FCH * TPC * 2
    OFF_SM = OFF_END - 19456

    din = {}
    def dram_in(name, shape, dt):
        din[name] = nc.dram_tensor(name, list(shape), dt, kind="ExternalInput").ap()

    dram_in("x_fm", (D, TPC), F32)
    dram_in("fconst", (128, 8), F32)
    for n, s, dt in [("ones_col", (128, 1), F32), ("ones_r1", (1, 128), F32),
                     ("ones_r1_bf", (1, 128), BF16), ("ones_row_bf", (1, 512), BF16),
                     ("iota_e", (128, 8), F32), ("ident", (128, 128), F32),
                     ("masks", (128, 4 * QW), BF16), ("sel8", (8, 1024), F32),
                     ("iota_col", (128, 1), F32), ("ecap1", (8, 1), F32),
                     ("ecap2", (8, 1), F32)]:
        dram_in(n, s, dt)
    if 0 in layers:
        for n, s, dt in [("l0_wq", (D, D), F32), ("l0_wk", (D, D), F32),
                         ("l0_wv", (2 * 128, FCH * 512), F32),
                         ("l0_wo", (D, D), F32),
                         ("l0_wg", (cfg.NHB_F * 128, FCH * 128), I32),
                         ("l0_wu", (cfg.NHB_F * 128, FCH * 128), I32),
                         ("l0_wd", (FCH * 128, cfg.NHB_F * 128), I32)]:
            dram_in(n, s, dt)
    if 1 in layers:
        for n, s, dt in [("l1_wq", (D, D), F32), ("l1_wk", (D, D), F32),
                         ("l1_wv", (2 * 128, FCH * 512), F32),
                         ("l1_bq", (128, 8), F32),
                         ("l1_bk", (128, 8), F32), ("l1_bv", (1, D), F32),
                         ("l1_wo", (D, D), F32), ("l1_bo", (128, 8), F32),
                         ("l1_rw", (D, 8), F32),
                         ("l1_w1", (E * cfg.NHB_M * 128, 9 * 128), F32),
                         ("l1_b1c", (128, E * cfg.MH // 128), F32),
                         ("l1_w2", (E * cfg.NHB_M * 128, 2 * 512), F32),
                         ("l1_b2", (E, D), F32)]:
            dram_in(n, s, dt)

    out_fm = nc.dram_tensor("out_fm", [D, TPC], F32, kind="ExternalOutput").ap()
    gate_tap = (nc.dram_tensor("tap_gates", [8, TPC], F32, kind="ExternalOutput").ap()
                if "gates" in cfg.debug_taps else None)
    h_tap = (nc.dram_tensor("tap_h", [cfg.MH, TP], F32, kind="ExternalOutput").ap()
             if "moe_h" in cfg.debug_taps else None)
    taps = {t: nc.dram_tensor("tap_" + t, [D, TPC], F32, kind="ExternalOutput").ap()
            for t in cfg.debug_taps if t != "gates"}

    k_contrib = nc.dram_tensor("k_contrib", [D + 1, TPC], BF16, kind="Internal").ap()
    k_gath = nc.dram_tensor("k_gath", [4 * (D + 1), TPC], BF16, kind="Internal").ap()
    v_contrib = nc.dram_tensor("v_contrib", [4 * TPC, 260], BF16, kind="Internal").ap()
    v_gath = nc.dram_tensor("v_gath", [16 * TPC, 260], BF16, kind="Internal").ap()
    q_park = nc.dram_tensor("q_park", [D, TPC], BF16, kind="Internal").ap()
    CAP, CAPC = cfg.CAP, cfg.CAPC
    NCH = E * CAPC
    if cfg.sparse_moe and 1 in layers:
        xn_tm = nc.dram_tensor("xn_tm", [TPC, D], BF16, kind="Internal").ap()
        cT_dram = nc.dram_tensor("cT_dram", [E * TPC, 1], F32, kind="Internal").ap()
        idx_dram = nc.dram_tensor("idx_dram", [NCH * 128, 1], F32, kind="Internal").ap()
        idxA_dram = nc.dram_tensor("idxA_dram", [NCH * 128, 1], F32, kind="Internal").ap()
        idxB_dram = nc.dram_tensor("idxB_dram", [NCH * 128, 1], F32, kind="Internal").ap()
        y1_dram = nc.dram_tensor("y1_dram", [TPC, D], F32, kind="Internal").ap()
        y2_dram = nc.dram_tensor("y2_dram", [TPC, D], F32, kind="Internal").ap()

    SBUF_BASE = 16512

    def sb(name, shape, dt, off):
        nbytes = int(np.prod(shape[1:])) * mybir.dt.size(dt)
        assert off >= 0 and off + nbytes <= OFF_END, (name, off, nbytes)
        return nc.alloc_sbuf_tensor_at(name, list(shape), dt,
                                       offset=SBUF_BASE + off)

    xacc = [sb(f"xacc{i}", (128, TPC), F32, OFF_A + i * TPC * 4) for i in range(FCH)]
    xn = [sb(f"xn{i}", (128, TPC), BF16, OFF_B + i * TPC * 2) for i in range(FCH)]
    kwin = [sb(f"kw{i}", (128, TPC), BF16, OFF_B + i * TPC * 2) for i in range(8)]
    q2 = [sb(f"q2_{i}", (128, TPC), BF16, OFF_C + i * TPC * 2) for i in range(8)]

    sm = [OFF_SM]
    def sbs(name, shape, dt):
        h = sb(name, shape, dt, sm[0])
        sm[0] += (int(np.prod(shape[1:])) * mybir.dt.size(dt) + 31) & ~31
        assert sm[0] <= OFF_END, name
        return h

    fconst_sb = sbs("c_fconst", (128, 8), F32)
    FCONSTS = [0.0, 1.0, 1e-5, -0.5, 0.125, 0.5, -1.0, 1.0 / 127.0]
    for i, v in enumerate(FCONSTS):
        nc.const_aps.register(F32, v, fconst_sb[:, i:i + 1])
    ones_col = sbs("c_ones", (128, 1), F32)
    ones_r1 = sbs("c_or1", (1, 128), F32)
    ones_r1_bf = sbs("c_or1b", (1, 128), BF16)
    ones_row_bf = sbs("c_orow", (1, 512), BF16)
    iota_e = sbs("c_iota", (128, 8), F32)
    sel8_sb = sbs("c_sel8", (8, 1024), F32)
    iota_col = sbs("c_iotac", (128, 1), F32)
    ecap1_sb = sbs("c_ecap1", (8, 1), F32)
    ecap2_sb = sbs("c_ecap2", (8, 1), F32)
    ident = sbs("c_ident", (128, 128), F32)
    masks_sb = sbs("c_masks", (128, 4 * QW), BF16)
    bq_sb = sbs("bq", (128, 8), F32)
    bk_sb = sbs("bk", (128, 8), F32)
    bo_sb = sbs("bo", (128, 8), F32)
    bv_sb = sbs("bv", (1, D), BF16)
    rw_sb = sbs("rw", (128, 64), BF16)
    b1c_sb = sbs("b1c", (128, E * cfg.MH // 128), F32)
    b2_sb = sbs("b2", (8, D), BF16)
    qn_sb = sbs("qn", (1, 8), F32)
    kn_sb = sbs("kn", (1, 8), F32)
    kn4_sb = sbs("kn4", (1, 32), F32)
    ntmp = sbs("ntmp", (1, 16), F32)
    msc = sbs("msc", (1, 8), F32)
    negM1 = sbs("negM1", (1, 1), F32)
    negM = sbs("negM", (128, 1), F32)
    recrow = sbs("recrow", (1, 512), F32)

    AL = lambda x: (x + 31) & ~31
    xsq = sb("xsq", (128, TPC), F32, OFF_D)
    mean_bc = sb("mean_bc", (128, TPC), F32, OFF_D + TPC * 4)
    rstd_bc = sb("rstd_bc", (128, TPC), F32, OFF_D + 2 * TPC * 4)
    mrow = sb("mrow", (1, TPC), F32, OFF_D + 3 * TPC * 4)
    vrow = sb("vrow", (1, TPC), F32, OFF_D + 3 * TPC * 4 + AL(TPC * 4))
    srow = sb("srow", (1, TPC), F32, OFF_D + 3 * TPC * 4 + 2 * AL(TPC * 4))
    o1 = OFF_D + AL(TPC * 4)
    wslab = [sb(f"wsl{i}", (128, FCH, 128), BF16, o1 + i * FCH * 256) for i in range(3)]
    o2 = o1 + 3 * FCH * 256
    vslab = [sb(f"vsl{i}", (128, FCH, 512), BF16, o2 + i * FCH * 1024) for i in range(2)]
    o3 = o2 + 2 * FCH * 1024
    k2tmp = [sb(f"k2t{i}", (128, TPC), BF16, o3 + i * TPC * 2) for i in range(2)]
    o4 = o3 + 2 * TPC * 2
    vstage = [sb(f"vst{i}", (128, 4, 260), BF16, o4 + i * 2080) for i in range(2)]

    HCAP = (8 * TPC) // TP

    def h_tiles(n, tag):
        out = []
        for i in range(n):
            off = OFF_C + i * TP * 2 if i < HCAP else OFF_D + (i - HCAP) * TP * 2
            out.append(sb(f"{tag}{i}", (128, TP), BF16, off))
        return out

    if 0 in layers:
        h_l0 = h_tiles(cfg.NHB_F, "hf")
        f1 = OFF_D + max(0, cfg.NHB_F - HCAP) * TP * 2
        gslab = [sb(f"gsl{i}", (128, FCH, 128), BF16, f1 + i * FCH * 256)
                 for i in range(3)]
        uslab = [sb(f"usl{i}", (128, FCH, 128), BF16, f1 + 3 * FCH * 256 + i * FCH * 256)
                 for i in range(3)]
        g_sb = [sb(f"g_s{i}", (128, CB), BF16, f1 + 6 * FCH * 256 + i * CB * 2)
                for i in range(2)]
        dslab = [sb(f"dsl{i}", (128, cfg.NHB_F, 128), BF16,
                    f1 + 6 * FCH * 256 + 2 * CB * 2 + i * cfg.NHB_F * 256)
                 for i in range(3)]
    if 1 in layers:
        h_moe = h_tiles(cfg.NHB_M, "hm")
        m0 = OFF_D + max(0, cfg.NHB_M - HCAP) * TP * 2
        m1slab = [sb(f"m1s{i}", (128, 9, 128), BF16, m0 + i * 9 * 256)
                  for i in range(3)]
        m1e = m0 + 3 * 9 * 256
        m2slab = [sb(f"m2s{i}", (128, 2, 512), BF16, m1e + i * 2 * 1024)
                  for i in range(2)]
        m2e = m1e + 2 * 2 * 1024
        gb_sb = [sb("gb0", (128, TP), F32, m2e)]
        gbe = m2e + TP * 4
        cT_sb = sb("cT", (8, TPC), F32, gbe)
        cTb_sb = sb("cTb", (8, TPC), BF16, gbe + TPC * 4)
        rl_buf = sb("rl0", (128, CB), BF16, gbe + TPC * 4 + TPC * 2)
        r0 = gbe + TPC * 4 + TPC * 2 + CB * 2
        if cfg.sparse_moe:
            pA_sb = sb("pA", (8, TPC), F32, OFF_C)
            pB_sb = sb("pB", (8, TPC), F32, OFF_C + TPC * 4)
            mA_sb = sb("mA", (8, TPC), F32, OFF_C + 2 * TPC * 4)
            mB_sb = sb("mB", (8, TPC), F32, OFF_C + 3 * TPC * 4)
            HCAP_S = (8 * TPC * 2) // (CAP * 2)
            h_sp = []
            for i in range(cfg.NHB_M):
                off = OFF_C + i * CAP * 2 if i < HCAP_S \
                    else OFF_D + (i - HCAP_S) * CAP * 2
                h_sp.append(sb(f"hs{i}", (128, CAP), BF16, off))
            sp0 = OFF_D + max(0, cfg.NHB_M - HCAP_S) * CAP * 2
            m1slab_s = [sb(f"m1ss{i}", (128, 9, 128), BF16, sp0 + i * 9 * 256)
                        for i in range(3)]
            sp1 = sp0 + 3 * 9 * 256
            m2slab_s = [sb(f"m2ss{i}", (128, 2, 512), BF16, sp1 + i * 2 * 1024)
                        for i in range(2)]
            sp2 = sp1 + 2 * 2 * 1024
            xe_g = [sb(f"xeg{i}", (128, D), BF16, sp2 + i * D * 2) for i in range(2)]
            sp3 = sp2 + 2 * D * 2
            xe_f = sb("xef", (128, D), F32, sp3)
            sp4 = sp3 + D * 4
            gbs_sb = sb("gbs", (128, CAP), F32, sp4)
            sp5 = sp4 + CAP * 4
            grow = sb("grow", (1, CAP), F32, sp5)
            sp6 = sp5 + ((CAP * 4 + 31) & ~31)
            growb = sb("growb", (1, CAP), BF16, sp6)
            sp7 = sp6 + ((CAP * 2 + 31) & ~31)
            gsc = sb("gsc", (128, 128), F32, sp7)
            sp8 = sp7 + 128 * 4
            b2row = [sb(f"b2r{i}", (1, D), BF16, sp8 + i * D * 2) for i in range(2)]
            sp9 = sp8 + 2 * D * 2
            y_fm = [sb(f"yfm{i}", (128, CAP), BF16, sp9 + i * CAP * 2)
                    for i in range(4)]
            spA = sp9 + 4 * CAP * 2
            ytm = [sb(f"ytm{i}", (128, 512), BF16, spA + i * 1024) for i in range(2)]
            spB = spA + 2 * 1024
            cnv = sb("cnv", (128, 128), F32, spB)
            spC = spB + 128 * 4
            zscan = sb("zscan", (8, CB), F32, spC)
            spD = spC + CB * 4
            carry = sb("carry", (8, 1), F32, spD)
            slotc = [sb(f"slc{i}", (128, 1), F32, spD + 32 + i * 32) for i in range(2)]
            slotu = [sb(f"slu{i}", (128, 1), U32, spD + 96 + i * 32) for i in range(2)]
            tokid = sb("tokid", (128, 1), F32, spD + 160)
            gcol_u = sb("gcol_u", (128, 1), U32, spD + 192)
            spE = spD + 224
            oobfill = sb("oobf", (128, NCH), F32, spE)
            spF = spE + ((NCH * 4 + 31) & ~31)
            idx_ld = sb("idx_ld", (NCH, 128), F32, spF)
            spG = spF + 512
            idxu = sb("idxu", (128, NCH), U32, spG)
            idxAu = sb("idxAu", (128, NCH), U32, spG + ((NCH * 4 + 31) & ~31))
            idxBu = sb("idxBu", (128, NCH), U32, spG + 2 * ((NCH * 4 + 31) & ~31))
            idxA2 = sb("idxA2", (128, NCH), U32, spG + 3 * ((NCH * 4 + 31) & ~31))
            idxB2 = sb("idxB2", (128, NCH), U32, spG + 4 * ((NCH * 4 + 31) & ~31))
            spH = spG + 5 * ((NCH * 4 + 31) & ~31)
            xe_need = FCH * CAP * 2
            if xe_need <= TPC * 6:
                xe_fm = [sb(f"xefm{i}", (128, CAP), BF16, gbe + i * CAP * 2)
                         for i in range(FCH)]
            else:
                xe_fm = [sb(f"xefm{i}", (128, CAP), BF16, spH + i * CAP * 2)
                         for i in range(FCH)]
                spH += xe_need
            tm_st = [sb(f"tmst{i}", (128, D), BF16, o4 + 2 * 2080 + i * D * 2)
                     for i in range(2)]
            assert spH <= OFF_SM, (spH, OFF_SM)
            sq_lo = o4 + 2 * 2080 + 2 * D * 2
            if 0 in layers:
                sq_lo = max(sq_lo, f1 + 6 * FCH * 256 + 2 * CB * 2
                            + 3 * cfg.NHB_F * 256)
            sq_lo = AL(sq_lo)
            sq_cap = (((OFF_SM - sq_lo) // 2) // 4) & ~7
            assert sq_cap >= 640, (sq_lo, OFF_SM)
            stga_bufs = [sb(f"stga{i}", (128, sq_cap), F32,
                            sq_lo + i * sq_cap * 4) for i in range(2)]
            stga_bufs_i = [sb(f"stgai{i}", (128, sq_cap), I32,
                             sq_lo + i * sq_cap * 4) for i in range(2)]
            sm_cap = ((FCH * TPC * 2 // 2) // 4) & ~7
            stgm_bufs = [sb(f"stgm{i}", (128, sm_cap), F32,
                            OFF_B + i * sm_cap * 4) for i in range(2)]
        lg_sb = sb("lg", (128, 8), F32, r0)
        r_m8 = sb("r_m8", (128, 8), F32, r0 + 32)
        r_i8 = sb("r_i8", (128, 8), U32, r0 + 64)
        r_if = sb("r_if", (128, 2), F32, r0 + 96)
        r_oh1 = sb("r_oh1", (128, 8), F32, r0 + 128)
        r_oh2 = sb("r_oh2", (128, 8), F32, r0 + 160)
        r_neg = sb("r_neg", (128, 1), F32, r0 + 192)
        r_p = sb("r_p", (128, 8), F32, r0 + 224)
        r_s = sb("r_s", (128, 1), F32, r0 + 256)
        r_t = sb("r_t", (128, 1), F32, r0 + 288)
        r_cf = sb("r_cf", (128, 8), F32, r0 + 320)

    vwstride = (KTPR * 520 + 31) & ~31
    vw = [sb(f"vw{i}", (128, KTPR, 260), BF16, OFF_D + i * vwstride)
          for i in range(4)]
    a0 = OFF_D + 4 * vwstride
    z_sb = [sb(f"z{i}", (128, TPC), BF16, a0 + i * TPC * 2) for i in range(2)]
    a1 = a0 + 2 * TPC * 2
    p_buf = [sb(f"pbf{i}", (128, 2 * QW), BF16, a1 + i * 2 * QW * 2)
             for i in range(2)]
    a3 = a1 + 5 * QW * 2
    oslab = [sb(f"osl{i}", (128, D), BF16, a3 + i * D * 2) for i in range(4)]
    a4 = a3 + 4 * D * 2
    recb = sb("recb", (64, 512), F32, a4)
    assert a4 + 2048 <= OFF_SM, (a4, OFF_SM)

    pq = [nc.alloc_psum_tensor(f"pq{i}", [128, 1024], F32) for i in range(2)]
    pb_s = [nc.alloc_psum_tensor(f"bank{i}", [128, 512], F32) for i in range(4, 8)]
    pbk = [pq[0][:, 0:512], pq[0][:, 512:1024],
           pq[1][:, 0:512], pq[1][:, 512:1024]] + [h[:] for h in pb_s]

    P = Prog()

    BNDREG = {}

    def bndreg(e, v):
        if v not in BNDREG:
            r = e.alloc_register(f"bnd{v}")
            e.reg_mov(r, v)
            BNDREG[v] = r
        return BNDREG[v]

    def dma(q, dst, src, waits=(), sem="MISC"):
        return P.emit(q, lambda e, d=dst, s=src: e.dma_start(out=d, in_=s),
                      waits=waits, inc=sem, amt=16)

    def mm(out, lhsT, rhs, start, stop, waits=(), inc=None):
        return P.emit("tensor",
                      lambda e, o=out, l=lhsT, r=rhs, a=start, b=stop:
                      e.matmul(o, l, r, start=a, stop=b),
                      waits=waits, inc=inc)

    def bk_acq(i, waits=()):
        t = P.bank_r.get(i)
        return (list(waits) + [t]) if t else list(waits)

    def bk_read(i, tok):
        P.bank_r[i] = tok

    def vec(fn, waits=(), inc="DV"):
        return P.emit("vector", fn, waits=waits, inc=inc)

    def act(fn, waits=(), inc="AC"):
        return P.emit("scalar", fn, waits=list(waits) + [cst], inc=inc)

    dma("sync", fconst_sb[:], din["fconst"], sem="CST")
    for h, name in [(ones_col, "ones_col"), (ones_r1, "ones_r1"),
                    (iota_e, "iota_e"), (ident, "ident"),
                    (ones_r1_bf, "ones_r1_bf"), (ones_row_bf, "ones_row_bf"),
                    (masks_sb, "masks"), (sel8_sb, "sel8"),
                    (iota_col, "iota_col"), (ecap1_sb, "ecap1"),
                    (ecap2_sb, "ecap2")]:
        dma("sync", h[:], din[name], sem="CST")
    if 1 in layers:
        for h, name in [(bq_sb, "l1_bq"), (bk_sb, "l1_bk"), (bo_sb, "l1_bo")]:
            dma("sync", h[:], din[name], sem="CST")
        for h, name in [(bv_sb, "l1_bv"), (b2_sb, "l1_b2")]:
            dma("gpsimd", h[:], din[name], sem="CSG")
        dma("sync", b1c_sb[:], din["l1_b1c"], sem="CST")
        dma("gpsimd", rw_sb.ap().rearrange("p (fc e) -> p fc e", e=8),
            din["l1_rw"].rearrange("(fc p) e -> p fc e", p=128), sem="CSG")
    cst = ("CST", P.cnt["CST"])
    cstg = ("CSG", P.cnt["CSG"]) if P.cnt["CSG"] else None

    for fc in range(FCH):
        dma("sync", xacc[fc][:], din["x_fm"][fc * 128:(fc + 1) * 128, :], sem="XL")
    xacc_tok = [("XL", P.cnt["XL"])] * FCH

    zt = P.emit("vector", lambda e: e.memset(mrow[0:1, :], 0.0), inc="DV")
    norm_zero = dma("gpsimd", k_contrib[D:D + 1, :], mrow[0:1, :], waits=[zt],
                    sem="KN")

    st = {
        "xacc": xacc_tok,
        "B_free": [None] * 8,
        "q2_free": [None] * 8,
        "h_free": None,
        "arena_free": None,
        "kgath_free": None,
        "vgath_free": None,
        "qpark_free": None,
        "contrib_free": None,
        "k2tmp_free": [None, None],
        "vstage_free": [None, None],
        "xsq_gate": None,
        "wstream": Stream(P, "gpsimd", "sW", wslab),
        "vstream": Stream(P, "gpsimd", "sV", vslab),
        "ostream": Stream(P, "gpsimd", "sO", oslab),
    }
    stagerA = stagerM = None
    if 1 in layers and cfg.sparse_moe:
        stagerA = Stager(P, "qg", stga_bufs, stga_bufs_i, sq_cap)
        stagerM = Stager(P, "mg", stgm_bufs, None, sm_cap)
        st["wstream"] = Stream(P, "gpsimd", "sW", wslab, stager=stagerA,
                               gsz=128, n_g=FCH, sqs=("gpsimd", "gpsimd"))
        st["vstream"] = Stream(P, "gpsimd", "sV", vslab, stager=stagerA,
                               gsz=512, n_g=FCH, sqs=("gpsimd", "gpsimd"))
    st["stagerQ"], st["stagerM"] = stagerA, stagerM
    if 0 in layers:
        st["gs"] = Stream(P, "gpsimd", "sG", gslab, stager=stagerA,
                          gsz=128, n_g=FCH, sqs=("sync", "sync"), i32=True)
        st["us"] = Stream(P, "gpsimd", "sU", uslab, stager=stagerA,
                          gsz=128, n_g=FCH, sqs=("sync", "sync"), i32=True)
        st["ds"] = Stream(P, "gpsimd", "sD", dslab, stager=stagerA,
                          gsz=128, n_g=cfg.NHB_F, sqs=("sync", "sync"),
                          i32=True)
    if 1 in layers:
        st["m1s"] = Stream(P, "gpsimd", "s1", m1slab)
        st["m2s"] = Stream(P, "gpsimd", "s2", m2slab)
        if cfg.sparse_moe:
            st["m1s_s"] = Stream(P, "gpsimd", "t1", m1slab_s,
                                 stager=stagerM, gsz=128, n_g=9)
            st["m2s_s"] = Stream(P, "gpsimd", "t2", m2slab_s,
                                 stager=stagerM, gsz=512, n_g=2)

    def emit_ln(lid):
        arena_w = [st["arena_free"]] if st["arena_free"] else []
        sum_tok = [None] * NCB
        sq_tok_f = [None] * NCB
        last_sq = st["xsq_gate"]
        for fc in range(FCH):
            for tb in range(NCB):
                w = [st["xacc"][fc], cst]
                if fc == 0:
                    w = bk_acq(tb, w)
                t = mm(pbk[tb][0:1, 0:CB], ones_col[:],
                       xacc[fc][:, tb * CB:(tb + 1) * CB],
                       start=(fc == 0), stop=(fc == FCH - 1),
                       waits=w, inc="TE" if fc == FCH - 1 else None)
                if fc == FCH - 1:
                    sum_tok[tb] = t
            w = [st["xacc"][fc], last_sq] + (arena_w if fc == 0 else [])
            s = act(lambda e, fc=fc: e.activation(xsq[:], xacc[fc][:], AF.Square),
                    waits=w)
            for tb in range(NCB):
                w = [s]
                if fc == 0:
                    w = bk_acq(4 + tb, w)
                t = mm(pbk[4 + tb][0:1, 0:CB], ones_col[:],
                       xsq[:, tb * CB:(tb + 1) * CB],
                       start=(fc == 0), stop=(fc == FCH - 1),
                       waits=w, inc="TE")
                if fc == FCH - 1:
                    sq_tok_f[tb] = t
                last_sq = t
        m_toks = []
        for tb in range(NCB):
            t = vec(lambda e, tb=tb: e.tensor_scalar_mul(
                mrow[0:1, tb * CB:(tb + 1) * CB], pbk[tb][0:1, 0:CB], 1.0 / D),
                waits=[sum_tok[tb], norm_zero] + arena_w)
            bk_read(tb, t)
            m_toks.append(t)
        msq = vec(lambda e: e.tensor_mul(srow[0:1, :], mrow[0:1, :], mrow[0:1, :]),
                  waits=m_toks)
        v_toks = []
        for tb in range(NCB):
            t = vec(lambda e, tb=tb: e.scalar_tensor_tensor(
                vrow[0:1, tb * CB:(tb + 1) * CB], pbk[4 + tb][0:1, 0:CB],
                1.0 / D, srow[0:1, tb * CB:(tb + 1) * CB], ALU.mult, ALU.subtract),
                waits=[sq_tok_f[tb], msq])
            bk_read(4 + tb, t)
            v_toks.append(t)
        t_ln = act(lambda e: e.activation(srow[0:1, :], vrow[0:1, :], AF.Ln,
                                          bias=1e-5), waits=v_toks + [msq])
        t_rs = act(lambda e: e.activation(srow[0:1, :], srow[0:1, :], AF.Exp,
                                          scale=-0.5), waits=[t_ln])
        bc = []
        for tb in range(NCB):
            w = bk_acq(tb, [m_toks[tb], cst])
            t1 = mm(pbk[tb][:, 0:CB], ones_r1[0:1, :],
                    mrow[0:1, tb * CB:(tb + 1) * CB], True, True, w, inc="TE")
            w = bk_acq(4 + tb, [t_rs])
            t2 = mm(pbk[4 + tb][:, 0:CB], ones_r1[0:1, :],
                    srow[0:1, tb * CB:(tb + 1) * CB], True, True, w, inc="TE")
            c1 = vec(lambda e, tb=tb: e.tensor_copy(
                mean_bc[:, tb * CB:(tb + 1) * CB], pbk[tb][:, 0:CB]), waits=[t1])
            c2 = vec(lambda e, tb=tb: e.tensor_copy(
                rstd_bc[:, tb * CB:(tb + 1) * CB], pbk[4 + tb][:, 0:CB]), waits=[t2])
            bk_read(tb, c1)
            bk_read(4 + tb, c2)
            bc += [c1, c2]
        xn_tok = []
        for fc in range(FCH):
            t1 = vec(lambda e, fc=fc: e.tensor_sub(xsq[:], xacc[fc][:], mean_bc[:]),
                     waits=bc + [st["xacc"][fc], last_sq])
            w2 = [t1] + ([st["B_free"][fc]] if st["B_free"][fc] else [])
            t2 = vec(lambda e, fc=fc: e.tensor_mul(xn[fc][:], xsq[:], rstd_bc[:]),
                     waits=w2)
            xn_tok.append(t2)
            last_sq = t2
        st["xsq_gate"] = last_sq
        return xn_tok

    def emit_qkv(lid, xn_tok):
        l0 = (lid == 0)
        pre = f"l{lid}_"
        ws = st["wstream"]
        g_ = xn_tok[-1]
        ws.gate(g_)
        st["vstream"].gate(g_)
        if st["stagerQ"] is not None:
            st["stagerQ"].gate([g_])
        st["k2tmp_free"] = [g_] * 2
        st["vstage_free"] = [g_] * 2

        def proj_qk(which):
            contrib_toks = []
            norm_done = None
            for hp in range(8):
                buf, slot = ws.load(
                    din[pre + ("wq" if which == "q" else "wk")]
                    [hp * 128:(hp + 1) * 128, :].rearrange("p (g c) -> p g c", c=128))
                ld = ws.ld[slot]
                if which == "q":
                    dest, dw = q2[hp], st["q2_free"][hp]
                else:
                    dest = k2tmp[hp % 2]
                    dw = contrib_toks[hp - 2] if hp >= 2 else st["k2tmp_free"][hp % 2]
                evac = []
                for tb in range(NCB):
                    bi = tb % 2
                    w = bk_acq(bi, [ld] + xn_tok)
                    t = None
                    for fc in range(FCH):
                        t = mm(pbk[bi][:, 0:CB], buf[:, fc, :],
                               xn[fc][:, tb * CB:(tb + 1) * CB],
                               start=(fc == 0), stop=(fc == FCH - 1),
                               waits=w if fc == 0 else (),
                               inc="TE" if fc == FCH - 1 else None)
                    ew = [t] + ([dw] if dw and tb == 0 else [])
                    if not l0:
                        b = bq_sb if which == "q" else bk_sb
                        et = vec(lambda e, d=dest, tb=tb, bi=bi, b=b, hp=hp:
                                 e.tensor_scalar_add(d[:, tb * CB:(tb + 1) * CB],
                                                     pbk[bi][:, 0:CB],
                                                     b[:, hp:hp + 1]),
                                 waits=ew + [cst])
                    else:
                        et = vec(lambda e, d=dest, tb=tb, bi=bi:
                                 e.tensor_copy(d[:, tb * CB:(tb + 1) * CB],
                                               pbk[bi][:, 0:CB]), waits=ew)
                    bk_read(bi, et)
                    evac.append(et)
                ws.release(slot, evac[-1])
                s = act(lambda e, d=dest: e.activation(xsq[:], d[:], AF.Square),
                        waits=evac + [st["xsq_gate"]])
                red = []
                for tb in range(NCB):
                    w = bk_acq(6, [s])
                    t = mm(pbk[6][0:1, 0:CB], ones_col[:],
                           xsq[:, tb * CB:(tb + 1) * CB], True, True, w, inc="TE")
                    rt = vec(lambda e, tb=tb: e.reduce_max(
                        ntmp[0:1, tb:tb + 1], pbk[6][0:1, 0:CB], AX.X),
                        waits=[t, norm_done])
                    bk_read(6, rt)
                    red.append(rt)
                dstrow = qn_sb if which == "q" else kn_sb
                norm_done = vec(lambda e, d=dstrow, hp=hp: e.reduce_max(
                    d[0:1, hp:hp + 1], ntmp[0:1, 0:NCB], AX.X), waits=red)
                st["xsq_gate"] = s
                st["xsq_gate"] = red[-1]
                if which == "q":
                    pk = dma("sync", q_park[hp * 128:(hp + 1) * 128, :], q2[hp][:],
                             waits=evac + ([st["qpark_free"]] if st["qpark_free"]
                                           else []), sem=f"pk{hp}")
                    st["q2_free"][hp] = pk
                else:
                    cw = [st["contrib_free"]] if st["contrib_free"] else []
                    kc = dma("sync", k_contrib[hp * 128:(hp + 1) * 128, :],
                             k2tmp[hp % 2][:], waits=evac + cw, sem=f"kc{hp % 2}")
                    contrib_toks.append(kc)
            return contrib_toks, norm_done

        _, qn_done = proj_qk("q")
        k_contrib_toks, kn_done = proj_qk("k")
        st["k2tmp_free"] = [k_contrib_toks[-2], k_contrib_toks[-1]]
        kc_norm = dma("gpsimd", k_contrib[D:D + 1, 0:8], kn_sb[0:1, 0:8],
                      waits=[kn_done], sem="KN")
        k_contrib_toks.append(kc_norm)

        vs = st["vstream"]
        ones_done = [None, None]
        for i in range(2):
            prev = st["vstage_free"][i]
            for q4 in range(4):
                prev = vec(lambda e, i=i, q4=q4: e.memset(
                    vstage[i].ap()[:, q4, :].rearrange("p (h c) -> p h c", c=65)
                    [:, :, 64:65], 1.0), waits=[prev])
            ones_done[i] = prev
        v_contrib_toks = []
        ev_last = None
        for vh in range(2):
            buf, slot = vs.load(din[pre + "wv"][vh * 128:(vh + 1) * 128, :]
                                .rearrange("p (g c) -> p g c", c=512))
            ld = vs.ld[slot]
            for tb in range(NTB):
                bi = 2 + tb % 2
                w = bk_acq(bi, [ld] + xn_tok)
                t = None
                for fc in range(FCH):
                    last = (fc == FCH - 1) and l0
                    t = mm(pbk[bi][:, 0:512], xn[fc][:, tb * 128:(tb + 1) * 128],
                           buf[:, fc, :], start=(fc == 0), stop=last,
                           waits=w if fc == 0 else (), inc="TE" if last else None)
                if not l0:
                    t = mm(pbk[bi][:, 0:512], ones_r1_bf[0:1, :],
                           bv_sb[0:1, vh * 512:(vh + 1) * 512], False, True,
                           waits=[cstg], inc="TE")
                si = tb % 2
                sw = [t, ones_done[si]]
                if len(v_contrib_toks) >= 2:
                    sw.append(v_contrib_toks[-2])
                ev = vec(lambda e, si=si, bi=bi, vh=vh: e.tensor_copy(
                    vstage[si].ap()[:, 2 * vh:2 * vh + 2, :]
                    .rearrange("p q (h c) -> p (q h) c", c=65)[:, :, 0:64],
                    pbk[bi][:, 0:512]), waits=sw)
                bk_read(bi, ev)
                cw = [st["contrib_free"]] if st["contrib_free"] else []
                vc = dma("sync",
                         v_contrib.rearrange("(q t p) c -> p q t c", q=4, p=128)
                         [:, 2 * vh:2 * vh + 2, tb, :],
                         vstage[si][:, 2 * vh:2 * vh + 2, :], waits=[ev] + cw,
                         sem=f"vc{si}")
                v_contrib_toks.append(vc)
                ev_last = ev
            vs.release(slot, ev_last)
        st["vstage_free"] = [v_contrib_toks[-2], v_contrib_toks[-1]]

        coll_w = k_contrib_toks + [v_contrib_toks[-1], v_contrib_toks[-2]]
        st["contrib_done"] = tmax(*(k_contrib_toks + v_contrib_toks))
        if st["kgath_free"]:
            coll_w.append(st["kgath_free"])
        ck = P.emit("gpsimd", lambda e: e.collective_compute(
            "AllGather", ALU.bypass, ins=[k_contrib.opt()], outs=[k_gath.opt()],
            replica_groups=[[0, 1, 2, 3], [4, 5, 6, 7]]), waits=coll_w, inc="CCK")
        cv_w = [v_contrib_toks[-1]]
        if st["vgath_free"]:
            cv_w.append(st["vgath_free"])
        cv = P.emit("gpsimd", lambda e: e.collective_compute(
            "AllGather", ALU.bypass, ins=[v_contrib.opt()], outs=[v_gath.opt()],
            replica_groups=[[0, 1, 2, 3], [4, 5, 6, 7]]), waits=cv_w, inc="CCV")
        st["contrib_free"] = cv
        return ck, cv

    def emit_ffn_l0(xn_tok, qkv_gate):
        gs, us, ds = st["gs"], st["us"], st["ds"]
        for _s in (gs, us, ds):
            for i in range(len(_s.free)):
                _s.free[i] = _s.free[i] + qkv_gate if _s.free[i] else list(qkv_gate)
        NHB = cfg.NHB_F
        h_tok = [None] * NHB
        g_free = [None, None]
        for tp in range(NTP):
            for hb in range(NHB):
                b1, s1 = gs.load(din["l0_wg"][hb * 128:(hb + 1) * 128, :]
                                 .rearrange("p (g c) -> p g c", c=128))
                b2, s2 = us.load(din["l0_wu"][hb * 128:(hb + 1) * 128, :]
                                 .rearrange("p (g c) -> p g c", c=128))
                for tbl in range(TP // CB):
                    col0 = tp * TP + tbl * CB
                    bg, bu = tbl % 2, 2 + tbl % 2
                    w = bk_acq(bg, [gs.ld[s1]] + xn_tok)
                    tg = None
                    for fc in range(FCH):
                        tg = mm(pbk[bg][:, 0:CB], b1[:, fc, :],
                                xn[fc][:, col0:col0 + CB],
                                start=(fc == 0), stop=(fc == FCH - 1),
                                waits=w if fc == 0 else (),
                                inc="TE" if fc == FCH - 1 else None)
                    w = bk_acq(bu, [us.ld[s2]])
                    tu = None
                    for fc in range(FCH):
                        tu = mm(pbk[bu][:, 0:CB], b2[:, fc, :],
                                xn[fc][:, col0:col0 + CB],
                                start=(fc == 0), stop=(fc == FCH - 1),
                                waits=w if fc == 0 else (),
                                inc="TE" if fc == FCH - 1 else None)
                    gi = tbl % 2
                    ga = act(lambda e, gi=gi, bg=bg: e.activation(
                        g_sb[gi][:, 0:CB], pbk[bg][:, 0:CB], AF.Silu,
                        scale=1.0 / 127.0), waits=[tg, g_free[gi]])
                    bk_read(bg, ga)
                    hw = [tu, ga]
                    if tp == 0 and tbl == 0:
                        hw += qkv_gate
                        if hb < HCAP:
                            hw.append(st["q2_free"][hb * TP // TPC])
                        elif st["arena_free"]:
                            hw.append(st["arena_free"])
                    ht = vec(lambda e, hb=hb, tbl=tbl, bu=bu, gi=gi:
                             e.scalar_tensor_tensor(
                                 h_l0[hb][:, tbl * CB:(tbl + 1) * CB],
                                 pbk[bu][:, 0:CB], 1.0 / 127.0, g_sb[gi][:, 0:CB],
                                 ALU.mult, ALU.mult), waits=hw)
                    bk_read(bu, ht)
                    g_free[gi] = ht
                    h_tok[hb] = ht
                gs.release(s1, h_tok[hb])
                us.release(s2, h_tok[hb])
            for ob in range(FCH):
                b3, s3 = ds.load(din["l0_wd"][ob * 128:(ob + 1) * 128, :]
                                 .rearrange("p (g c) -> p g c", c=128))
                last_at = None
                for tbl in range(TP // CB):
                    col0 = tp * TP + tbl * CB
                    bd = 4 + tbl % 2
                    w = bk_acq(bd, [ds.ld[s3]] + h_tok)
                    t = None
                    for hb in range(NHB):
                        t = mm(pbk[bd][:, 0:CB], b3[:, hb, :],
                               h_l0[hb][:, tbl * CB:(tbl + 1) * CB],
                               start=(hb == 0), stop=(hb == NHB - 1),
                               waits=w if hb == 0 else (),
                               inc="TE" if hb == NHB - 1 else None)
                    at = vec(lambda e, ob=ob, col0=col0, bd=bd:
                             e.scalar_tensor_tensor(
                                 xacc[ob][:, col0:col0 + CB], pbk[bd][:, 0:CB],
                                 1.0 / 127.0, xacc[ob][:, col0:col0 + CB],
                                 ALU.mult, ALU.add),
                             waits=[t, st["xacc"][ob]])
                    bk_read(bd, at)
                    st["xacc"][ob] = at
                    last_at = at
                ds.release(s3, last_at)
        return

    def emit_moe(xn_tok, qkv_gate):
        NHB = cfg.NHB_M
        for _s in (st["m1s"], st["m2s"]):
            for i in range(len(_s.free)):
                _s.free[i] = _s.free[i] + qkv_gate if _s.free[i] else list(qkv_gate)
        arena_w = [st["arena_free"]] if st["arena_free"] else []
        cT_toks = []
        rprev = None
        for tb in range(NTB):
            w = bk_acq(7, xn_tok + [cstg] + arena_w)
            t = None
            for fc in range(FCH):
                t = mm(pbk[7][:, 0:8], xn[fc][:, tb * 128:(tb + 1) * 128],
                       rw_sb[:, fc * 8:(fc + 1) * 8],
                       start=(fc == 0), stop=(fc == FCH - 1),
                       waits=w if fc == 0 else (), inc="TE" if fc == FCH - 1 else None)
            c0 = vec(lambda e: e.tensor_copy(lg_sb[:], pbk[7][:, 0:8]),
                     waits=[t, rprev])
            bk_read(7, c0)
            t1 = vec(lambda e: e.max(r_m8[:], lg_sb[:]), waits=[c0])
            t2 = vec(lambda e: e.max_index(r_i8[:], r_m8[:], lg_sb[:]), waits=[t1])
            t2b = vec(lambda e: e.tensor_copy(r_if[:], r_i8[:, 0:2]), waits=[t2])
            t3 = vec(lambda e: e.tensor_single_scalar(r_oh1[:], iota_e[:],
                                                      r_if[:, 0:1], ALU.is_equal),
                     waits=[t2b, cst])
            t7 = vec(lambda e: e.tensor_single_scalar(r_oh2[:], iota_e[:],
                                                      r_if[:, 1:2], ALU.is_equal),
                     waits=[t3])
            t8 = vec(lambda e: e.tensor_scalar_mul(r_neg[:], r_m8[:, 0:1], -1.0),
                     waits=[t1])
            t9 = act(lambda e: e.activation(r_p[:], lg_sb[:], AF.Exp,
                                            bias=r_neg[:, 0:1]), waits=[t8, t2])
            ta = vec(lambda e: e.reduce_sum(r_s[:], r_p[:], AX.X), waits=[t9])
            tb_ = vec(lambda e: e.reciprocal(r_t[:], r_s[:]), waits=[ta])
            tc = vec(lambda e: e.tensor_scalar_mul(r_p[:], r_p[:], r_t[:, 0:1]),
                     waits=[tb_])
            td = vec(lambda e: e.tensor_add(r_oh2[:], r_oh1[:], r_oh2[:]),
                     waits=[t7, tc])
            te_ = vec(lambda e: e.tensor_mul(r_p[:], r_p[:], r_oh2[:]), waits=[td])
            tf = vec(lambda e: e.reduce_sum(r_s[:], r_p[:], AX.X), waits=[te_])
            tg = vec(lambda e: e.tensor_scalar_add(r_s[:], r_s[:], 1e-9), waits=[tf])
            th = vec(lambda e: e.reciprocal(r_t[:], r_s[:]), waits=[tg])
            ti = vec(lambda e: e.tensor_scalar_mul(r_cf[:], r_p[:], r_t[:, 0:1]),
                     waits=[th])
            w = bk_acq(6, [ti, cst])
            tj = P.emit("tensor", lambda e: e.transpose(pbk[6][0:8, 0:128],
                                                        r_cf[:], ident[:]),
                        waits=w, inc="TE")
            tk = vec(lambda e, tb=tb: e.tensor_copy(
                cT_sb[0:8, tb * 128:(tb + 1) * 128], pbk[6][0:8, 0:128]),
                waits=[tj] + arena_w)
            bk_read(6, tk)
            tl = vec(lambda e, tb=tb: e.tensor_copy(
                cTb_sb[0:8, tb * 128:(tb + 1) * 128],
                cT_sb[0:8, tb * 128:(tb + 1) * 128]), waits=[tk])
            cT_toks.append(tl)
            rprev = ti

        if gate_tap is not None:
            dma("sync", gate_tap, cT_sb[0:8, :], waits=cT_toks, sem="OUT")
        m1s, m2s = st["m1s"], st["m2s"]
        h_tok = [None] * NHB
        rl_free = [arena_w[0] if arena_w else None]
        gb_tok = [None]
        for e_i in range(E):
            for tp in range(NTP):
                gi = 0
                gt = None
                for yw in range(TP // YW):
                    w = bk_acq(6, cT_toks + ([gb_tok[gi]] if gb_tok[gi] else [])
                               + (arena_w if e_i == 0 else []))
                    t = mm(pbk[6][:, 0:YW],
                           sel8_sb[0:8, e_i * 128:(e_i + 1) * 128],
                           cT_sb[0:8, tp * TP + yw * YW:tp * TP + (yw + 1) * YW],
                           True, True, waits=w, inc="TE")
                    gt = vec(lambda e, gi=gi, yw=yw: e.tensor_copy(
                        gb_sb[gi][:, yw * YW:(yw + 1) * YW], pbk[6][:, 0:YW]),
                        waits=[t])
                    bk_read(6, gt)
                gb_tok[gi] = gt
                for hb in range(NHB):
                    b1, s1 = m1s.load(
                        din["l1_w1"][(e_i * NHB + hb) * 128:
                                     (e_i * NHB + hb + 1) * 128, :]
                        .rearrange("p (g c) -> p g c", c=128))
                    for tbl in range(TP // CB):
                        col0 = tp * TP + tbl * CB
                        bi = tbl % 2
                        w = bk_acq(bi, [m1s.ld[s1]] + xn_tok)
                        t = None
                        for fc in range(FCH):
                            t = mm(pbk[bi][:, 0:CB], b1[:, fc, :],
                                   xn[fc][:, col0:col0 + CB],
                                   start=(fc == 0), stop=(fc == FCH - 1),
                                   waits=w if fc == 0 else (),
                                   inc="TE" if fc == FCH - 1 else None)
                        rt = act(lambda e, bi=bi, e_i=e_i, hb=hb: e.activation(
                            rl_buf[:, 0:CB], pbk[bi][:, 0:CB], AF.Relu,
                            bias=b1c_sb[:, e_i * NHB + hb:e_i * NHB + hb + 1]),
                            waits=[t, rl_free[0]])
                        bk_read(bi, rt)
                        hw = [rt, gb_tok[gi]]
                        if e_i == 0 and tp == 0 and tbl == 0:
                            hw += qkv_gate
                            if hb < HCAP:
                                hw.append(st["q2_free"][hb * TP // TPC])
                            elif st["arena_free"]:
                                hw.append(st["arena_free"])
                        ht = vec(lambda e, hb=hb, tbl=tbl, gi=gi:
                                 e.tensor_mul(
                                     h_moe[hb][:, tbl * CB:(tbl + 1) * CB],
                                     rl_buf[:, 0:CB],
                                     gb_sb[gi][:, tbl * CB:(tbl + 1) * CB]),
                                 waits=hw)
                        rl_free[0] = ht
                        h_tok[hb] = ht
                    m1s.release(s1, h_tok[hb])
                if h_tap is not None and e_i == 3 and tp == 0:
                    for hb in range(NHB):
                        dma("gpsimd", h_tap[hb * 128:(hb + 1) * 128, :],
                            h_moe[hb][:], waits=[h_tok[hb]], sem="HT")
                    htt = ("HT", P.cnt["HT"])
                    h_tok = [htt] * NHB
                ytiles = [(cb, th) for cb in range(4) for th in range(NYW)]
                for ch in range(2):
                    if e_i == 0:
                        for (cb, th) in ytiles:
                            bi = cb * NYW + th
                            w = bk_acq(bi, cT_toks + [cstg])
                            mm(pbk[bi][:, 0:YW],
                               b2_sb[0:8, ch * 512 + cb * 128:ch * 512 + (cb + 1) * 128],
                               cTb_sb[0:8, tp * TP + th * YW:tp * TP + (th + 1) * YW],
                               True, False, waits=w)
                    last = {}
                    for hbg in range(NHB // 2):
                        b2w, s2 = m2s.load(
                            din["l1_w2"]
                            [((e_i * 2 + ch) * (NHB // 2) + hbg) * 128:
                             ((e_i * 2 + ch) * (NHB // 2) + hbg + 1) * 128, :]
                            .rearrange("p (g c) -> p g c", c=512))
                        slab_last = None
                        for g4 in range(2):
                            hb = hbg * 2 + g4
                            for (cb, th) in ytiles:
                                bi = cb * NYW + th
                                isend = (hb == NHB - 1)
                                isfirst = (e_i > 0 and hb == 0)
                                w = [m2s.ld[s2], h_tok[hb]] if (cb, th) == ytiles[0] else ()
                                if isfirst:
                                    w = bk_acq(bi, list(w) if w else [])
                                t = mm(pbk[bi][:, 0:YW],
                                       b2w[:, g4, cb * 128:(cb + 1) * 128],
                                       h_moe[hb][:, th * YW:(th + 1) * YW],
                                       isfirst, isend, waits=w,
                                       inc="TE" if isend or ((g4 == 1) and
                                                             (cb, th) == ytiles[-1])
                                       else None)
                                if isend:
                                    last[(cb, th)] = t
                                if (g4 == 1) and (cb, th) == ytiles[-1]:
                                    slab_last = t
                        m2s.release(s2, slab_last)
                    for (cb, th) in ytiles:
                        bi = cb * NYW + th
                        ob = ch * 4 + cb
                        col0 = tp * TP + th * YW
                        at = vec(lambda e, ob=ob, col0=col0, bi=bi:
                                 e.tensor_add(xacc[ob][:, col0:col0 + YW],
                                              pbk[bi][:, 0:YW],
                                              xacc[ob][:, col0:col0 + YW]),
                                 waits=[last[(cb, th)], st["xacc"][ob]])
                        bk_read(bi, at)
                        st["xacc"][ob] = at
        return

    def emit_moe_sparse(xn_tok, qkv_gate, xn_tm_toks):
        NHB = cfg.NHB_M
        CAPW = cfg.CAPW
        arena_w = [st["arena_free"]] if st["arena_free"] else []
        IOA = IndirectOffsetOnAxis

        cT_toks = []
        mask_toks = []
        rprev = None
        slot_gate = [st["q2_free"][i] for i in range(8) if st["q2_free"][i]]
        for tb in range(NTB):
            w = bk_acq(7, xn_tok + [cstg] + arena_w)
            t = None
            for fc in range(FCH):
                t = mm(pbk[7][:, 0:8], xn[fc][:, tb * 128:(tb + 1) * 128],
                       rw_sb[:, fc * 8:(fc + 1) * 8],
                       start=(fc == 0), stop=(fc == FCH - 1),
                       waits=w if fc == 0 else (), inc="TE" if fc == FCH - 1 else None)
            c0 = vec(lambda e: e.tensor_copy(lg_sb[:], pbk[7][:, 0:8]),
                     waits=[t, rprev])
            bk_read(7, c0)
            t1 = vec(lambda e: e.max(r_m8[:], lg_sb[:]), waits=[c0])
            t2 = vec(lambda e: e.max_index(r_i8[:], r_m8[:], lg_sb[:]), waits=[t1])
            t2b = vec(lambda e: e.tensor_copy(r_if[:], r_i8[:, 0:2]), waits=[t2])
            t3 = vec(lambda e: e.tensor_single_scalar(r_oh1[:], iota_e[:],
                                                      r_if[:, 0:1], ALU.is_equal),
                     waits=[t2b, cst])
            t7 = vec(lambda e: e.tensor_single_scalar(r_oh2[:], iota_e[:],
                                                      r_if[:, 1:2], ALU.is_equal),
                     waits=[t3])
            t8 = vec(lambda e: e.tensor_scalar_mul(r_neg[:], r_m8[:, 0:1], -1.0),
                     waits=[t1])
            t9 = act(lambda e: e.activation(r_p[:], lg_sb[:], AF.Exp,
                                            bias=r_neg[:, 0:1]), waits=[t8, t2])
            ta = vec(lambda e: e.reduce_sum(r_s[:], r_p[:], AX.X), waits=[t9])
            tb_ = vec(lambda e: e.reciprocal(r_t[:], r_s[:]), waits=[ta])
            tc = vec(lambda e: e.tensor_scalar_mul(r_p[:], r_p[:], r_t[:, 0:1]),
                     waits=[tb_])
            w = bk_acq(6, [t7, cst] + (slot_gate if tb == 0 else []))
            tt1 = P.emit("tensor", lambda e: e.transpose(pbk[6][0:8, 0:128],
                                                         r_oh1[:], ident[:]),
                         waits=w, inc="TE")
            mc1 = vec(lambda e, tb=tb: e.tensor_copy(
                mA_sb[0:8, tb * 128:(tb + 1) * 128], pbk[6][0:8, 0:128]),
                waits=[tt1])
            bk_read(6, mc1)
            w = bk_acq(6, [t7])
            tt2 = P.emit("tensor", lambda e: e.transpose(pbk[6][0:8, 0:128],
                                                         r_oh2[:], ident[:]),
                         waits=w, inc="TE")
            mc2 = vec(lambda e, tb=tb: e.tensor_copy(
                mB_sb[0:8, tb * 128:(tb + 1) * 128], pbk[6][0:8, 0:128]),
                waits=[tt2])
            bk_read(6, mc2)
            mask_toks += [mc1, mc2]
            td = vec(lambda e: e.tensor_add(r_oh2[:], r_oh1[:], r_oh2[:]),
                     waits=[t7, tc, tt1, tt2])
            te_ = vec(lambda e: e.tensor_mul(r_p[:], r_p[:], r_oh2[:]), waits=[td])
            tf = vec(lambda e: e.reduce_sum(r_s[:], r_p[:], AX.X), waits=[te_])
            tg = vec(lambda e: e.tensor_scalar_add(r_s[:], r_s[:], 1e-9), waits=[tf])
            th = vec(lambda e: e.reciprocal(r_t[:], r_s[:]), waits=[tg])
            ti = vec(lambda e: e.tensor_scalar_mul(r_cf[:], r_p[:], r_t[:, 0:1]),
                     waits=[th])
            w = bk_acq(6, [ti, cst])
            tj = P.emit("tensor", lambda e: e.transpose(pbk[6][0:8, 0:128],
                                                        r_cf[:], ident[:]),
                        waits=w, inc="TE")
            tk = vec(lambda e, tb=tb: e.tensor_copy(
                cT_sb[0:8, tb * 128:(tb + 1) * 128], pbk[6][0:8, 0:128]),
                waits=[tj] + arena_w)
            bk_read(6, tk)
            cT_toks.append(tk)
            rprev = ti
        ct_w = dma("sync", cT_dram.rearrange("(a b) c -> a (b c)", a=E),
                   cT_sb[0:8, :], waits=cT_toks, sem="ctd")
        st["stagerM"].gate([ct_w] + list(xn_tm_toks))

        z0 = vec(lambda e: e.memset(zscan[:], 0.0),
                 waits=arena_w + [cstg] + xn_tm_toks)
        z1 = vec(lambda e: e.memset(carry[:], 0.0), waits=[z0])
        z2 = vec(lambda e: e.memset(gsc[:], 0.0), waits=arena_w + xn_tm_toks)
        scan_toks = {}
        carry_tok = z1
        for nm, msk, dst in (("A", mA_sb, pA_sb), ("B", mB_sb, pB_sb)):
            prev = carry_tok
            for c in range(TPC // CB):
                c0_, c1_ = c * CB, (c + 1) * CB
                t = vec(lambda e, msk=msk, dst=dst, c0_=c0_, c1_=c1_:
                        e.tensor_tensor_scan(dst[0:8, c0_:c1_], msk[0:8, c0_:c1_],
                                             zscan[0:8, 0:c1_ - c0_],
                                             carry[0:8, 0:1],
                                             ALU.add, ALU.add),
                        waits=[prev, z0] + mask_toks + slot_gate)
                prev = vec(lambda e, dst=dst, c1_=c1_: e.tensor_copy(
                    carry[0:8, 0:1], dst[0:8, c1_ - 1:c1_]), waits=[t])
            rz = vec(lambda e: e.memset(carry[:], 0.0), waits=[prev])
            carry_tok = rz
            scan_toks[nm] = prev
        import os as _os
        if _os.environ.get("DBG_SCANTAP"):
            dbg_scan = nc.dram_tensor("dbg_scan", [24, TPC], F32,
                                      kind="ExternalOutput").ap()
            dma("sync", dbg_scan[0:8, :], pA_sb[0:8, :],
                waits=[scan_toks["A"]], sem="OUT")
            dma("sync", dbg_scan[8:16, :], mA_sb[0:8, :],
                waits=[scan_toks["A"]], sem="OUT")
            dma("sync", dbg_scan[16:24, :], pB_sb[0:8, :],
                waits=[scan_toks["B"]], sem="OUT")
            _sc_tap = ("OUT", P.cnt["OUT"])
        if _os.environ.get("DBG_NOSLOT"):
            sA4 = scan_toks["A"]
            sB5 = scan_toks["B"]
        else:
            _dummy = None
        if not _os.environ.get("DBG_NOSLOT"):
            _tapw = ([("OUT", P.cnt["OUT"])]
                     if _os.environ.get("DBG_SCANTAP") else [])
            sA1 = vec(lambda e: e.tensor_scalar_add(pA_sb[0:8, :], pA_sb[0:8, :],
                                                ecap1_sb[0:8, 0:1]),
                  waits=[scan_toks["A"], cst] + _tapw)
            import os as _o4
            if _o4.environ.get("DBG_S1TAP"):
                dbg_s1 = nc.dram_tensor("dbg_s1", [8, TPC], F32,
                                        kind="ExternalOutput").ap()
                _t1 = dma("sync", dbg_s1[0:8, :], pA_sb[0:8, :], waits=[sA1],
                          sem="OUT")
                sA1 = _t1
            sA2 = vec(lambda e: e.tensor_mul(pA_sb[0:8, :], pA_sb[0:8, :], mA_sb[0:8, :]),
                      waits=[sA1])
            sA3 = vec(lambda e: e.scalar_tensor_tensor(pA_sb[0:8, :], mA_sb[0:8, :],
                                                       -65536.0, pA_sb[0:8, :],
                                                       ALU.mult, ALU.add), waits=[sA2])
            sA4 = vec(lambda e: e.tensor_scalar_add(pA_sb[0:8, :], pA_sb[0:8, :],
                                                    65536.0),
                      waits=[sA3])
            sB1 = vec(lambda e: e.tensor_scalar_mul(pB_sb[0:8, :], pB_sb[0:8, :], -1.0),
                      waits=[scan_toks["B"]] + _tapw)
            sB2 = vec(lambda e: e.tensor_scalar_add(pB_sb[0:8, :], pB_sb[0:8, :],
                                                    ecap2_sb[0:8, 0:1]), waits=[sB1, cst])
            sB3 = vec(lambda e: e.tensor_mul(pB_sb[0:8, :], pB_sb[0:8, :], mB_sb[0:8, :]),
                      waits=[sB2])
            sB4 = vec(lambda e: e.scalar_tensor_tensor(pB_sb[0:8, :], mB_sb[0:8, :],
                                                       -65536.0, pB_sb[0:8, :],
                                                       ALU.mult, ALU.add), waits=[sB3])
            sB5 = vec(lambda e: e.tensor_scalar_add(pB_sb[0:8, :], pB_sb[0:8, :],
                                                    65536.0),
                      waits=[sB4])

        import os as _o6
        if _o6.environ.get("DBG_SLFIN"):
            dbg_slf = nc.dram_tensor("dbg_slf", [16, TPC], F32,
                                     kind="ExternalOutput").ap()
            _w1 = dma("sync", dbg_slf[0:8, :], pA_sb[0:8, :], waits=[sA4],
                      sem="OUT")
            _w2 = dma("sync", dbg_slf[8:16, :], pB_sb[0:8, :], waits=[sB5],
                      sem="OUT")
            sA4 = _w1
            sB5 = _w2
        ofill = vec(lambda e: e.memset(oobfill[:], 65536.0),
                    waits=arena_w + xn_tm_toks)
        init_toks = {}
        for nm, arr in (("I", idx_dram), ("A", idxA_dram), ("B", idxB_dram)):
            init_toks[nm] = dma("gpsimd",
                                arr.rearrange("(p c) a -> p (c a)", p=128),
                                oobfill[:], waits=[ofill], sem=f"ix{nm}")
        tok_prev = None
        sc_done = {"I": init_toks["I"], "A": init_toks["A"], "B": init_toks["B"]}
        for tb in range(NTB):
            tkt = vec(lambda e, tb=tb: e.tensor_single_scalar(
                tokid[:], iota_col[:], float(tb * 128), ALU.add),
                waits=[cst] + (tok_prev or []) + (xn_tm_toks if tb == 0 else []))
            sl_toks = {}
            for j, (nm, slots, s_last) in enumerate(
                    (("A", pA_sb, sA4), ("B", pB_sb, sB5))):
                w = bk_acq(6, [s_last])
                tt = P.emit("tensor", lambda e, slots=slots, tb=tb: e.transpose(
                    pbk[6][:, 0:8], slots[0:8, tb * 128:(tb + 1) * 128],
                    ident[0:8, 0:8]), waits=w, inc="TE")
                rd = vec(lambda e, j=j: e.tensor_reduce(
                    slotc[j][:], pbk[6][:, 0:8], AX.X, ALU.min), waits=[tt])
                bk_read(6, rd)
                import os as _o5
                if _o5.environ.get("DBG_SCTAP"):
                    if "dbg_sc" not in taps:
                        taps["dbg_sc"] = nc.dram_tensor(
                            "dbg_sc", [128, 2 * NTB], F32,
                            kind="ExternalOutput").ap()
                    rd = P.emit("sync", lambda e, tb=tb, j=j: e.dma_start(
                        out=taps["dbg_sc"][:, 2 * tb + j:2 * tb + j + 1],
                        in_=slotc[j][:], allow_slow_non_contiguous=True),
                        waits=[rd], inc="OUT", amt=16)
                cu = vec(lambda e, j=j: e.tensor_copy(slotu[j][:], slotc[j][:]),
                         waits=[rd] + (tok_prev or []))
                sl_toks[nm] = cu
            this_sc = []
            for nm, j in (("A", 0), ("B", 1)):
                arr = idxA_dram if nm == "A" else idxB_dram
                d1 = P.emit("gpsimd", lambda e, j=j, arr=arr: e.indirect_dma_start(
                    arr, IOA(slotu[j][:], 0), tokid[:], None,
                    bounds_check=bndreg(e, NCH * 128 - 1), oob_is_err=False),
                    waits=[sl_toks[nm], tkt, sc_done[nm]], inc=f"ix{nm}", amt=16)
                sc_done[nm] = d1
                d2 = P.emit("gpsimd", lambda e, j=j: e.indirect_dma_start(
                    idx_dram, IOA(slotu[j][:], 0), tokid[:], None,
                    bounds_check=bndreg(e, NCH * 128 - 1), oob_is_err=False),
                    waits=[sl_toks[nm], tkt, sc_done["I"]], inc="ixI", amt=16)
                sc_done["I"] = d2
                this_sc += [d1, d2]
            tok_prev = this_sc

        import os as _os3
        if _os3.environ.get("DBG_IXTAP"):
            dbg_ix = nc.dram_tensor("dbg_ix", [3, NCH * 128], F32,
                                    kind="ExternalOutput").ap()
            for _k, (_nm, _arr) in enumerate((("I", idx_dram), ("A", idxA_dram),
                                              ("B", idxB_dram))):
                dma("sync", dbg_ix[_k:_k + 1, :],
                    _arr.rearrange("(a) c -> c (a)", a=NCH * 128)
                    if False else _arr.rearrange("a c -> c a"),
                    waits=[sc_done[_nm]], sem="OUT")
        idx_toks = {}
        ld_prev = None
        for nm, arr, dstu in (("I", idx_dram, idxu), ("A", idxA_dram, idxAu),
                              ("B", idxB_dram, idxBu)):
            l1 = dma("gpsimd", idx_ld[:],
                     arr.rearrange("(c p) a -> c (p a)", c=NCH),
                     waits=[sc_done[nm], ld_prev] + xn_tm_toks, sem="ixL")
            w = bk_acq(6, [l1])
            t1 = P.emit("tensor", lambda e: e.transpose(
                pbk[6][:, 0:NCH], idx_ld[0:NCH, :], ident[0:NCH, 0:NCH]),
                waits=w, inc="TE")
            cu = vec(lambda e, dstu=dstu: e.tensor_copy(dstu[:], pbk[6][:, 0:NCH]),
                     waits=[t1])
            bk_read(6, cu)
            idx_toks[nm] = cu
            ld_prev = cu

        m1s, m2s = st["m1s_s"], st["m2s_s"]
        for _s in (m1s, m2s):
            for i in range(len(_s.free)):
                _s.free[i] = _s.free[i] + qkv_gate if _s.free[i] else list(qkv_gate)
        h_tok = [None] * NHB
        rl_free = [None]
        gsc_free = z2
        xe_free = [qkv_gate, qkv_gate]
        xef_free = qkv_gate
        xefm_free = [cT_toks[-1]] + arena_w + [ct_w]
        gbs_free = arena_w
        grow_free = arena_w + xn_tm_toks
        yfm_free = [arena_w + xn_tm_toks] * 4
        ytm_free = [arena_w, arena_w]
        b2r_free = [tmax(*(arena_w + xn_tm_toks))[0] if (arena_w + xn_tm_toks)
                    else None] * 2
        b2r_free = [None, None]
        if xn_tm_toks:
            b2r_free = [xn_tm_toks[-1], xn_tm_toks[-1]]
        iax_free = [None, None]
        gg_prev = [None]
        ysc_tok = None
        for e_i in range(E):
            b2i = e_i % 2
            b2t = dma("gpsimd", b2row[b2i][:], din["l1_b2"][e_i:e_i + 1, :],
                      waits=[b2r_free[b2i]], sem=f"b2r{b2i}")
            g_prev = gsc_free
            grow_toks = []
            xe_toks = [None] * FCH
            ck_tok = [None] * CAPC
            for sc in range(CAPC):
                col = e_i * CAPC + sc
                gg = P.emit("gpsimd", lambda e, col=col, e_i=e_i:
                            e.indirect_dma_start(
                                gsc[:, 0:1], None, cT_dram,
                                IOA(idxu[:, col:col + 1], 0),
                                element_offset=e_i * TPC,
                                bounds_check=bndreg(e, TPC - 1),
                                oob_is_err=False),
                            waits=[idx_toks["I"], g_prev, ct_w, gg_prev[0]],
                            inc="gcl", amt=16)
                gg_prev[0] = gg
                w = bk_acq(6, [gg])
                tt = P.emit("tensor", lambda e: e.transpose(
                    pbk[6][:, 0:128], gsc[:], ident[:]), waits=w, inc="TE")
                g_prev = tt
                gw = [tt] + (grow_free if sc == 0 else [])
                gcp = vec(lambda e, sc=sc: e.tensor_copy(
                    grow[0:1, sc * 128:(sc + 1) * 128], pbk[6][0:1, 0:128]),
                    waits=gw)
                bk_read(6, gcp)
                grow_toks.append(gcp)
                xi = sc % 2
                xg = P.emit("gpsimd", lambda e, xi=xi, col=col: e.indirect_dma_start(
                    xe_g[xi][:], None, xn_tm, IOA(idxu[:, col:col + 1], 0),
                    bounds_check=bndreg(e, TPC - 1), oob_is_err=False),
                    waits=[idx_toks["I"]] + xn_tm_toks +
                          (xe_free[xi] if isinstance(xe_free[xi], list)
                           else [xe_free[xi]]),
                    inc=f"xg{xi}", amt=16)
                cf_ = vec(lambda e, xi=xi: e.tensor_copy(xe_f[:], xe_g[xi][:]),
                          waits=[xg, xef_free] if not isinstance(xef_free, list)
                          else [xg] + xef_free)
                xe_free[xi] = cf_
                last_fc = None
                for fc in range(FCH):
                    w = bk_acq(7, [cf_])
                    t1 = P.emit("tensor", lambda e, fc=fc: e.transpose(
                        pbk[7][:, 0:128], xe_f[:, fc * 128:(fc + 1) * 128],
                        ident[:]), waits=w, inc="TE")
                    xw = [t1] + (xefm_free if (e_i == 0 and sc == 0) else [])
                    if xe_toks[fc]:
                        xw.append(xe_toks[fc])
                    c2 = vec(lambda e, fc=fc, sc=sc: e.tensor_copy(
                        xe_fm[fc][:, sc * 128:(sc + 1) * 128], pbk[7][:, 0:128]),
                        waits=xw)
                    bk_read(7, c2)
                    xe_toks[fc] = c2
                    last_fc = c2
                xef_free = last_fc
                ck_tok[sc] = last_fc
            gsc_free = g_prev
            gbc = vec(lambda e: e.tensor_copy(growb[0:1, :], grow[0:1, :]),
                      waits=grow_toks)
            grow_free = [gbc]
            gb_toks = []
            off = 0
            for w_ in CAPW:
                w = bk_acq(6, grow_toks + [cst])
                t = mm(pbk[6][:, 0:w_], ones_r1[0:1, :], grow[0:1, off:off + w_],
                       True, True, waits=w, inc="TE")
                gc_ = vec(lambda e, off=off, w_=w_: e.tensor_copy(
                    gbs_sb[:, off:off + w_], pbk[6][:, 0:w_]),
                    waits=[t] + (gbs_free if e_i == 0 and off == 0 else []))
                bk_read(6, gc_)
                gb_toks.append(gc_)
                off += w_
            gbs_free = gb_toks
            for hb in range(NHB):
                b1, s1 = m1s.load(
                    din["l1_w1"][(e_i * NHB + hb) * 128:
                                 (e_i * NHB + hb + 1) * 128, :]
                    .rearrange("p (g c) -> p g c", c=128))
                off = 0
                for ti, w_ in enumerate(CAPW):
                    bi = ti % 2
                    w = bk_acq(bi, [m1s.ld[s1], ck_tok[(off + w_) // 128 - 1]])
                    t = None
                    for fc in range(FCH):
                        t = mm(pbk[bi][:, 0:w_], b1[:, fc, :],
                               xe_fm[fc][:, off:off + w_],
                               start=(fc == 0), stop=(fc == FCH - 1),
                               waits=w if fc == 0 else (),
                               inc="TE" if fc == FCH - 1 else None)
                    rt = act(lambda e, bi=bi, e_i=e_i, hb=hb, w_=w_: e.activation(
                        rl_buf[:, 0:w_], pbk[bi][:, 0:w_], AF.Relu,
                        bias=b1c_sb[:, e_i * NHB + hb:e_i * NHB + hb + 1]),
                        waits=[t, rl_free[0]])
                    bk_read(bi, rt)
                    hw = [rt] + gb_toks
                    if e_i == 0 and ti == 0:
                        hw += qkv_gate + slot_gate + [sA4, sB5, idx_toks["B"]]
                    ht = vec(lambda e, hb=hb, off=off, w_=w_: e.tensor_mul(
                        h_sp[hb][:, off:off + w_], rl_buf[:, 0:w_],
                        gbs_sb[:, off:off + w_]), waits=hw)
                    rl_free[0] = ht
                    h_tok[hb] = ht
                    off += w_
                m1s.release(s1, h_tok[hb])
            for ch in range(2):
                ytiles = []
                off = 0
                for w_ in CAPW:
                    for cb in range(4):
                        ytiles.append((cb, off, w_))
                    off += w_
                for k_, (cb, off, w_) in enumerate(ytiles):
                    bi = k_ % 8
                    w = bk_acq(bi, [gbc, b2t, cstg])
                    mm(pbk[bi][:, 0:w_],
                       b2row[b2i][0:1, ch * 512 + cb * 128:ch * 512 + (cb + 1) * 128],
                       growb[0:1, off:off + w_], True, False, waits=w)
                last = {}
                for hbg in range(NHB // 2):
                    b2w, s2 = m2s.load(
                        din["l1_w2"]
                        [((e_i * 2 + ch) * (NHB // 2) + hbg) * 128:
                         ((e_i * 2 + ch) * (NHB // 2) + hbg + 1) * 128, :]
                        .rearrange("p (g c) -> p g c", c=512))
                    slab_last = None
                    for g2 in range(2):
                        hb = hbg * 2 + g2
                        for k_, (cb, off, w_) in enumerate(ytiles):
                            bi = k_ % 8
                            isend = (hb == NHB - 1)
                            wv = [m2s.ld[s2], h_tok[hb]] if k_ == 0 else ()
                            t = mm(pbk[bi][:, 0:w_],
                                   b2w[:, g2, cb * 128:(cb + 1) * 128],
                                   h_sp[hb][:, off:off + w_], False, isend,
                                   waits=wv,
                                   inc="TE" if isend or (g2 == 1 and
                                                         k_ == len(ytiles) - 1)
                                   else None)
                            if isend:
                                last[k_] = t
                            if g2 == 1 and k_ == len(ytiles) - 1:
                                slab_last = t
                    m2s.release(s2, slab_last)
                yev = {}
                for k_, (cb, off, w_) in enumerate(ytiles):
                    bi = k_ % 8
                    ev = vec(lambda e, cb=cb, off=off, w_=w_, bi=bi: e.tensor_copy(
                        y_fm[cb][:, off:off + w_], pbk[bi][:, 0:w_]),
                        waits=[last[k_]] + (yfm_free[cb]
                                            if isinstance(yfm_free[cb], list)
                                            else [yfm_free[cb]]))
                    bk_read(bi, ev)
                    yev.setdefault(cb, []).append(ev)
                iA = vec(lambda e, ch=ch: e.tensor_scalar(
                    idxA2[:], idxAu[:], 2.0, float(ch), ALU.mult, ALU.add),
                    waits=[idx_toks["A"], iax_free[0]])
                iB = vec(lambda e, ch=ch: e.tensor_scalar(
                    idxB2[:], idxBu[:], 2.0, float(ch), ALU.mult, ALU.add),
                    waits=[idx_toks["B"], iax_free[1]])
                cnv_t = None
                for sc in range(CAPC):
                    yi = sc % 2
                    tcp = []
                    for cb in range(4):
                        cc = vec(lambda e, cb=cb, sc=sc: e.tensor_copy(
                            cnv[:], y_fm[cb][:, sc * 128:(sc + 1) * 128]),
                            waits=yev[cb] + [cnv_t])
                        w = bk_acq(7, [cc])
                        t1 = P.emit("tensor", lambda e: e.transpose(
                            pbk[7][:, 0:128], cnv[:], ident[:]), waits=w, inc="TE")
                        cnv_t = t1
                        c2 = vec(lambda e, cb=cb, yi=yi: e.tensor_copy(
                            ytm[yi][:, cb * 128:(cb + 1) * 128], pbk[7][:, 0:128]),
                            waits=([t1] + ytm_free[yi])
                            if isinstance(ytm_free[yi], list)
                            else [t1, ytm_free[yi]])
                        bk_read(7, c2)
                        tcp.append(c2)
                    col = e_i * CAPC + sc
                    s1_ = P.emit("gpsimd", lambda e, yi=yi, col=col, ch=ch:
                                 e.indirect_dma_start(
                                     y1_dram.rearrange("t (h c) -> (t h) c", h=2),
                                     IOA(idxA2[:, col:col + 1], 0),
                                     ytm[yi][:], None,
                                     bounds_check=bndreg(e, 2 * TPC - 1), oob_is_err=False),
                                 waits=tcp + [iA, ysc_tok], inc="yS", amt=16)
                    s2_ = P.emit("gpsimd", lambda e, yi=yi, col=col, ch=ch:
                                 e.indirect_dma_start(
                                     y2_dram.rearrange("t (h c) -> (t h) c", h=2),
                                     IOA(idxB2[:, col:col + 1], 0),
                                     ytm[yi][:], None,
                                     bounds_check=bndreg(e, 2 * TPC - 1), oob_is_err=False),
                                 waits=[iB, s1_], inc="yS", amt=16)
                    ysc_tok = s2_
                    ytm_free[yi] = [s1_, s2_]
                    iax_free = [s1_, s2_]
                for cb in range(4):
                    yfm_free[cb] = yev[cb][-1] if False else yev[cb]
                for cb in range(4):
                    yfm_free[cb] = [ysc_tok]
            b2r_free[b2i] = ("TE", P.cnt["TE"])
        ys_all = ("yS", P.cnt["yS"])
        cmb_prev = [None, None]
        cnv_t = None
        for tbl in range(NTB):
            xi = tbl % 2
            xw = [ys_all] + ([cmb_prev[xi]] if cmb_prev[xi] else [])
            if not isinstance(xe_free[xi], list):
                xw.append(xe_free[xi])
            l1 = dma("gpsimd", xe_g[xi][:], y1_dram[tbl * 128:(tbl + 1) * 128, :],
                     waits=xw, sem=f"yc{xi}")
            l2 = dma("gpsimd", tm_st[xi][:], y2_dram[tbl * 128:(tbl + 1) * 128, :],
                     waits=[ys_all] + ([cmb_prev[xi]] if cmb_prev[xi] else []),
                     sem=f"yd{xi}")
            for fc in range(FCH):
                c1 = vec(lambda e, xi=xi, fc=fc: e.tensor_copy(
                    cnv[:], xe_g[xi][:, fc * 128:(fc + 1) * 128]),
                    waits=[l1, cnv_t])
                w = bk_acq(4 + (fc % 2), [c1])
                t1 = P.emit("tensor", lambda e, fc=fc: e.transpose(
                    pbk[4 + (fc % 2)][:, 0:128], cnv[:], ident[:]),
                    waits=w, inc="TE")
                c1b = vec(lambda e, xi=xi, fc=fc: e.tensor_copy(
                    cnv[:], tm_st[xi][:, fc * 128:(fc + 1) * 128]),
                    waits=[l2, t1])
                cnv_t = c1b
                w2 = bk_acq(6 + (fc % 2), [c1b])
                t2 = P.emit("tensor", lambda e, fc=fc: e.transpose(
                    pbk[6 + (fc % 2)][:, 0:128], cnv[:], ident[:]),
                    waits=w2, inc="TE")
                a1 = vec(lambda e, fc=fc, tbl=tbl: e.tensor_add(
                    xacc[fc][:, tbl * 128:(tbl + 1) * 128],
                    pbk[4 + (fc % 2)][:, 0:128],
                    xacc[fc][:, tbl * 128:(tbl + 1) * 128]),
                    waits=[t1, st["xacc"][fc]])
                bk_read(4 + (fc % 2), a1)
                a2 = vec(lambda e, fc=fc, tbl=tbl: e.tensor_add(
                    xacc[fc][:, tbl * 128:(tbl + 1) * 128],
                    pbk[6 + (fc % 2)][:, 0:128],
                    xacc[fc][:, tbl * 128:(tbl + 1) * 128]),
                    waits=[t2, a1])
                bk_read(6 + (fc % 2), a2)
                st["xacc"][fc] = a2
                cmb_prev[xi] = a2
        return

    def emit_attn(lid, ck, cv, xn_dead):
        l0 = (lid == 0)
        pre = f"l{lid}_"
        for r in range(4):
            dma("gpsimd", kn4_sb[0:1, r * 8:(r + 1) * 8],
                k_gath[r * (D + 1) + D:r * (D + 1) + D + 1, 0:8],
                waits=[ck], sem="K4")
        kn_toks = [("K4", P.cnt["K4"])]
        t1 = vec(lambda e: e.reduce_max(
            msc[0:1, 0:8],
            kn4_sb.ap()[0:1, 0:32].rearrange("a (r h) -> a h r", r=4), AX.X),
            waits=kn_toks)
        t2 = vec(lambda e: e.tensor_mul(msc[0:1, 0:8], msc[0:1, 0:8],
                                        qn_sb[0:1, 0:8]), waits=[t1])
        t3 = vec(lambda e: e.reduce_max(msc[0:1, 0:1], msc[0:1, 0:8], AX.X),
                 waits=[t2])
        t4 = act(lambda e: e.activation(msc[0:1, 0:1], msc[0:1, 0:1], AF.Ln),
                 waits=[t3])
        t5 = act(lambda e: e.activation(msc[0:1, 0:1], msc[0:1, 0:1], AF.Exp,
                                        scale=0.5), waits=[t4])
        t6 = vec(lambda e: e.tensor_scalar(negM1[0:1, 0:1], msc[0:1, 0:1],
                                           -0.1275, -1.0, ALU.mult, ALU.add),
                 waits=[t5])
        w = bk_acq(6, [t6, cst])
        t7 = mm(pbk[6][:, 0:1], ones_r1[0:1, :], negM1[0:1, 0:1], True, True, w,
                inc="TE")
        negM_tok = vec(lambda e: e.tensor_copy(negM[:], pbk[6][:, 0:1]), waits=[t7])
        bk_read(6, negM_tok)

        unpark = []
        for hp in range(8):
            w = [st["q2_free"][hp], st["h_free"]]
            unpark.append(dma("sync", q2[hp][:], q_park[hp * 128:(hp + 1) * 128, :],
                              waits=w, sem=f"up{hp}"))
        st["qpark_free"] = unpark[-1]

        def jmm(n=1):
            for _ in range(n):
                mm(pbk[5][:, 0:QW], masks_sb[:, 0:128], masks_sb[:, 0:QW],
                   True, True, waits=[cst])

        os_ = st["ostream"]
        gate = st["h_free"]
        os_.gate(gate)
        o_last = None
        kg_last = None
        vg_last = None
        rec_chain = gate
        recb_chain = gate
        attn_gate = gate
        slot_exp = [gate, gate]
        slot_av = [gate, gate]
        for quad in range(4):
            k_toks = {}
            for hpi in range(2):
                hp = quad * 2 + hpi
                for r in range(4):
                    ki = hpi * 4 + r
                    w = [ck] + ([st["B_free"][ki]] if st["B_free"][ki] else [])
                    if quad == 0:
                        w += xn_dead
                    t = dma("sync", kwin[ki][:],
                            k_gath[r * (D + 1) + hp * 128:
                                   r * (D + 1) + (hp + 1) * 128, :], waits=w,
                            sem=f"kw{ki}")
                    k_toks[(hpi, r)] = t
                    kg_last = t
            v_toks = {}
            for r in range(4):
                w = [cv, attn_gate]
                t = dma("sync", vw[r][:],
                        v_gath.rearrange("(r q t p) c -> p r q t c", r=4, q=4, p=128)
                        [:, r, quad, :, :], waits=w, sem=f"vq{r}")
                v_toks[r] = t
                vg_last = t
            oz_toks = []
            z_free = [attn_gate, attn_gate]
            for hpi in range(2):
                hp = quad * 2 + hpi
                hp_qk_last = None
                for hi in range(2):
                    hloc = 2 * hpi + hi
                    rlo, rhi = hi * 64, (hi + 1) * 64
                    for qb in range(NQB):
                        diag_kb = qb * QW // 128
                        if l0:
                            kl = [(r, kb) for r in range(4)
                                  for kb in range(diag_kb + 1)]
                        else:
                            kl = [(r, kb) for r in range(4) for kb in range(KTPR)]
                        pairs = [kl[i:i + 2] for i in range(0, len(kl), 2)]
                        np_ = len(pairs)
                        qk_tok = [None] * np_

                        def emit_qk(p_i):
                            pr = pairs[p_i]
                            slot = p_i % 2
                            for j, (r, kb) in enumerate(pr):
                                ww = [k_toks[(hpi, r)]]
                                if j == 0:
                                    ww += [slot_exp[slot], unpark[hp]]
                                qk_tok[p_i] = mm(
                                    pq[slot][:, j * QW:(j + 1) * QW],
                                    kwin[hpi * 4 + r][rlo:rhi,
                                                      kb * 128:(kb + 1) * 128],
                                    q2[hp][rlo:rhi, qb * QW:(qb + 1) * QW],
                                    True, True, waits=ww, inc="TE")

                        emit_qk(0)
                        zlast_t = None
                        for i in range(np_):
                            jmm(1)
                            if i + 1 < np_:
                                emit_qk(i + 1)
                            pr = pairs[i]
                            slot = i % 2
                            W = len(pr) * QW
                            et = act(lambda e, slot=slot, W=W: e.activation(
                                p_buf[slot][:, 0:W], pq[slot][:, 0:W], AF.Exp,
                                bias=negM[:, 0:1], scale=0.125),
                                waits=[qk_tok[i], negM_tok, slot_av[slot]])
                            slot_exp[slot] = et
                            half_tok = [et] * len(pr)
                            for j, (r, kb) in enumerate(pr):
                                if l0 and kb == diag_kb:
                                    half_tok[j] = vec(
                                        lambda e, slot=slot, j=j, r=r:
                                        e.tensor_mul(
                                            p_buf[slot][:, j * QW:(j + 1) * QW],
                                            p_buf[slot][:, j * QW:(j + 1) * QW],
                                            masks_sb[:, r * QW:(r + 1) * QW]),
                                        waits=[et, cst])
                            for j, (r, kb) in enumerate(pr):
                                first = (i == 0 and j == 0)
                                last = (i == np_ - 1 and j == len(pr) - 1)
                                w2 = [half_tok[j], v_toks[r]]
                                if first:
                                    w2 = bk_acq(4, w2)
                                zt = mm(pbk[4][0:65, 0:QW],
                                        vw[r][:, kb, hloc * 65:(hloc + 1) * 65],
                                        p_buf[slot][:, j * QW:(j + 1) * QW],
                                        start=first, stop=last,
                                        waits=w2, inc="TE")
                                slot_av[slot] = zt
                                if last:
                                    zlast_t = zt
                        hp_qk_last = qk_tok[np_ - 1]
                        jmm(6)
                        e1 = act(lambda e: e.activation(
                            recrow[0:1, 0:QW], pbk[4][64:65, 0:QW], AF.Ln),
                            waits=[zlast_t, rec_chain])
                        e2 = act(lambda e: e.activation(
                            recrow[0:1, 0:QW], recrow[0:1, 0:QW], AF.Exp,
                            scale=-1.0), waits=[e1])
                        w = bk_acq(6, [e2])
                        e3 = mm(pbk[6][0:64, 0:QW], ones_r1[0:1, 0:64],
                                recrow[0:1, 0:QW], True, True, w, inc="TE")
                        rec_chain = e3
                        e4 = vec(lambda e: e.tensor_copy(recb[:, 0:QW],
                                                         pbk[6][0:64, 0:QW]),
                                 waits=[e3, recb_chain])
                        bk_read(6, e4)
                        e5 = vec(lambda e, hpi=hpi, hi=hi, qb=qb: e.tensor_mul(
                            z_sb[hpi][hi * 64:(hi + 1) * 64,
                                      qb * QW:(qb + 1) * QW],
                            pbk[4][0:64, 0:QW], recb[:, 0:QW]),
                            waits=[e4, zlast_t, z_free[hpi]])
                        bk_read(4, e5)
                        recb_chain = e5
                        oz_toks.append(e5)
                st["q2_free"][hp] = hp_qk_last
            buf0, s0 = os_.load(din[pre + "wo"]
                                [(quad * 2) * 128:(quad * 2 + 1) * 128, :])
            buf1, s1 = os_.load(din[pre + "wo"]
                                [(quad * 2 + 1) * 128:(quad * 2 + 2) * 128, :])
            for ob in range(FCH):
                for tbl in range(NCB):
                    bo_i = 6 + tbl % 2
                    col0 = tbl * CB
                    w = bk_acq(bo_i, [os_.ld[s0], os_.ld[s1]] + oz_toks)
                    mm(pbk[bo_i][:, 0:CB], buf0[:, ob * 128:(ob + 1) * 128],
                       z_sb[0][:, col0:col0 + CB], True, False, w)
                    t = mm(pbk[bo_i][:, 0:CB], buf1[:, ob * 128:(ob + 1) * 128],
                           z_sb[1][:, col0:col0 + CB], False, True, (), inc="TE")
                    if (not l0) and quad == 0:
                        at = vec(lambda e, ob=ob, col0=col0, bo_i=bo_i:
                                 e.scalar_tensor_tensor(
                                     xacc[ob][:, col0:col0 + CB],
                                     pbk[bo_i][:, 0:CB], bo_sb[:, ob:ob + 1],
                                     xacc[ob][:, col0:col0 + CB],
                                     ALU.add, ALU.add),
                                 waits=[t, st["xacc"][ob], cst])
                    else:
                        at = vec(lambda e, ob=ob, col0=col0, bo_i=bo_i:
                                 e.tensor_add(xacc[ob][:, col0:col0 + CB],
                                              pbk[bo_i][:, 0:CB],
                                              xacc[ob][:, col0:col0 + CB]),
                                 waits=[t, st["xacc"][ob]])
                    bk_read(bo_i, at)
                    st["xacc"][ob] = at
                    o_last = at
            os_.release(s0, o_last)
            os_.release(s1, o_last)
            for ki in range(8):
                st["B_free"][ki] = o_last
            attn_gate = o_last
        for s_i in range(2):
            P.bank_r[2 * s_i] = slot_exp[s_i]
            P.bank_r[2 * s_i + 1] = slot_exp[s_i]
        st["kgath_free"] = kg_last
        st["vgath_free"] = vg_last
        return o_last

    for lid in layers:
        xn_tok = emit_ln(lid)
        ck, cv = emit_qkv(lid, xn_tok)
        xn_tm_toks = []
        if lid == 1 and cfg.sparse_moe:
            stga_last = [t for fl in st["stagerQ"].free for t in fl] \
                if st["stagerQ"] else []
            cnv_tok = ([st["arena_free"]] if st["arena_free"] else []) \
                + st["contrib_done"] + tmax(*stga_last)
            tm_free = [None, None]
            for tb in range(NTB):
                si = tb % 2
                cp_toks = []
                for fc in range(FCH):
                    c1 = vec(lambda e, fc=fc, tb=tb: e.tensor_copy(
                        cnv[:], xn[fc][:, tb * 128:(tb + 1) * 128]),
                        waits=xn_tok + (cnv_tok if isinstance(cnv_tok, list)
                                        else [cnv_tok]))
                    w = bk_acq(7, [c1])
                    t1 = P.emit("tensor", lambda e: e.transpose(
                        pbk[7][:, 0:128], cnv[:], ident[:]), waits=w, inc="TE")
                    cnv_tok = [t1]
                    c2 = vec(lambda e, fc=fc, si=si: e.tensor_copy(
                        tm_st[si][:, fc * 128:(fc + 1) * 128], pbk[7][:, 0:128]),
                        waits=[t1, tm_free[si]] if tm_free[si] else [t1])
                    bk_read(7, c2)
                    cp_toks.append(c2)
                dt_ = dma("sync", xn_tm[tb * 128:(tb + 1) * 128, :], tm_st[si][:],
                          waits=cp_toks, sem=f"tm{si}")
                tm_free[si] = dt_
                xn_tm_toks.append(dt_)
        qkv_gate = [ck, cv]
        if lid == 0:
            emit_ffn_l0(xn_tok, st["contrib_done"])
        elif cfg.sparse_moe:
            emit_moe_sparse(xn_tok, st["contrib_done"], xn_tm_toks)
        else:
            emit_moe(xn_tok, qkv_gate)
        lastacc = max(st["xacc"], key=lambda t: t[1])
        st["h_free"] = lastacc
        if f"mid{lid}" in taps:
            for fc in range(FCH):
                dma("sync", taps[f"mid{lid}"][fc * 128:(fc + 1) * 128, :],
                    xacc[fc][:], waits=[st["xacc"][fc]], sem="OUT")
            ttok = ("OUT", P.cnt["OUT"])
            st["xacc"] = [ttok] * FCH
        xn_dead = [lastacc]
        o_last = emit_attn(lid, ck, cv, xn_dead)
        st["arena_free"] = o_last
        if f"x{lid}" in taps:
            for fc in range(FCH):
                dma("sync", taps[f"x{lid}"][fc * 128:(fc + 1) * 128, :], xacc[fc][:],
                    waits=[st["xacc"][fc]], sem="OUT")

    for fc in range(FCH):
        dma("sync", out_fm[fc * 128:(fc + 1) * 128, :], xacc[fc][:],
            waits=[st["xacc"][fc]], sem="OUT")
    final_v = P.cnt["OUT"]
    P.emit("sync", lambda e: e.nop(), waits=[("OUT", final_v)])

    import contextlib
    ctx = contextlib.ExitStack()
    sems = {s: ctx.enter_context(nc.semaphore(s)) for s in sorted(P.sems)}
    runners = P.replay_into(sems)
    with nc.Block() as blk:
        blk.tensor(lambda e: runners["tensor"](e))
        blk.vector(lambda e: runners["vector"](e))
        blk.scalar(lambda e: runners["scalar"](e))
        blk.sync(lambda e: runners["sync"](e))
        blk.gpsimd(lambda e: runners["gpsimd"](e))
    ctx.close()
    return nc


_BUILD_CACHE = {}


def kernel(**inputs):
    cfg = CFG()
    in_maps, idxs = host_prepare(cfg, inputs)
    key = (cfg.S, cfg.FH, cfg.MH, cfg.layers)
    if key not in _BUILD_CACHE:
        _BUILD_CACHE[key] = build_program(cfg, cfg.layers)
    nc = _BUILD_CACHE[key]
    res = run_bass_kernel_spmd(nc, in_maps, core_ids=list(range(8)))
    return host_assemble(cfg, res.results, idxs)


# version 15
# speedup vs baseline: 1.1862x; 1.1816x over previous
import numpy as np
import ml_dtypes
from collections import defaultdict

import concourse.bass as bass
import concourse.mybir as mybir
from concourse.bass_core_rust import IndirectOffsetOnAxis
from concourse.bass_utils import run_bass_kernel_spmd

F32 = mybir.dt.float32
BF16 = mybir.dt.bfloat16
I32 = mybir.dt.int32
U32 = mybir.dt.uint32
AF = mybir.ActivationFunctionType
ALU = mybir.AluOpType
AX = mybir.AxisListType
BF16NP = ml_dtypes.bfloat16


class CFG:
    def __init__(self, S=8192, FH=3072, MH=4096, layers=(0, 1), debug_taps=()):
        self.S, self.FH, self.MH = S, FH, MH
        self.D, self.H, self.DH, self.E, self.B, self.NC = 1024, 16, 64, 8, 2, 8
        self.layers = tuple(layers)
        self.debug_taps = tuple(debug_taps)
        self.TPC = self.S * self.B // self.NC
        self.CB = min(512, self.TPC)
        self.NCB = self.TPC // self.CB
        self.QW = self.CB
        self.NQB = self.TPC // self.QW
        self.NTB = self.TPC // 128
        self.TP = min(1024, self.TPC)
        self.NTP = self.TPC // self.TP
        self.YW = min(512, self.TP)
        self.NYW = self.TP // self.YW
        self.FCH = self.D // 128
        self.NHB_F = self.FH // 128
        self.NHB_M = self.MH // 128
        self.KTPR = self.TPC // 128
        self.sparse_moe = True
        cap = int(np.ceil(self.TPC * 0.375 / 128.0)) * 128
        self.CAP = max(128, cap)
        self.CAPC = self.CAP // 128
        self.CAPW = [512] * (self.CAP // 512) + ([self.CAP % 512] if self.CAP % 512 else [])
        assert self.FH % 256 == 0 and self.MH % 512 == 0 and self.TPC % 128 == 0


ENGINES = ("tensor", "vector", "scalar", "sync", "gpsimd")


def tmax(*toks):
    best = {}
    for t in toks:
        if t is None:
            continue
        s, v = t
        if v > best.get(s, -1):
            best[s] = v
    return [(s, v) for s, v in best.items()]


class Prog:
    def __init__(self):
        self.ev = {e: [] for e in ENGINES}
        self.cnt = defaultdict(int)
        self.sems = set()
        self.bank_r = {}

    def emit(self, eng, fn, waits=(), inc=None, amt=1):
        w = [t for t in waits if t is not None]
        tok = None
        if inc is not None:
            self.sems.add(inc)
            self.cnt[inc] += amt
            tok = (inc, self.cnt[inc])
        self.ev[eng].append((fn, w, inc, amt))
        return tok

    def replay_into(self, sems):
        waited = {e: defaultdict(int) for e in ENGINES}

        def mk(eng_name):
            def run(eng):
                for fn, waits, inc, amt in self.ev[eng_name]:
                    for s, v in waits:
                        if v > waited[eng_name][s]:
                            eng.wait_ge(sems[s], v)
                            waited[eng_name][s] = v
                    inst = fn(eng)
                    if inc is not None:
                        inst.then_inc(sems[inc], amt)
            return run
        return {e: mk(e) for e in ENGINES}


class Stager:

    def __init__(self, P, tag, bufs, bufs_i32, cap):
        self.P, self.tag, self.cap = P, tag, cap
        self.bufs, self.bufs_i32 = bufs, bufs_i32
        self.free = [[] for _ in bufs]
        self.i = 0

    def gate(self, toks):
        self.free = [list(toks) for _ in self.bufs]

    def xfer(self, src, slab, gsz, n_g, waits, sqs, i32=False):
        maxg = max(1, self.cap // gsz)
        slab2 = slab.ap().rearrange("p g c -> p (g c)") if n_g > 1 else slab[:]
        tok = None
        g0 = 0
        while g0 < n_g:
            g1 = min(n_g, g0 + maxg)
            s = self.i % len(self.bufs)
            self.i += 1
            stg = (self.bufs_i32 if i32 else self.bufs)[s]
            n = (g1 - g0) * gsz
            if n_g > 1:
                dst = stg[:, 0:n].rearrange("p (g c) -> p g c", c=gsz)
                sp = src[:, g0:g1, :]
            else:
                dst = stg[:, 0:n]
                sp = src
            q = sqs[s % len(sqs)]
            d = self.P.emit(q,
                            lambda e, o=dst, i_=sp: e.dma_start(out=o, in_=i_),
                            waits=self.free[s], inc=f"{self.tag}{q[0]}{s}",
                            amt=16)
            c = self.P.emit("vector",
                            lambda e, o=slab2[:, g0 * gsz:g0 * gsz + n],
                            i_=stg[:, 0:n]: e.tensor_copy(o, i_),
                            waits=[d] + (list(waits) if g0 == 0 else []),
                            inc="DV")
            self.free[s] = [c]
            tok = c
            g0 = g1
        return tok


class Stream:

    def __init__(self, P, q, tag, bufs, stager=None, gsz=None, n_g=1,
                 sqs=("sync", "sync"), i32=False):
        self.P, self.q, self.bufs = P, q, bufs
        self.tag = tag
        self.i = 0
        self.free = [[] for _ in bufs]
        self.ld = [None] * len(bufs)
        self.stager, self.gsz, self.n_g = stager, gsz, n_g
        self.sqs, self.i32 = sqs, i32

    def load(self, src_ap):
        s = self.i % len(self.bufs)
        self.i += 1
        buf = self.bufs[s]
        if self.stager is not None:
            self.ld[s] = self.stager.xfer(src_ap, buf, self.gsz, self.n_g,
                                          self.free[s], self.sqs, self.i32)
        else:
            self.ld[s] = self.P.emit(
                self.q, lambda e, d=buf, src=src_ap: e.dma_start(out=d[:], in_=src),
                waits=self.free[s], inc=f"{self.tag}{s}", amt=16)
        return buf, s

    def release(self, s, tok):
        self.free[s] = [tok]

    def gate(self, tok):
        self.free = [[tok] for _ in self.bufs]



def slabize(wT, nsl):
    G = wT.shape[0] // 128
    W = wT.shape[1] // nsl
    return np.ascontiguousarray(
        wT.reshape(G, 128, nsl, W).transpose(2, 1, 0, 3).reshape(nsl * 128, G * W))


def host_prepare(cfg, inp):
    D, H, DH, E, S = cfg.D, cfg.H, cfg.DH, cfg.E, cfg.S
    sh = {}
    if 0 in cfg.layers:
        sh["l0_wq"] = slabize(
            np.asarray(inp["L0_attn_W_Q"]).transpose(2, 0, 1).reshape(D, H * DH), 8)
        sh["l0_wk"] = slabize(
            np.asarray(inp["L0_attn_W_K"]).transpose(2, 0, 1).reshape(D, H * DH), 8)
        sh["l0_wv"] = slabize(
            np.asarray(inp["L0_attn_W_V"]).transpose(2, 0, 1).reshape(D, H * DH), 2)
        sh["l0_wo"] = np.ascontiguousarray(np.asarray(inp["L0_attn_W_O"]).T)
        sh["l0_wg"] = slabize(np.asarray(inp["L0_ffn_w_gate"]).T, cfg.NHB_F)
        sh["l0_wu"] = slabize(np.asarray(inp["L0_ffn_w_up"]).T, cfg.NHB_F)
        sh["l0_wd"] = slabize(np.asarray(inp["L0_ffn_w_down"]).T, cfg.FCH)
    if 1 in cfg.layers:
        wqkv = np.asarray(inp["L1_attn_Wqkv"]).reshape(H, 3 * DH, D)
        bqkv = np.asarray(inp["L1_attn_bqkv"]).reshape(H, 3 * DH)
        sh["l1_wq"] = slabize(
            wqkv[:, 0:DH, :].transpose(2, 0, 1).reshape(D, H * DH), 8)
        sh["l1_wk"] = slabize(
            wqkv[:, DH:2 * DH, :].transpose(2, 0, 1).reshape(D, H * DH), 8)
        sh["l1_wv"] = slabize(
            wqkv[:, 2 * DH:, :].transpose(2, 0, 1).reshape(D, H * DH), 2)
        sh["l1_bq"] = np.ascontiguousarray(bqkv[:, 0:DH].reshape(8, 128).T)
        sh["l1_bk"] = np.ascontiguousarray(bqkv[:, DH:2 * DH].reshape(8, 128).T)
        sh["l1_bv"] = np.ascontiguousarray(bqkv[:, 2 * DH:].reshape(1, H * DH))
        sh["l1_wo"] = np.ascontiguousarray(np.asarray(inp["L1_attn_Wo"]).T)
        sh["l1_bo"] = np.ascontiguousarray(
            np.asarray(inp["L1_attn_bo"]).reshape(8, 128).T)
        sh["l1_rw"] = np.ascontiguousarray(np.asarray(inp["L1_ffn_router_w"]).T)
        w1 = np.asarray(inp["L1_ffn_w1"])
        b1 = np.asarray(inp["L1_ffn_b1"])
        w2 = np.asarray(inp["L1_ffn_w2"])
        w1aug = np.zeros((E, D + 128, cfg.MH), np.float32)
        for e in range(E):
            w1aug[e, :D, :] = w1[e].T
            w1aug[e, D, :] = b1[e]
        sh["l1_w1"] = np.concatenate(
            [slabize(w1aug[e], cfg.NHB_M) for e in range(E)], axis=0)
        sh["l1_b1c"] = np.ascontiguousarray(
            b1.reshape(E, cfg.MH // 128, 128).transpose(2, 0, 1).reshape(128, -1))
        NH2 = cfg.NHB_M // 2
        w2T = w2.transpose(0, 2, 1)
        sh["l1_w2"] = np.ascontiguousarray(
            w2T.reshape(E, NH2, 2, 128, 2, 512).transpose(0, 4, 1, 3, 2, 5)
            .reshape(E * 2 * NH2 * 128, 2 * 512))
        sh["l1_b2"] = np.ascontiguousarray(np.asarray(inp["L1_ffn_b2"]))

    sh["fconst"] = np.broadcast_to(
        np.array([0.0, 1.0, 1e-5, -0.5, 0.125, 0.5, -1.0, 1.0 / 127.0],
                 np.float32), (128, 8)).copy()
    sh["ones_col"] = np.ones((128, 1), np.float32)
    sh["ones_r1"] = np.ones((1, 128), np.float32)
    sh["ones_r1_bf"] = np.ones((1, 128), BF16NP)
    sh["ones_row_bf"] = np.ones((1, 512), BF16NP)
    sh["iota_e"] = np.broadcast_to(np.arange(8, dtype=np.float32), (128, 8)).copy()
    sh["iota_col"] = np.arange(128, dtype=np.float32).reshape(128, 1)
    sh["ecap1"] = (np.arange(8, dtype=np.float32) * cfg.CAP - 1.0).reshape(8, 1)
    sh["ecap2"] = (np.arange(8, dtype=np.float32) * cfg.CAP + cfg.CAP).reshape(8, 1)
    sel8 = np.zeros((8, 8 * 128), np.float32)
    for e in range(8):
        sel8[e, e * 128:(e + 1) * 128] = 1.0
    sh["sel8"] = sel8
    sh["ident"] = np.eye(128, dtype=np.float32)

    x = np.asarray(inp["L0_attn_x"])
    in_maps, idxs = [], []
    kk = np.arange(128)[:, None]
    qq = np.arange(cfg.QW)[None, :]
    for c in range(8):
        b, j = divmod(c, 4)
        idx = j + 4 * np.arange(cfg.TPC)
        idxs.append((b, idx))
        m = dict(sh)
        m["x_fm"] = np.ascontiguousarray(x[b, idx, :].T)
        masks = np.zeros((128, 4 * cfg.QW), BF16NP)
        for r in range(4):
            d = 0 if r <= j else 1
            masks[:, r * cfg.QW:(r + 1) * cfg.QW] = (kk <= qq - d)
        m["masks"] = masks
        in_maps.append(m)
    return in_maps, idxs


def host_assemble(cfg, results, idxs, key="out_fm"):
    out = np.zeros((cfg.B, cfg.S, cfg.D), np.float32)
    for c in range(8):
        b, idx = idxs[c]
        out[b, idx, :] = results[c][key].T
    return out


OFF_END = 212832


def build_program(cfg, layers):
    nc = bass.Bass("TRN2", target_bir_lowering=False, debug=False)
    D, H, DH, E, TPC = cfg.D, cfg.H, cfg.DH, cfg.E, cfg.TPC
    FCH, CB, NCB, QW, NQB = cfg.FCH, cfg.CB, cfg.NCB, cfg.QW, cfg.NQB
    TP, NTP, YW, NYW, KTPR = cfg.TP, cfg.NTP, cfg.YW, cfg.NYW, cfg.KTPR
    NTB = cfg.NTB

    OFF_A = 0
    OFF_B = OFF_A + FCH * TPC * 4
    OFF_C = OFF_B + FCH * TPC * 2
    OFF_D = OFF_C + FCH * TPC * 2
    OFF_SM = OFF_END - 19456

    din = {}
    def dram_in(name, shape, dt):
        din[name] = nc.dram_tensor(name, list(shape), dt, kind="ExternalInput").ap()

    dram_in("x_fm", (D, TPC), F32)
    dram_in("fconst", (128, 8), F32)
    for n, s, dt in [("ones_col", (128, 1), F32), ("ones_r1", (1, 128), F32),
                     ("ones_r1_bf", (1, 128), BF16), ("ones_row_bf", (1, 512), BF16),
                     ("iota_e", (128, 8), F32), ("ident", (128, 128), F32),
                     ("masks", (128, 4 * QW), BF16), ("sel8", (8, 1024), F32),
                     ("iota_col", (128, 1), F32), ("ecap1", (8, 1), F32),
                     ("ecap2", (8, 1), F32)]:
        dram_in(n, s, dt)
    if 0 in layers:
        for n, s, dt in [("l0_wq", (D, D), F32), ("l0_wk", (D, D), F32),
                         ("l0_wv", (2 * 128, FCH * 512), F32),
                         ("l0_wo", (D, D), F32),
                         ("l0_wg", (cfg.NHB_F * 128, FCH * 128), I32),
                         ("l0_wu", (cfg.NHB_F * 128, FCH * 128), I32),
                         ("l0_wd", (FCH * 128, cfg.NHB_F * 128), I32)]:
            dram_in(n, s, dt)
    if 1 in layers:
        for n, s, dt in [("l1_wq", (D, D), F32), ("l1_wk", (D, D), F32),
                         ("l1_wv", (2 * 128, FCH * 512), F32),
                         ("l1_bq", (128, 8), F32),
                         ("l1_bk", (128, 8), F32), ("l1_bv", (1, D), F32),
                         ("l1_wo", (D, D), F32), ("l1_bo", (128, 8), F32),
                         ("l1_rw", (D, 8), F32),
                         ("l1_w1", (E * cfg.NHB_M * 128, 9 * 128), F32),
                         ("l1_b1c", (128, E * cfg.MH // 128), F32),
                         ("l1_w2", (E * cfg.NHB_M * 128, 2 * 512), F32),
                         ("l1_b2", (E, D), F32)]:
            dram_in(n, s, dt)

    out_fm = nc.dram_tensor("out_fm", [D, TPC], F32, kind="ExternalOutput").ap()
    gate_tap = (nc.dram_tensor("tap_gates", [8, TPC], F32, kind="ExternalOutput").ap()
                if "gates" in cfg.debug_taps else None)
    h_tap = (nc.dram_tensor("tap_h", [cfg.MH, TP], F32, kind="ExternalOutput").ap()
             if "moe_h" in cfg.debug_taps else None)
    taps = {t: nc.dram_tensor("tap_" + t, [D, TPC], F32, kind="ExternalOutput").ap()
            for t in cfg.debug_taps if t != "gates"}

    k_contrib = nc.dram_tensor("k_contrib", [D + 1, TPC], BF16, kind="Internal").ap()
    k_gath = nc.dram_tensor("k_gath", [4 * (D + 1), TPC], BF16, kind="Internal").ap()
    v_contrib = nc.dram_tensor("v_contrib", [4 * TPC, 260], BF16, kind="Internal").ap()
    v_gath = nc.dram_tensor("v_gath", [16 * TPC, 260], BF16, kind="Internal").ap()
    q_park = nc.dram_tensor("q_park", [D, TPC], BF16, kind="Internal").ap()
    CAP, CAPC = cfg.CAP, cfg.CAPC
    NCH = E * CAPC
    if cfg.sparse_moe and 1 in layers:
        xn_tm = nc.dram_tensor("xn_tm", [TPC, D], BF16, kind="Internal").ap()
        cT_dram = nc.dram_tensor("cT_dram", [E * TPC, 1], F32, kind="Internal").ap()
        idx_dram = nc.dram_tensor("idx_dram", [NCH * 128, 1], F32, kind="Internal").ap()
        idxA_dram = nc.dram_tensor("idxA_dram", [NCH * 128, 1], F32, kind="Internal").ap()
        idxB_dram = nc.dram_tensor("idxB_dram", [NCH * 128, 1], F32, kind="Internal").ap()
        y1_dram = nc.dram_tensor("y1_dram", [TPC, D], F32, kind="Internal").ap()
        y2_dram = nc.dram_tensor("y2_dram", [TPC, D], F32, kind="Internal").ap()

    SBUF_BASE = 16512

    def sb(name, shape, dt, off):
        nbytes = int(np.prod(shape[1:])) * mybir.dt.size(dt)
        assert off >= 0 and off + nbytes <= OFF_END, (name, off, nbytes)
        return nc.alloc_sbuf_tensor_at(name, list(shape), dt,
                                       offset=SBUF_BASE + off)

    xacc = [sb(f"xacc{i}", (128, TPC), F32, OFF_A + i * TPC * 4) for i in range(FCH)]
    xn = [sb(f"xn{i}", (128, TPC), BF16, OFF_B + i * TPC * 2) for i in range(FCH)]
    kwin = [sb(f"kw{i}", (128, TPC), BF16, OFF_B + i * TPC * 2) for i in range(8)]
    q2 = [sb(f"q2_{i}", (128, TPC), BF16, OFF_C + i * TPC * 2) for i in range(8)]

    sm = [OFF_SM]
    def sbs(name, shape, dt):
        h = sb(name, shape, dt, sm[0])
        sm[0] += (int(np.prod(shape[1:])) * mybir.dt.size(dt) + 31) & ~31
        assert sm[0] <= OFF_END, name
        return h

    fconst_sb = sbs("c_fconst", (128, 8), F32)
    FCONSTS = [0.0, 1.0, 1e-5, -0.5, 0.125, 0.5, -1.0, 1.0 / 127.0]
    for i, v in enumerate(FCONSTS):
        nc.const_aps.register(F32, v, fconst_sb[:, i:i + 1])
    ones_col = sbs("c_ones", (128, 1), F32)
    ones_r1 = sbs("c_or1", (1, 128), F32)
    ones_r1_bf = sbs("c_or1b", (1, 128), BF16)
    ones_row_bf = sbs("c_orow", (1, 512), BF16)
    iota_e = sbs("c_iota", (128, 8), F32)
    sel8_sb = sbs("c_sel8", (8, 1024), F32)
    iota_col = sbs("c_iotac", (128, 1), F32)
    ecap1_sb = sbs("c_ecap1", (8, 1), F32)
    ecap2_sb = sbs("c_ecap2", (8, 1), F32)
    ident = sbs("c_ident", (128, 128), F32)
    masks_sb = sbs("c_masks", (128, 4 * QW), BF16)
    bq_sb = sbs("bq", (128, 8), F32)
    bk_sb = sbs("bk", (128, 8), F32)
    bo_sb = sbs("bo", (128, 8), F32)
    bv_sb = sbs("bv", (1, D), BF16)
    rw_sb = sbs("rw", (128, 64), BF16)
    b1c_sb = sbs("b1c", (128, E * cfg.MH // 128), F32)
    b2_sb = sbs("b2", (8, D), BF16)
    qn_sb = sbs("qn", (1, 8), F32)
    kn_sb = sbs("kn", (1, 8), F32)
    kn4_sb = sbs("kn4", (1, 32), F32)
    ntmp = sbs("ntmp", (1, 16), F32)
    msc = sbs("msc", (1, 8), F32)
    negM1 = sbs("negM1", (1, 1), F32)
    negM = sbs("negM", (128, 1), F32)
    recrow = sbs("recrow", (1, 512), F32)

    AL = lambda x: (x + 31) & ~31
    xsq = sb("xsq", (128, TPC), F32, OFF_D)
    mean_bc = sb("mean_bc", (128, TPC), F32, OFF_D + TPC * 4)
    rstd_bc = sb("rstd_bc", (128, TPC), F32, OFF_D + 2 * TPC * 4)
    mrow = sb("mrow", (1, TPC), F32, OFF_D + 3 * TPC * 4)
    vrow = sb("vrow", (1, TPC), F32, OFF_D + 3 * TPC * 4 + AL(TPC * 4))
    srow = sb("srow", (1, TPC), F32, OFF_D + 3 * TPC * 4 + 2 * AL(TPC * 4))
    o1 = OFF_D + AL(TPC * 4)
    wslab = [sb(f"wsl{i}", (128, FCH, 128), BF16, o1 + i * FCH * 256) for i in range(3)]
    o2 = o1 + 3 * FCH * 256
    vslab = [sb(f"vsl{i}", (128, FCH, 512), BF16, o2 + i * FCH * 1024) for i in range(2)]
    o3 = o2 + 2 * FCH * 1024
    k2tmp = [sb(f"k2t{i}", (128, TPC), BF16, o3 + i * TPC * 2) for i in range(2)]
    o4 = o3 + 2 * TPC * 2
    vstage = [sb(f"vst{i}", (128, 4, 260), BF16, o4 + i * 2080) for i in range(2)]

    HCAP = (8 * TPC) // TP

    def h_tiles(n, tag):
        out = []
        for i in range(n):
            off = OFF_C + i * TP * 2 if i < HCAP else OFF_D + (i - HCAP) * TP * 2
            out.append(sb(f"{tag}{i}", (128, TP), BF16, off))
        return out

    if 0 in layers:
        h_l0 = h_tiles(cfg.NHB_F, "hf")
        f1 = OFF_D + max(0, cfg.NHB_F - HCAP) * TP * 2
        gslab = [sb(f"gsl{i}", (128, FCH, 128), BF16, f1 + i * FCH * 256)
                 for i in range(3)]
        uslab = [sb(f"usl{i}", (128, FCH, 128), BF16, f1 + 3 * FCH * 256 + i * FCH * 256)
                 for i in range(3)]
        g_sb = [sb(f"g_s{i}", (128, CB), BF16, f1 + 6 * FCH * 256 + i * CB * 2)
                for i in range(2)]
        dslab = [sb(f"dsl{i}", (128, cfg.NHB_F, 128), BF16,
                    f1 + 6 * FCH * 256 + 2 * CB * 2 + i * cfg.NHB_F * 256)
                 for i in range(3)]
    if 1 in layers:
        h_moe = h_tiles(cfg.NHB_M, "hm")
        m0 = OFF_D + max(0, cfg.NHB_M - HCAP) * TP * 2
        m1slab = [sb(f"m1s{i}", (128, 9, 128), BF16, m0 + i * 9 * 256)
                  for i in range(3)]
        m1e = m0 + 3 * 9 * 256
        m2slab = [sb(f"m2s{i}", (128, 2, 512), BF16, m1e + i * 2 * 1024)
                  for i in range(2)]
        m2e = m1e + 2 * 2 * 1024
        gb_sb = [sb("gb0", (128, TP), F32, m2e)]
        gbe = m2e + TP * 4
        cT_sb = sb("cT", (8, TPC), F32, gbe)
        cTb_sb = sb("cTb", (8, TPC), BF16, gbe + TPC * 4)
        rl_buf = sb("rl0", (128, CB), BF16, gbe + TPC * 4 + TPC * 2)
        r0 = gbe + TPC * 4 + TPC * 2 + CB * 2
        if cfg.sparse_moe:
            pA_sb = sb("pA", (8, TPC), F32, OFF_C)
            pB_sb = sb("pB", (8, TPC), F32, OFF_C + TPC * 4)
            mA_sb = sb("mA", (8, TPC), F32, OFF_C + 2 * TPC * 4)
            mB_sb = sb("mB", (8, TPC), F32, OFF_C + 3 * TPC * 4)
            HCAP_S = (8 * TPC * 2) // (CAP * 2)
            h_sp = []
            for i in range(cfg.NHB_M):
                off = OFF_C + i * CAP * 2 if i < HCAP_S \
                    else OFF_D + (i - HCAP_S) * CAP * 2
                h_sp.append(sb(f"hs{i}", (128, CAP), BF16, off))
            sp0 = OFF_D + max(0, cfg.NHB_M - HCAP_S) * CAP * 2
            m1slab_s = [sb(f"m1ss{i}", (128, 9, 128), BF16, sp0 + i * 9 * 256)
                        for i in range(3)]
            sp1 = sp0 + 3 * 9 * 256
            m2slab_s = [sb(f"m2ss{i}", (128, 2, 512), BF16, sp1 + i * 2 * 1024)
                        for i in range(2)]
            sp2 = sp1 + 2 * 2 * 1024
            xe_g = [sb(f"xeg{i}", (128, D), BF16, sp2 + i * D * 2) for i in range(2)]
            sp3 = sp2 + 2 * D * 2
            xe_f = sb("xef", (128, D), F32, sp3)
            sp4 = sp3 + D * 4
            gbs_sb = sb("gbs", (128, CAP), F32, sp4)
            sp5 = sp4 + CAP * 4
            grow = sb("grow", (1, CAP), F32, sp5)
            sp6 = sp5 + ((CAP * 4 + 31) & ~31)
            growb = sb("growb", (1, CAP), BF16, sp6)
            sp7 = sp6 + ((CAP * 2 + 31) & ~31)
            gsc = sb("gsc", (128, 128), F32, sp7)
            sp8 = sp7 + 128 * 4
            b2row = [sb(f"b2r{i}", (1, D), BF16, sp8 + i * D * 2) for i in range(2)]
            sp9 = sp8 + 2 * D * 2
            y_fm = [sb(f"yfm{i}", (128, CAP), BF16, sp9 + i * CAP * 2)
                    for i in range(4)]
            spA = sp9 + 4 * CAP * 2
            ytm = [sb(f"ytm{i}", (128, 512), BF16, spA + i * 1024) for i in range(2)]
            spB = spA + 2 * 1024
            cnv = sb("cnv", (128, 128), F32, spB)
            spC = spB + 128 * 4
            zscan = sb("zscan", (8, CB), F32, spC)
            spD = spC + CB * 4
            carry = sb("carry", (8, 1), F32, spD)
            slotc = [sb(f"slc{i}", (128, 1), F32, spD + 32 + i * 32) for i in range(2)]
            slotu = [sb(f"slu{i}", (128, 1), U32, spD + 96 + i * 32) for i in range(2)]
            tokid = sb("tokid", (128, 1), F32, spD + 160)
            gcol_u = sb("gcol_u", (128, 1), U32, spD + 192)
            spE = spD + 224
            oobfill = sb("oobf", (128, NCH), F32, spE)
            spF = spE + ((NCH * 4 + 31) & ~31)
            idx_ld = sb("idx_ld", (NCH, 128), F32, spF)
            spG = spF + 512
            idxu = sb("idxu", (128, NCH), U32, spG)
            idxAu = sb("idxAu", (128, NCH), U32, spG + ((NCH * 4 + 31) & ~31))
            idxBu = sb("idxBu", (128, NCH), U32, spG + 2 * ((NCH * 4 + 31) & ~31))
            idxA2 = sb("idxA2", (128, NCH), U32, spG + 3 * ((NCH * 4 + 31) & ~31))
            idxB2 = sb("idxB2", (128, NCH), U32, spG + 4 * ((NCH * 4 + 31) & ~31))
            spH = spG + 5 * ((NCH * 4 + 31) & ~31)
            xe_need = FCH * CAP * 2
            if xe_need <= TPC * 6:
                xe_fm = [sb(f"xefm{i}", (128, CAP), BF16, gbe + i * CAP * 2)
                         for i in range(FCH)]
            else:
                xe_fm = [sb(f"xefm{i}", (128, CAP), BF16, spH + i * CAP * 2)
                         for i in range(FCH)]
                spH += xe_need
            tm_st = [sb(f"tmst{i}", (128, D), BF16, o4 + 2 * 2080 + i * D * 2)
                     for i in range(2)]
            assert spH <= OFF_SM, (spH, OFF_SM)
            sq_lo = o4 + 2 * 2080 + 2 * D * 2
            if 0 in layers:
                sq_lo = max(sq_lo, f1 + 6 * FCH * 256 + 2 * CB * 2
                            + 3 * cfg.NHB_F * 256)
            sq_lo = AL(sq_lo)
            sq_cap = (((OFF_SM - sq_lo) // 2) // 4) & ~7
            assert sq_cap >= 640, (sq_lo, OFF_SM)
            stga_bufs = [sb(f"stga{i}", (128, sq_cap), F32,
                            sq_lo + i * sq_cap * 4) for i in range(2)]
            stga_bufs_i = [sb(f"stgai{i}", (128, sq_cap), I32,
                             sq_lo + i * sq_cap * 4) for i in range(2)]
            sm_cap = ((FCH * TPC * 2 // 2) // 4) & ~7
            stgm_bufs = [sb(f"stgm{i}", (128, sm_cap), F32,
                            OFF_B + i * sm_cap * 4) for i in range(2)]
        lg_sb = sb("lg", (128, 8), F32, r0)
        r_m8 = sb("r_m8", (128, 8), F32, r0 + 32)
        r_i8 = sb("r_i8", (128, 8), U32, r0 + 64)
        r_if = sb("r_if", (128, 2), F32, r0 + 96)
        r_oh1 = sb("r_oh1", (128, 8), F32, r0 + 128)
        r_oh2 = sb("r_oh2", (128, 8), F32, r0 + 160)
        r_neg = sb("r_neg", (128, 1), F32, r0 + 192)
        r_p = sb("r_p", (128, 8), F32, r0 + 224)
        r_s = sb("r_s", (128, 1), F32, r0 + 256)
        r_t = sb("r_t", (128, 1), F32, r0 + 288)
        r_cf = sb("r_cf", (128, 8), F32, r0 + 320)

    vwstride = (KTPR * 520 + 31) & ~31
    vw = [sb(f"vw{i}", (128, KTPR, 260), BF16, OFF_D + i * vwstride)
          for i in range(4)]
    a0 = OFF_D + 4 * vwstride
    z_sb = [sb(f"z{i}", (128, TPC), BF16, a0 + i * TPC * 2) for i in range(2)]
    a1 = a0 + 2 * TPC * 2
    p_buf = [sb(f"pbf{i}", (128, 2 * QW), BF16, a1 + i * 2 * QW * 2)
             for i in range(2)]
    a3 = a1 + 5 * QW * 2
    oslab = [sb(f"osl{i}", (128, D), BF16, a3 + i * D * 2) for i in range(4)]
    a4 = a3 + 4 * D * 2
    recb = sb("recb", (64, 512), F32, a4)
    assert a4 + 2048 <= OFF_SM, (a4, OFF_SM)

    pq = [nc.alloc_psum_tensor(f"pq{i}", [128, 1024], F32) for i in range(2)]
    pb_s = [nc.alloc_psum_tensor(f"bank{i}", [128, 512], F32) for i in range(4, 8)]
    pbk = [pq[0][:, 0:512], pq[0][:, 512:1024],
           pq[1][:, 0:512], pq[1][:, 512:1024]] + [h[:] for h in pb_s]

    P = Prog()

    BNDREG = {}

    def bndreg(e, v):
        if v not in BNDREG:
            r = e.alloc_register(f"bnd{v}")
            e.reg_mov(r, v)
            BNDREG[v] = r
        return BNDREG[v]

    def dma(q, dst, src, waits=(), sem="MISC"):
        return P.emit(q, lambda e, d=dst, s=src: e.dma_start(out=d, in_=s),
                      waits=waits, inc=sem, amt=16)

    def mm(out, lhsT, rhs, start, stop, waits=(), inc=None):
        return P.emit("tensor",
                      lambda e, o=out, l=lhsT, r=rhs, a=start, b=stop:
                      e.matmul(o, l, r, start=a, stop=b),
                      waits=waits, inc=inc)

    def bk_acq(i, waits=()):
        t = P.bank_r.get(i)
        return (list(waits) + [t]) if t else list(waits)

    def bk_read(i, tok):
        P.bank_r[i] = tok

    def vec(fn, waits=(), inc="DV"):
        return P.emit("vector", fn, waits=waits, inc=inc)

    def act(fn, waits=(), inc="AC"):
        return P.emit("scalar", fn, waits=list(waits) + [cst], inc=inc)

    dma("sync", fconst_sb[:], din["fconst"], sem="CST")
    for h, name in [(ones_col, "ones_col"), (ones_r1, "ones_r1"),
                    (iota_e, "iota_e"), (ident, "ident"),
                    (ones_r1_bf, "ones_r1_bf"), (ones_row_bf, "ones_row_bf"),
                    (masks_sb, "masks"), (sel8_sb, "sel8"),
                    (iota_col, "iota_col"), (ecap1_sb, "ecap1"),
                    (ecap2_sb, "ecap2")]:
        dma("sync", h[:], din[name], sem="CST")
    if 1 in layers:
        for h, name in [(bq_sb, "l1_bq"), (bk_sb, "l1_bk"), (bo_sb, "l1_bo")]:
            dma("sync", h[:], din[name], sem="CST")
        for h, name in [(bv_sb, "l1_bv"), (b2_sb, "l1_b2")]:
            dma("gpsimd", h[:], din[name], sem="CSG")
        dma("sync", b1c_sb[:], din["l1_b1c"], sem="CST")
        dma("gpsimd", rw_sb.ap().rearrange("p (fc e) -> p fc e", e=8),
            din["l1_rw"].rearrange("(fc p) e -> p fc e", p=128), sem="CSG")
    cst = ("CST", P.cnt["CST"])
    cstg = ("CSG", P.cnt["CSG"]) if P.cnt["CSG"] else None

    for fc in range(FCH):
        dma("sync", xacc[fc][:], din["x_fm"][fc * 128:(fc + 1) * 128, :], sem="XL")
    xacc_tok = [("XL", P.cnt["XL"])] * FCH

    zt = P.emit("vector", lambda e: e.memset(mrow[0:1, :], 0.0), inc="DV")
    norm_zero = dma("gpsimd", k_contrib[D:D + 1, :], mrow[0:1, :], waits=[zt],
                    sem="KN")

    st = {
        "xacc": xacc_tok,
        "B_free": [None] * 8,
        "q2_free": [None] * 8,
        "h_free": None,
        "arena_free": None,
        "kgath_free": None,
        "vgath_free": None,
        "qpark_free": None,
        "contrib_free": None,
        "k2tmp_free": [None, None],
        "vstage_free": [None, None],
        "xsq_gate": None,
        "wstream": Stream(P, "gpsimd", "sW", wslab),
        "vstream": Stream(P, "gpsimd", "sV", vslab),
        "ostream": Stream(P, "gpsimd", "sO", oslab),
    }
    stagerA = stagerM = None
    if 1 in layers and cfg.sparse_moe:
        stagerA = Stager(P, "qg", stga_bufs, stga_bufs_i, sq_cap)
        stagerM = Stager(P, "mg", stgm_bufs, None, sm_cap)
        st["wstream"] = Stream(P, "gpsimd", "sW", wslab, stager=stagerA,
                               gsz=128, n_g=FCH, sqs=("gpsimd", "gpsimd"))
        st["vstream"] = Stream(P, "gpsimd", "sV", vslab, stager=stagerA,
                               gsz=512, n_g=FCH, sqs=("gpsimd", "gpsimd"))
    st["stagerQ"], st["stagerM"] = stagerA, stagerM
    if 0 in layers:
        st["gs"] = Stream(P, "gpsimd", "sG", gslab, stager=stagerA,
                          gsz=128, n_g=FCH, sqs=("sync", "sync"), i32=True)
        st["us"] = Stream(P, "gpsimd", "sU", uslab, stager=stagerA,
                          gsz=128, n_g=FCH, sqs=("sync", "sync"), i32=True)
        st["ds"] = Stream(P, "gpsimd", "sD", dslab, stager=stagerA,
                          gsz=128, n_g=cfg.NHB_F, sqs=("sync", "sync"),
                          i32=True)
    if 1 in layers:
        st["m1s"] = Stream(P, "gpsimd", "s1", m1slab)
        st["m2s"] = Stream(P, "gpsimd", "s2", m2slab)
        if cfg.sparse_moe:
            st["m1s_s"] = Stream(P, "gpsimd", "t1", m1slab_s,
                                 stager=stagerM, gsz=128, n_g=9)
            st["m2s_s"] = Stream(P, "gpsimd", "t2", m2slab_s,
                                 stager=stagerM, gsz=512, n_g=2)

    def emit_ln(lid):
        arena_w = [st["arena_free"]] if st["arena_free"] else []
        sum_tok = [None] * NCB
        sq_tok_f = [None] * NCB
        last_sq = st["xsq_gate"]
        for fc in range(FCH):
            for tb in range(NCB):
                w = [st["xacc"][fc], cst]
                if fc == 0:
                    w = bk_acq(tb, w)
                t = mm(pbk[tb][0:1, 0:CB], ones_col[:],
                       xacc[fc][:, tb * CB:(tb + 1) * CB],
                       start=(fc == 0), stop=(fc == FCH - 1),
                       waits=w, inc="TE" if fc == FCH - 1 else None)
                if fc == FCH - 1:
                    sum_tok[tb] = t
            w = [st["xacc"][fc], last_sq] + (arena_w if fc == 0 else [])
            s = act(lambda e, fc=fc: e.activation(xsq[:], xacc[fc][:], AF.Square),
                    waits=w)
            for tb in range(NCB):
                w = [s]
                if fc == 0:
                    w = bk_acq(4 + tb, w)
                t = mm(pbk[4 + tb][0:1, 0:CB], ones_col[:],
                       xsq[:, tb * CB:(tb + 1) * CB],
                       start=(fc == 0), stop=(fc == FCH - 1),
                       waits=w, inc="TE")
                if fc == FCH - 1:
                    sq_tok_f[tb] = t
                last_sq = t
        m_toks = []
        for tb in range(NCB):
            t = vec(lambda e, tb=tb: e.tensor_scalar_mul(
                mrow[0:1, tb * CB:(tb + 1) * CB], pbk[tb][0:1, 0:CB], 1.0 / D),
                waits=[sum_tok[tb], norm_zero] + arena_w)
            bk_read(tb, t)
            m_toks.append(t)
        msq = vec(lambda e: e.tensor_mul(srow[0:1, :], mrow[0:1, :], mrow[0:1, :]),
                  waits=m_toks)
        v_toks = []
        for tb in range(NCB):
            t = vec(lambda e, tb=tb: e.scalar_tensor_tensor(
                vrow[0:1, tb * CB:(tb + 1) * CB], pbk[4 + tb][0:1, 0:CB],
                1.0 / D, srow[0:1, tb * CB:(tb + 1) * CB], ALU.mult, ALU.subtract),
                waits=[sq_tok_f[tb], msq])
            bk_read(4 + tb, t)
            v_toks.append(t)
        t_ln = act(lambda e: e.activation(srow[0:1, :], vrow[0:1, :], AF.Ln,
                                          bias=1e-5), waits=v_toks + [msq])
        t_rs = act(lambda e: e.activation(srow[0:1, :], srow[0:1, :], AF.Exp,
                                          scale=-0.5), waits=[t_ln])
        bc = []
        for tb in range(NCB):
            w = bk_acq(tb, [m_toks[tb], cst])
            t1 = mm(pbk[tb][:, 0:CB], ones_r1[0:1, :],
                    mrow[0:1, tb * CB:(tb + 1) * CB], True, True, w, inc="TE")
            w = bk_acq(4 + tb, [t_rs])
            t2 = mm(pbk[4 + tb][:, 0:CB], ones_r1[0:1, :],
                    srow[0:1, tb * CB:(tb + 1) * CB], True, True, w, inc="TE")
            c1 = vec(lambda e, tb=tb: e.tensor_copy(
                mean_bc[:, tb * CB:(tb + 1) * CB], pbk[tb][:, 0:CB]), waits=[t1])
            c2 = vec(lambda e, tb=tb: e.tensor_copy(
                rstd_bc[:, tb * CB:(tb + 1) * CB], pbk[4 + tb][:, 0:CB]), waits=[t2])
            bk_read(tb, c1)
            bk_read(4 + tb, c2)
            bc += [c1, c2]
        xn_tok = []
        for fc in range(FCH):
            t1 = vec(lambda e, fc=fc: e.tensor_sub(xsq[:], xacc[fc][:], mean_bc[:]),
                     waits=bc + [st["xacc"][fc], last_sq])
            w2 = [t1] + ([st["B_free"][fc]] if st["B_free"][fc] else [])
            t2 = vec(lambda e, fc=fc: e.tensor_mul(xn[fc][:], xsq[:], rstd_bc[:]),
                     waits=w2)
            xn_tok.append(t2)
            last_sq = t2
        st["xsq_gate"] = last_sq
        return xn_tok

    def emit_qkv(lid, xn_tok):
        l0 = (lid == 0)
        pre = f"l{lid}_"
        ws = st["wstream"]
        g_ = xn_tok[-1]
        ws.gate(g_)
        st["vstream"].gate(g_)
        if st["stagerQ"] is not None:
            st["stagerQ"].gate([g_])
        st["k2tmp_free"] = [g_] * 2
        st["vstage_free"] = [g_] * 2

        def proj_qk(which):
            contrib_toks = []
            norm_done = None
            for hp in range(8):
                buf, slot = ws.load(
                    din[pre + ("wq" if which == "q" else "wk")]
                    [hp * 128:(hp + 1) * 128, :].rearrange("p (g c) -> p g c", c=128))
                ld = ws.ld[slot]
                if which == "q":
                    dest, dw = q2[hp], st["q2_free"][hp]
                else:
                    dest = k2tmp[hp % 2]
                    dw = contrib_toks[hp - 2] if hp >= 2 else st["k2tmp_free"][hp % 2]
                evac = []
                for tb in range(NCB):
                    bi = tb % 2
                    w = bk_acq(bi, [ld] + xn_tok)
                    t = None
                    for fc in range(FCH):
                        t = mm(pbk[bi][:, 0:CB], buf[:, fc, :],
                               xn[fc][:, tb * CB:(tb + 1) * CB],
                               start=(fc == 0), stop=(fc == FCH - 1),
                               waits=w if fc == 0 else (),
                               inc="TE" if fc == FCH - 1 else None)
                    ew = [t] + ([dw] if dw and tb == 0 else [])
                    if not l0:
                        b = bq_sb if which == "q" else bk_sb
                        et = vec(lambda e, d=dest, tb=tb, bi=bi, b=b, hp=hp:
                                 e.tensor_scalar_add(d[:, tb * CB:(tb + 1) * CB],
                                                     pbk[bi][:, 0:CB],
                                                     b[:, hp:hp + 1]),
                                 waits=ew + [cst])
                    else:
                        et = vec(lambda e, d=dest, tb=tb, bi=bi:
                                 e.tensor_copy(d[:, tb * CB:(tb + 1) * CB],
                                               pbk[bi][:, 0:CB]), waits=ew)
                    bk_read(bi, et)
                    evac.append(et)
                ws.release(slot, evac[-1])
                s = act(lambda e, d=dest: e.activation(xsq[:], d[:], AF.Square),
                        waits=evac + [st["xsq_gate"]])
                red = []
                for tb in range(NCB):
                    bnk = 6 if tb % 2 == 0 else 5
                    w = bk_acq(bnk, [s])
                    t = mm(pbk[bnk][0:1, 0:CB], ones_col[:],
                           xsq[:, tb * CB:(tb + 1) * CB], True, True, w, inc="TE")
                    rt = vec(lambda e, tb=tb, bnk=bnk: e.reduce_max(
                        ntmp[0:1, tb:tb + 1], pbk[bnk][0:1, 0:CB], AX.X),
                        waits=[t, norm_done])
                    bk_read(bnk, rt)
                    red.append(rt)
                dstrow = qn_sb if which == "q" else kn_sb
                norm_done = vec(lambda e, d=dstrow, hp=hp: e.reduce_max(
                    d[0:1, hp:hp + 1], ntmp[0:1, 0:NCB], AX.X), waits=red)
                st["xsq_gate"] = s
                st["xsq_gate"] = red[-1]
                if which == "q":
                    pk = dma("sync", q_park[hp * 128:(hp + 1) * 128, :], q2[hp][:],
                             waits=evac + ([st["qpark_free"]] if st["qpark_free"]
                                           else []), sem=f"pk{hp}")
                    st["q2_free"][hp] = pk
                else:
                    cw = [st["contrib_free"]] if st["contrib_free"] else []
                    kc = dma("sync", k_contrib[hp * 128:(hp + 1) * 128, :],
                             k2tmp[hp % 2][:], waits=evac + cw, sem=f"kc{hp % 2}")
                    contrib_toks.append(kc)
            return contrib_toks, norm_done

        _, qn_done = proj_qk("q")
        k_contrib_toks, kn_done = proj_qk("k")
        st["k2tmp_free"] = [k_contrib_toks[-2], k_contrib_toks[-1]]
        kc_norm = dma("gpsimd", k_contrib[D:D + 1, 0:8], kn_sb[0:1, 0:8],
                      waits=[kn_done], sem="KN")
        k_contrib_toks.append(kc_norm)

        vs = st["vstream"]
        ones_done = [None, None]
        for i in range(2):
            prev = st["vstage_free"][i]
            for q4 in range(4):
                prev = vec(lambda e, i=i, q4=q4: e.memset(
                    vstage[i].ap()[:, q4, :].rearrange("p (h c) -> p h c", c=65)
                    [:, :, 64:65], 1.0), waits=[prev])
            ones_done[i] = prev
        v_contrib_toks = []
        ev_last = None
        for vh in range(2):
            buf, slot = vs.load(din[pre + "wv"][vh * 128:(vh + 1) * 128, :]
                                .rearrange("p (g c) -> p g c", c=512))
            ld = vs.ld[slot]
            for tb in range(NTB):
                bi = 2 + tb % 2
                w = bk_acq(bi, [ld] + xn_tok)
                t = None
                for fc in range(FCH):
                    last = (fc == FCH - 1) and l0
                    t = mm(pbk[bi][:, 0:512], xn[fc][:, tb * 128:(tb + 1) * 128],
                           buf[:, fc, :], start=(fc == 0), stop=last,
                           waits=w if fc == 0 else (), inc="TE" if last else None)
                if not l0:
                    t = mm(pbk[bi][:, 0:512], ones_r1_bf[0:1, :],
                           bv_sb[0:1, vh * 512:(vh + 1) * 512], False, True,
                           waits=[cstg], inc="TE")
                si = tb % 2
                sw = [t, ones_done[si]]
                if len(v_contrib_toks) >= 2:
                    sw.append(v_contrib_toks[-2])
                ev = vec(lambda e, si=si, bi=bi, vh=vh: e.tensor_copy(
                    vstage[si].ap()[:, 2 * vh:2 * vh + 2, :]
                    .rearrange("p q (h c) -> p (q h) c", c=65)[:, :, 0:64],
                    pbk[bi][:, 0:512]), waits=sw)
                bk_read(bi, ev)
                cw = [st["contrib_free"]] if st["contrib_free"] else []
                vc = dma("sync",
                         v_contrib.rearrange("(q t p) c -> p q t c", q=4, p=128)
                         [:, 2 * vh:2 * vh + 2, tb, :],
                         vstage[si][:, 2 * vh:2 * vh + 2, :], waits=[ev] + cw,
                         sem=f"vc{si}")
                v_contrib_toks.append(vc)
                ev_last = ev
            vs.release(slot, ev_last)
        st["vstage_free"] = [v_contrib_toks[-2], v_contrib_toks[-1]]

        coll_w = k_contrib_toks + [v_contrib_toks[-1], v_contrib_toks[-2]]
        st["contrib_done"] = tmax(*(k_contrib_toks + v_contrib_toks))
        if st["kgath_free"]:
            coll_w.append(st["kgath_free"])
        ck = P.emit("gpsimd", lambda e: e.collective_compute(
            "AllGather", ALU.bypass, ins=[k_contrib.opt()], outs=[k_gath.opt()],
            replica_groups=[[0, 1, 2, 3], [4, 5, 6, 7]]), waits=coll_w, inc="CCK")
        cv_w = [v_contrib_toks[-1]]
        if st["vgath_free"]:
            cv_w.append(st["vgath_free"])
        cv = P.emit("gpsimd", lambda e: e.collective_compute(
            "AllGather", ALU.bypass, ins=[v_contrib.opt()], outs=[v_gath.opt()],
            replica_groups=[[0, 1, 2, 3], [4, 5, 6, 7]]), waits=cv_w, inc="CCV")
        st["contrib_free"] = cv
        return ck, cv

    def emit_ffn_l0(xn_tok, qkv_gate):
        gs, us, ds = st["gs"], st["us"], st["ds"]
        for _s in (gs, us, ds):
            for i in range(len(_s.free)):
                _s.free[i] = _s.free[i] + qkv_gate if _s.free[i] else list(qkv_gate)
        NHB = cfg.NHB_F
        h_tok = [None] * NHB
        g_free = [None, None]
        for tp in range(NTP):
            for hb in range(NHB):
                b1, s1 = gs.load(din["l0_wg"][hb * 128:(hb + 1) * 128, :]
                                 .rearrange("p (g c) -> p g c", c=128))
                b2, s2 = us.load(din["l0_wu"][hb * 128:(hb + 1) * 128, :]
                                 .rearrange("p (g c) -> p g c", c=128))
                for tbl in range(TP // CB):
                    col0 = tp * TP + tbl * CB
                    bg, bu = tbl % 2, 2 + tbl % 2
                    w = bk_acq(bg, [gs.ld[s1]] + xn_tok)
                    tg = None
                    for fc in range(FCH):
                        tg = mm(pbk[bg][:, 0:CB], b1[:, fc, :],
                                xn[fc][:, col0:col0 + CB],
                                start=(fc == 0), stop=(fc == FCH - 1),
                                waits=w if fc == 0 else (),
                                inc="TE" if fc == FCH - 1 else None)
                    w = bk_acq(bu, [us.ld[s2]])
                    tu = None
                    for fc in range(FCH):
                        tu = mm(pbk[bu][:, 0:CB], b2[:, fc, :],
                                xn[fc][:, col0:col0 + CB],
                                start=(fc == 0), stop=(fc == FCH - 1),
                                waits=w if fc == 0 else (),
                                inc="TE" if fc == FCH - 1 else None)
                    gi = tbl % 2
                    ga = act(lambda e, gi=gi, bg=bg: e.activation(
                        g_sb[gi][:, 0:CB], pbk[bg][:, 0:CB], AF.Silu,
                        scale=1.0 / 127.0), waits=[tg, g_free[gi]])
                    bk_read(bg, ga)
                    hw = [tu, ga]
                    if tp == 0 and tbl == 0:
                        hw += qkv_gate
                        if hb < HCAP:
                            hw.append(st["q2_free"][hb * TP // TPC])
                        elif st["arena_free"]:
                            hw.append(st["arena_free"])
                    ht = vec(lambda e, hb=hb, tbl=tbl, bu=bu, gi=gi:
                             e.scalar_tensor_tensor(
                                 h_l0[hb][:, tbl * CB:(tbl + 1) * CB],
                                 pbk[bu][:, 0:CB], 1.0 / 127.0, g_sb[gi][:, 0:CB],
                                 ALU.mult, ALU.mult), waits=hw)
                    bk_read(bu, ht)
                    g_free[gi] = ht
                    h_tok[hb] = ht
                gs.release(s1, h_tok[hb])
                us.release(s2, h_tok[hb])
            for ob in range(FCH):
                b3, s3 = ds.load(din["l0_wd"][ob * 128:(ob + 1) * 128, :]
                                 .rearrange("p (g c) -> p g c", c=128))
                last_at = None
                for tbl in range(TP // CB):
                    col0 = tp * TP + tbl * CB
                    bd = 4 + tbl % 2
                    w = bk_acq(bd, [ds.ld[s3]] + h_tok)
                    t = None
                    for hb in range(NHB):
                        t = mm(pbk[bd][:, 0:CB], b3[:, hb, :],
                               h_l0[hb][:, tbl * CB:(tbl + 1) * CB],
                               start=(hb == 0), stop=(hb == NHB - 1),
                               waits=w if hb == 0 else (),
                               inc="TE" if hb == NHB - 1 else None)
                    at = vec(lambda e, ob=ob, col0=col0, bd=bd:
                             e.scalar_tensor_tensor(
                                 xacc[ob][:, col0:col0 + CB], pbk[bd][:, 0:CB],
                                 1.0 / 127.0, xacc[ob][:, col0:col0 + CB],
                                 ALU.mult, ALU.add),
                             waits=[t, st["xacc"][ob]])
                    bk_read(bd, at)
                    st["xacc"][ob] = at
                    last_at = at
                ds.release(s3, last_at)
        return

    def emit_moe(xn_tok, qkv_gate):
        NHB = cfg.NHB_M
        for _s in (st["m1s"], st["m2s"]):
            for i in range(len(_s.free)):
                _s.free[i] = _s.free[i] + qkv_gate if _s.free[i] else list(qkv_gate)
        arena_w = [st["arena_free"]] if st["arena_free"] else []
        cT_toks = []
        rprev = None
        for tb in range(NTB):
            w = bk_acq(7, xn_tok + [cstg] + arena_w)
            t = None
            for fc in range(FCH):
                t = mm(pbk[7][:, 0:8], xn[fc][:, tb * 128:(tb + 1) * 128],
                       rw_sb[:, fc * 8:(fc + 1) * 8],
                       start=(fc == 0), stop=(fc == FCH - 1),
                       waits=w if fc == 0 else (), inc="TE" if fc == FCH - 1 else None)
            c0 = vec(lambda e: e.tensor_copy(lg_sb[:], pbk[7][:, 0:8]),
                     waits=[t, rprev])
            bk_read(7, c0)
            t1 = vec(lambda e: e.max(r_m8[:], lg_sb[:]), waits=[c0])
            t2 = vec(lambda e: e.max_index(r_i8[:], r_m8[:], lg_sb[:]), waits=[t1])
            t2b = vec(lambda e: e.tensor_copy(r_if[:], r_i8[:, 0:2]), waits=[t2])
            t3 = vec(lambda e: e.tensor_single_scalar(r_oh1[:], iota_e[:],
                                                      r_if[:, 0:1], ALU.is_equal),
                     waits=[t2b, cst])
            t7 = vec(lambda e: e.tensor_single_scalar(r_oh2[:], iota_e[:],
                                                      r_if[:, 1:2], ALU.is_equal),
                     waits=[t3])
            t8 = vec(lambda e: e.tensor_scalar_mul(r_neg[:], r_m8[:, 0:1], -1.0),
                     waits=[t1])
            t9 = act(lambda e: e.activation(r_p[:], lg_sb[:], AF.Exp,
                                            bias=r_neg[:, 0:1]), waits=[t8, t2])
            ta = vec(lambda e: e.reduce_sum(r_s[:], r_p[:], AX.X), waits=[t9])
            tb_ = vec(lambda e: e.reciprocal(r_t[:], r_s[:]), waits=[ta])
            tc = vec(lambda e: e.tensor_scalar_mul(r_p[:], r_p[:], r_t[:, 0:1]),
                     waits=[tb_])
            td = vec(lambda e: e.tensor_add(r_oh2[:], r_oh1[:], r_oh2[:]),
                     waits=[t7, tc])
            te_ = vec(lambda e: e.tensor_mul(r_p[:], r_p[:], r_oh2[:]), waits=[td])
            tf = vec(lambda e: e.reduce_sum(r_s[:], r_p[:], AX.X), waits=[te_])
            tg = vec(lambda e: e.tensor_scalar_add(r_s[:], r_s[:], 1e-9), waits=[tf])
            th = vec(lambda e: e.reciprocal(r_t[:], r_s[:]), waits=[tg])
            ti = vec(lambda e: e.tensor_scalar_mul(r_cf[:], r_p[:], r_t[:, 0:1]),
                     waits=[th])
            w = bk_acq(6, [ti, cst])
            tj = P.emit("tensor", lambda e: e.transpose(pbk[6][0:8, 0:128],
                                                        r_cf[:], ident[:]),
                        waits=w, inc="TE")
            tk = vec(lambda e, tb=tb: e.tensor_copy(
                cT_sb[0:8, tb * 128:(tb + 1) * 128], pbk[6][0:8, 0:128]),
                waits=[tj] + arena_w)
            bk_read(6, tk)
            tl = vec(lambda e, tb=tb: e.tensor_copy(
                cTb_sb[0:8, tb * 128:(tb + 1) * 128],
                cT_sb[0:8, tb * 128:(tb + 1) * 128]), waits=[tk])
            cT_toks.append(tl)
            rprev = ti

        if gate_tap is not None:
            dma("sync", gate_tap, cT_sb[0:8, :], waits=cT_toks, sem="OUT")
        m1s, m2s = st["m1s"], st["m2s"]
        h_tok = [None] * NHB
        rl_free = [arena_w[0] if arena_w else None]
        gb_tok = [None]
        for e_i in range(E):
            for tp in range(NTP):
                gi = 0
                gt = None
                for yw in range(TP // YW):
                    w = bk_acq(6, cT_toks + ([gb_tok[gi]] if gb_tok[gi] else [])
                               + (arena_w if e_i == 0 else []))
                    t = mm(pbk[6][:, 0:YW],
                           sel8_sb[0:8, e_i * 128:(e_i + 1) * 128],
                           cT_sb[0:8, tp * TP + yw * YW:tp * TP + (yw + 1) * YW],
                           True, True, waits=w, inc="TE")
                    gt = vec(lambda e, gi=gi, yw=yw: e.tensor_copy(
                        gb_sb[gi][:, yw * YW:(yw + 1) * YW], pbk[6][:, 0:YW]),
                        waits=[t])
                    bk_read(6, gt)
                gb_tok[gi] = gt
                for hb in range(NHB):
                    b1, s1 = m1s.load(
                        din["l1_w1"][(e_i * NHB + hb) * 128:
                                     (e_i * NHB + hb + 1) * 128, :]
                        .rearrange("p (g c) -> p g c", c=128))
                    for tbl in range(TP // CB):
                        col0 = tp * TP + tbl * CB
                        bi = tbl % 2
                        w = bk_acq(bi, [m1s.ld[s1]] + xn_tok)
                        t = None
                        for fc in range(FCH):
                            t = mm(pbk[bi][:, 0:CB], b1[:, fc, :],
                                   xn[fc][:, col0:col0 + CB],
                                   start=(fc == 0), stop=(fc == FCH - 1),
                                   waits=w if fc == 0 else (),
                                   inc="TE" if fc == FCH - 1 else None)
                        rt = act(lambda e, bi=bi, e_i=e_i, hb=hb: e.activation(
                            rl_buf[:, 0:CB], pbk[bi][:, 0:CB], AF.Relu,
                            bias=b1c_sb[:, e_i * NHB + hb:e_i * NHB + hb + 1]),
                            waits=[t, rl_free[0]])
                        bk_read(bi, rt)
                        hw = [rt, gb_tok[gi]]
                        if e_i == 0 and tp == 0 and tbl == 0:
                            hw += qkv_gate
                            if hb < HCAP:
                                hw.append(st["q2_free"][hb * TP // TPC])
                            elif st["arena_free"]:
                                hw.append(st["arena_free"])
                        ht = vec(lambda e, hb=hb, tbl=tbl, gi=gi:
                                 e.tensor_mul(
                                     h_moe[hb][:, tbl * CB:(tbl + 1) * CB],
                                     rl_buf[:, 0:CB],
                                     gb_sb[gi][:, tbl * CB:(tbl + 1) * CB]),
                                 waits=hw)
                        rl_free[0] = ht
                        h_tok[hb] = ht
                    m1s.release(s1, h_tok[hb])
                if h_tap is not None and e_i == 3 and tp == 0:
                    for hb in range(NHB):
                        dma("gpsimd", h_tap[hb * 128:(hb + 1) * 128, :],
                            h_moe[hb][:], waits=[h_tok[hb]], sem="HT")
                    htt = ("HT", P.cnt["HT"])
                    h_tok = [htt] * NHB
                ytiles = [(cb, th) for cb in range(4) for th in range(NYW)]
                for ch in range(2):
                    if e_i == 0:
                        for (cb, th) in ytiles:
                            bi = cb * NYW + th
                            w = bk_acq(bi, cT_toks + [cstg])
                            mm(pbk[bi][:, 0:YW],
                               b2_sb[0:8, ch * 512 + cb * 128:ch * 512 + (cb + 1) * 128],
                               cTb_sb[0:8, tp * TP + th * YW:tp * TP + (th + 1) * YW],
                               True, False, waits=w)
                    last = {}
                    for hbg in range(NHB // 2):
                        b2w, s2 = m2s.load(
                            din["l1_w2"]
                            [((e_i * 2 + ch) * (NHB // 2) + hbg) * 128:
                             ((e_i * 2 + ch) * (NHB // 2) + hbg + 1) * 128, :]
                            .rearrange("p (g c) -> p g c", c=512))
                        slab_last = None
                        for g4 in range(2):
                            hb = hbg * 2 + g4
                            for (cb, th) in ytiles:
                                bi = cb * NYW + th
                                isend = (hb == NHB - 1)
                                isfirst = (e_i > 0 and hb == 0)
                                w = [m2s.ld[s2], h_tok[hb]] if (cb, th) == ytiles[0] else ()
                                if isfirst:
                                    w = bk_acq(bi, list(w) if w else [])
                                t = mm(pbk[bi][:, 0:YW],
                                       b2w[:, g4, cb * 128:(cb + 1) * 128],
                                       h_moe[hb][:, th * YW:(th + 1) * YW],
                                       isfirst, isend, waits=w,
                                       inc="TE" if isend or ((g4 == 1) and
                                                             (cb, th) == ytiles[-1])
                                       else None)
                                if isend:
                                    last[(cb, th)] = t
                                if (g4 == 1) and (cb, th) == ytiles[-1]:
                                    slab_last = t
                        m2s.release(s2, slab_last)
                    for (cb, th) in ytiles:
                        bi = cb * NYW + th
                        ob = ch * 4 + cb
                        col0 = tp * TP + th * YW
                        at = vec(lambda e, ob=ob, col0=col0, bi=bi:
                                 e.tensor_add(xacc[ob][:, col0:col0 + YW],
                                              pbk[bi][:, 0:YW],
                                              xacc[ob][:, col0:col0 + YW]),
                                 waits=[last[(cb, th)], st["xacc"][ob]])
                        bk_read(bi, at)
                        st["xacc"][ob] = at
        return

    def emit_moe_sparse(xn_tok, qkv_gate, xn_tm_toks):
        NHB = cfg.NHB_M
        CAPW = cfg.CAPW
        arena_w = [st["arena_free"]] if st["arena_free"] else []
        IOA = IndirectOffsetOnAxis

        cT_toks = []
        mask_toks = []
        rprev = None
        slot_gate = [st["q2_free"][i] for i in range(8) if st["q2_free"][i]]
        for tb in range(NTB):
            w = bk_acq(7, xn_tok + [cstg] + arena_w)
            t = None
            for fc in range(FCH):
                t = mm(pbk[7][:, 0:8], xn[fc][:, tb * 128:(tb + 1) * 128],
                       rw_sb[:, fc * 8:(fc + 1) * 8],
                       start=(fc == 0), stop=(fc == FCH - 1),
                       waits=w if fc == 0 else (), inc="TE" if fc == FCH - 1 else None)
            c0 = vec(lambda e: e.tensor_copy(lg_sb[:], pbk[7][:, 0:8]),
                     waits=[t, rprev])
            bk_read(7, c0)
            t1 = vec(lambda e: e.max(r_m8[:], lg_sb[:]), waits=[c0])
            t2 = vec(lambda e: e.max_index(r_i8[:], r_m8[:], lg_sb[:]), waits=[t1])
            t2b = vec(lambda e: e.tensor_copy(r_if[:], r_i8[:, 0:2]), waits=[t2])
            t3 = vec(lambda e: e.tensor_single_scalar(r_oh1[:], iota_e[:],
                                                      r_if[:, 0:1], ALU.is_equal),
                     waits=[t2b, cst])
            t7 = vec(lambda e: e.tensor_single_scalar(r_oh2[:], iota_e[:],
                                                      r_if[:, 1:2], ALU.is_equal),
                     waits=[t3])
            t8 = vec(lambda e: e.tensor_scalar_mul(r_neg[:], r_m8[:, 0:1], -1.0),
                     waits=[t1])
            t9 = act(lambda e: e.activation(r_p[:], lg_sb[:], AF.Exp,
                                            bias=r_neg[:, 0:1]), waits=[t8, t2])
            ta = vec(lambda e: e.reduce_sum(r_s[:], r_p[:], AX.X), waits=[t9])
            tb_ = vec(lambda e: e.reciprocal(r_t[:], r_s[:]), waits=[ta])
            tc = vec(lambda e: e.tensor_scalar_mul(r_p[:], r_p[:], r_t[:, 0:1]),
                     waits=[tb_])
            w = bk_acq(6, [t7, cst] + (slot_gate if tb == 0 else []))
            tt1 = P.emit("tensor", lambda e: e.transpose(pbk[6][0:8, 0:128],
                                                         r_oh1[:], ident[:]),
                         waits=w, inc="TE")
            mc1 = vec(lambda e, tb=tb: e.tensor_copy(
                mA_sb[0:8, tb * 128:(tb + 1) * 128], pbk[6][0:8, 0:128]),
                waits=[tt1])
            bk_read(6, mc1)
            w = bk_acq(6, [t7])
            tt2 = P.emit("tensor", lambda e: e.transpose(pbk[6][0:8, 0:128],
                                                         r_oh2[:], ident[:]),
                         waits=w, inc="TE")
            mc2 = vec(lambda e, tb=tb: e.tensor_copy(
                mB_sb[0:8, tb * 128:(tb + 1) * 128], pbk[6][0:8, 0:128]),
                waits=[tt2])
            bk_read(6, mc2)
            mask_toks += [mc1, mc2]
            td = vec(lambda e: e.tensor_add(r_oh2[:], r_oh1[:], r_oh2[:]),
                     waits=[t7, tc, tt1, tt2])
            te_ = vec(lambda e: e.tensor_mul(r_p[:], r_p[:], r_oh2[:]), waits=[td])
            tf = vec(lambda e: e.reduce_sum(r_s[:], r_p[:], AX.X), waits=[te_])
            tg = vec(lambda e: e.tensor_scalar_add(r_s[:], r_s[:], 1e-9), waits=[tf])
            th = vec(lambda e: e.reciprocal(r_t[:], r_s[:]), waits=[tg])
            ti = vec(lambda e: e.tensor_scalar_mul(r_cf[:], r_p[:], r_t[:, 0:1]),
                     waits=[th])
            w = bk_acq(6, [ti, cst])
            tj = P.emit("tensor", lambda e: e.transpose(pbk[6][0:8, 0:128],
                                                        r_cf[:], ident[:]),
                        waits=w, inc="TE")
            tk = vec(lambda e, tb=tb: e.tensor_copy(
                cT_sb[0:8, tb * 128:(tb + 1) * 128], pbk[6][0:8, 0:128]),
                waits=[tj] + arena_w)
            bk_read(6, tk)
            cT_toks.append(tk)
            rprev = ti
        ct_w = dma("sync", cT_dram.rearrange("(a b) c -> a (b c)", a=E),
                   cT_sb[0:8, :], waits=cT_toks, sem="ctd")
        st["stagerM"].gate([ct_w] + list(xn_tm_toks))

        z0 = vec(lambda e: e.memset(zscan[:], 0.0),
                 waits=arena_w + [cstg] + xn_tm_toks)
        z1 = vec(lambda e: e.memset(carry[:], 0.0), waits=[z0])
        z2 = vec(lambda e: e.memset(gsc[:], 0.0), waits=arena_w + xn_tm_toks)
        scan_toks = {}
        carry_tok = z1
        for nm, msk, dst in (("A", mA_sb, pA_sb), ("B", mB_sb, pB_sb)):
            prev = carry_tok
            for c in range(TPC // CB):
                c0_, c1_ = c * CB, (c + 1) * CB
                t = vec(lambda e, msk=msk, dst=dst, c0_=c0_, c1_=c1_:
                        e.tensor_tensor_scan(dst[0:8, c0_:c1_], msk[0:8, c0_:c1_],
                                             zscan[0:8, 0:c1_ - c0_],
                                             carry[0:8, 0:1],
                                             ALU.add, ALU.add),
                        waits=[prev, z0] + mask_toks + slot_gate)
                prev = vec(lambda e, dst=dst, c1_=c1_: e.tensor_copy(
                    carry[0:8, 0:1], dst[0:8, c1_ - 1:c1_]), waits=[t])
            rz = vec(lambda e: e.memset(carry[:], 0.0), waits=[prev])
            carry_tok = rz
            scan_toks[nm] = prev
        import os as _os
        if _os.environ.get("DBG_SCANTAP"):
            dbg_scan = nc.dram_tensor("dbg_scan", [24, TPC], F32,
                                      kind="ExternalOutput").ap()
            dma("sync", dbg_scan[0:8, :], pA_sb[0:8, :],
                waits=[scan_toks["A"]], sem="OUT")
            dma("sync", dbg_scan[8:16, :], mA_sb[0:8, :],
                waits=[scan_toks["A"]], sem="OUT")
            dma("sync", dbg_scan[16:24, :], pB_sb[0:8, :],
                waits=[scan_toks["B"]], sem="OUT")
            _sc_tap = ("OUT", P.cnt["OUT"])
        if _os.environ.get("DBG_NOSLOT"):
            sA4 = scan_toks["A"]
            sB5 = scan_toks["B"]
        else:
            _dummy = None
        if not _os.environ.get("DBG_NOSLOT"):
            _tapw = ([("OUT", P.cnt["OUT"])]
                     if _os.environ.get("DBG_SCANTAP") else [])
            sA1 = vec(lambda e: e.tensor_scalar_add(pA_sb[0:8, :], pA_sb[0:8, :],
                                                ecap1_sb[0:8, 0:1]),
                  waits=[scan_toks["A"], cst] + _tapw)
            import os as _o4
            if _o4.environ.get("DBG_S1TAP"):
                dbg_s1 = nc.dram_tensor("dbg_s1", [8, TPC], F32,
                                        kind="ExternalOutput").ap()
                _t1 = dma("sync", dbg_s1[0:8, :], pA_sb[0:8, :], waits=[sA1],
                          sem="OUT")
                sA1 = _t1
            sA2 = vec(lambda e: e.tensor_mul(pA_sb[0:8, :], pA_sb[0:8, :], mA_sb[0:8, :]),
                      waits=[sA1])
            sA3 = vec(lambda e: e.scalar_tensor_tensor(pA_sb[0:8, :], mA_sb[0:8, :],
                                                       -65536.0, pA_sb[0:8, :],
                                                       ALU.mult, ALU.add), waits=[sA2])
            sA4 = vec(lambda e: e.tensor_scalar_add(pA_sb[0:8, :], pA_sb[0:8, :],
                                                    65536.0),
                      waits=[sA3])
            sB1 = vec(lambda e: e.tensor_scalar_mul(pB_sb[0:8, :], pB_sb[0:8, :], -1.0),
                      waits=[scan_toks["B"]] + _tapw)
            sB2 = vec(lambda e: e.tensor_scalar_add(pB_sb[0:8, :], pB_sb[0:8, :],
                                                    ecap2_sb[0:8, 0:1]), waits=[sB1, cst])
            sB3 = vec(lambda e: e.tensor_mul(pB_sb[0:8, :], pB_sb[0:8, :], mB_sb[0:8, :]),
                      waits=[sB2])
            sB4 = vec(lambda e: e.scalar_tensor_tensor(pB_sb[0:8, :], mB_sb[0:8, :],
                                                       -65536.0, pB_sb[0:8, :],
                                                       ALU.mult, ALU.add), waits=[sB3])
            sB5 = vec(lambda e: e.tensor_scalar_add(pB_sb[0:8, :], pB_sb[0:8, :],
                                                    65536.0),
                      waits=[sB4])

        import os as _o6
        if _o6.environ.get("DBG_SLFIN"):
            dbg_slf = nc.dram_tensor("dbg_slf", [16, TPC], F32,
                                     kind="ExternalOutput").ap()
            _w1 = dma("sync", dbg_slf[0:8, :], pA_sb[0:8, :], waits=[sA4],
                      sem="OUT")
            _w2 = dma("sync", dbg_slf[8:16, :], pB_sb[0:8, :], waits=[sB5],
                      sem="OUT")
            sA4 = _w1
            sB5 = _w2
        ofill = vec(lambda e: e.memset(oobfill[:], 65536.0),
                    waits=arena_w + xn_tm_toks)
        init_toks = {}
        for nm, arr in (("I", idx_dram), ("A", idxA_dram), ("B", idxB_dram)):
            init_toks[nm] = dma("gpsimd",
                                arr.rearrange("(p c) a -> p (c a)", p=128),
                                oobfill[:], waits=[ofill], sem=f"ix{nm}")
        tok_prev = None
        sc_done = {"I": init_toks["I"], "A": init_toks["A"], "B": init_toks["B"]}
        for tb in range(NTB):
            tkt = vec(lambda e, tb=tb: e.tensor_single_scalar(
                tokid[:], iota_col[:], float(tb * 128), ALU.add),
                waits=[cst] + (tok_prev or []) + (xn_tm_toks if tb == 0 else []))
            sl_toks = {}
            for j, (nm, slots, s_last) in enumerate(
                    (("A", pA_sb, sA4), ("B", pB_sb, sB5))):
                w = bk_acq(6, [s_last])
                tt = P.emit("tensor", lambda e, slots=slots, tb=tb: e.transpose(
                    pbk[6][:, 0:8], slots[0:8, tb * 128:(tb + 1) * 128],
                    ident[0:8, 0:8]), waits=w, inc="TE")
                rd = vec(lambda e, j=j: e.tensor_reduce(
                    slotc[j][:], pbk[6][:, 0:8], AX.X, ALU.min), waits=[tt])
                bk_read(6, rd)
                import os as _o5
                if _o5.environ.get("DBG_SCTAP"):
                    if "dbg_sc" not in taps:
                        taps["dbg_sc"] = nc.dram_tensor(
                            "dbg_sc", [128, 2 * NTB], F32,
                            kind="ExternalOutput").ap()
                    rd = P.emit("sync", lambda e, tb=tb, j=j: e.dma_start(
                        out=taps["dbg_sc"][:, 2 * tb + j:2 * tb + j + 1],
                        in_=slotc[j][:], allow_slow_non_contiguous=True),
                        waits=[rd], inc="OUT", amt=16)
                cu = vec(lambda e, j=j: e.tensor_copy(slotu[j][:], slotc[j][:]),
                         waits=[rd] + (tok_prev or []))
                sl_toks[nm] = cu
            this_sc = []
            for nm, j in (("A", 0), ("B", 1)):
                arr = idxA_dram if nm == "A" else idxB_dram
                d1 = P.emit("gpsimd", lambda e, j=j, arr=arr: e.indirect_dma_start(
                    arr, IOA(slotu[j][:], 0), tokid[:], None,
                    bounds_check=bndreg(e, NCH * 128 - 1), oob_is_err=False),
                    waits=[sl_toks[nm], tkt, sc_done[nm]], inc=f"ix{nm}", amt=16)
                sc_done[nm] = d1
                d2 = P.emit("gpsimd", lambda e, j=j: e.indirect_dma_start(
                    idx_dram, IOA(slotu[j][:], 0), tokid[:], None,
                    bounds_check=bndreg(e, NCH * 128 - 1), oob_is_err=False),
                    waits=[sl_toks[nm], tkt, sc_done["I"]], inc="ixI", amt=16)
                sc_done["I"] = d2
                this_sc += [d1, d2]
            tok_prev = this_sc

        import os as _os3
        if _os3.environ.get("DBG_IXTAP"):
            dbg_ix = nc.dram_tensor("dbg_ix", [3, NCH * 128], F32,
                                    kind="ExternalOutput").ap()
            for _k, (_nm, _arr) in enumerate((("I", idx_dram), ("A", idxA_dram),
                                              ("B", idxB_dram))):
                dma("sync", dbg_ix[_k:_k + 1, :],
                    _arr.rearrange("(a) c -> c (a)", a=NCH * 128)
                    if False else _arr.rearrange("a c -> c a"),
                    waits=[sc_done[_nm]], sem="OUT")
        idx_toks = {}
        ld_prev = None
        for nm, arr, dstu in (("I", idx_dram, idxu), ("A", idxA_dram, idxAu),
                              ("B", idxB_dram, idxBu)):
            l1 = dma("gpsimd", idx_ld[:],
                     arr.rearrange("(c p) a -> c (p a)", c=NCH),
                     waits=[sc_done[nm], ld_prev] + xn_tm_toks, sem="ixL")
            w = bk_acq(6, [l1])
            t1 = P.emit("tensor", lambda e: e.transpose(
                pbk[6][:, 0:NCH], idx_ld[0:NCH, :], ident[0:NCH, 0:NCH]),
                waits=w, inc="TE")
            cu = vec(lambda e, dstu=dstu: e.tensor_copy(dstu[:], pbk[6][:, 0:NCH]),
                     waits=[t1])
            bk_read(6, cu)
            idx_toks[nm] = cu
            ld_prev = cu

        m1s, m2s = st["m1s_s"], st["m2s_s"]
        for _s in (m1s, m2s):
            for i in range(len(_s.free)):
                _s.free[i] = _s.free[i] + qkv_gate if _s.free[i] else list(qkv_gate)
        h_tok = [None] * NHB
        rl_free = [None]
        gsc_free = z2
        xe_free = [qkv_gate, qkv_gate]
        xef_free = qkv_gate
        xefm_free = [cT_toks[-1]] + arena_w + [ct_w]
        gbs_free = arena_w
        grow_free = arena_w + xn_tm_toks
        yfm_free = [arena_w + xn_tm_toks] * 4
        ytm_free = [arena_w, arena_w]
        b2r_free = [tmax(*(arena_w + xn_tm_toks))[0] if (arena_w + xn_tm_toks)
                    else None] * 2
        b2r_free = [None, None]
        if xn_tm_toks:
            b2r_free = [xn_tm_toks[-1], xn_tm_toks[-1]]
        iax_free = [None, None]
        gg_prev = [None]
        ysc_tok = None
        for e_i in range(E):
            b2i = e_i % 2
            b2t = dma("gpsimd", b2row[b2i][:], din["l1_b2"][e_i:e_i + 1, :],
                      waits=[b2r_free[b2i]], sem=f"b2r{b2i}")
            g_prev = gsc_free
            grow_toks = []
            xe_toks = [None] * FCH
            ck_tok = [None] * CAPC
            for sc in range(CAPC):
                col = e_i * CAPC + sc
                gg = P.emit("gpsimd", lambda e, col=col, e_i=e_i:
                            e.indirect_dma_start(
                                gsc[:, 0:1], None, cT_dram,
                                IOA(idxu[:, col:col + 1], 0),
                                element_offset=e_i * TPC,
                                bounds_check=bndreg(e, TPC - 1),
                                oob_is_err=False),
                            waits=[idx_toks["I"], g_prev, ct_w, gg_prev[0]],
                            inc="gcl", amt=16)
                gg_prev[0] = gg
                w = bk_acq(6, [gg])
                tt = P.emit("tensor", lambda e: e.transpose(
                    pbk[6][:, 0:128], gsc[:], ident[:]), waits=w, inc="TE")
                g_prev = tt
                gw = [tt] + (grow_free if sc == 0 else [])
                gcp = vec(lambda e, sc=sc: e.tensor_copy(
                    grow[0:1, sc * 128:(sc + 1) * 128], pbk[6][0:1, 0:128]),
                    waits=gw)
                bk_read(6, gcp)
                grow_toks.append(gcp)
                xi = sc % 2
                xg = P.emit("gpsimd", lambda e, xi=xi, col=col: e.indirect_dma_start(
                    xe_g[xi][:], None, xn_tm, IOA(idxu[:, col:col + 1], 0),
                    bounds_check=bndreg(e, TPC - 1), oob_is_err=False),
                    waits=[idx_toks["I"]] + xn_tm_toks +
                          (xe_free[xi] if isinstance(xe_free[xi], list)
                           else [xe_free[xi]]),
                    inc=f"xg{xi}", amt=16)
                cf_ = vec(lambda e, xi=xi: e.tensor_copy(xe_f[:], xe_g[xi][:]),
                          waits=[xg, xef_free] if not isinstance(xef_free, list)
                          else [xg] + xef_free)
                xe_free[xi] = cf_
                last_fc = None
                for fc in range(FCH):
                    w = bk_acq(7, [cf_])
                    t1 = P.emit("tensor", lambda e, fc=fc: e.transpose(
                        pbk[7][:, 0:128], xe_f[:, fc * 128:(fc + 1) * 128],
                        ident[:]), waits=w, inc="TE")
                    xw = [t1] + (xefm_free if (e_i == 0 and sc == 0) else [])
                    if xe_toks[fc]:
                        xw.append(xe_toks[fc])
                    c2 = vec(lambda e, fc=fc, sc=sc: e.tensor_copy(
                        xe_fm[fc][:, sc * 128:(sc + 1) * 128], pbk[7][:, 0:128]),
                        waits=xw)
                    bk_read(7, c2)
                    xe_toks[fc] = c2
                    last_fc = c2
                xef_free = last_fc
                ck_tok[sc] = last_fc
            gsc_free = g_prev
            gbc = vec(lambda e: e.tensor_copy(growb[0:1, :], grow[0:1, :]),
                      waits=grow_toks)
            grow_free = [gbc]
            gb_toks = []
            off = 0
            for w_ in CAPW:
                w = bk_acq(6, grow_toks + [cst])
                t = mm(pbk[6][:, 0:w_], ones_r1[0:1, :], grow[0:1, off:off + w_],
                       True, True, waits=w, inc="TE")
                gc_ = vec(lambda e, off=off, w_=w_: e.tensor_copy(
                    gbs_sb[:, off:off + w_], pbk[6][:, 0:w_]),
                    waits=[t] + (gbs_free if e_i == 0 and off == 0 else []))
                bk_read(6, gc_)
                gb_toks.append(gc_)
                off += w_
            gbs_free = gb_toks
            for hb in range(NHB):
                b1, s1 = m1s.load(
                    din["l1_w1"][(e_i * NHB + hb) * 128:
                                 (e_i * NHB + hb + 1) * 128, :]
                    .rearrange("p (g c) -> p g c", c=128))
                off = 0
                for ti, w_ in enumerate(CAPW):
                    bi = ti % 2
                    w = bk_acq(bi, [m1s.ld[s1], ck_tok[(off + w_) // 128 - 1]])
                    t = None
                    for fc in range(FCH):
                        t = mm(pbk[bi][:, 0:w_], b1[:, fc, :],
                               xe_fm[fc][:, off:off + w_],
                               start=(fc == 0), stop=(fc == FCH - 1),
                               waits=w if fc == 0 else (),
                               inc="TE" if fc == FCH - 1 else None)
                    rt = act(lambda e, bi=bi, e_i=e_i, hb=hb, w_=w_: e.activation(
                        rl_buf[:, 0:w_], pbk[bi][:, 0:w_], AF.Relu,
                        bias=b1c_sb[:, e_i * NHB + hb:e_i * NHB + hb + 1]),
                        waits=[t, rl_free[0]])
                    bk_read(bi, rt)
                    hw = [rt] + gb_toks
                    if e_i == 0 and ti == 0:
                        hw += qkv_gate + slot_gate + [sA4, sB5, idx_toks["B"]]
                    ht = vec(lambda e, hb=hb, off=off, w_=w_: e.tensor_mul(
                        h_sp[hb][:, off:off + w_], rl_buf[:, 0:w_],
                        gbs_sb[:, off:off + w_]), waits=hw)
                    rl_free[0] = ht
                    h_tok[hb] = ht
                    off += w_
                m1s.release(s1, h_tok[hb])
            for ch in range(2):
                ytiles = []
                off = 0
                for w_ in CAPW:
                    for cb in range(4):
                        ytiles.append((cb, off, w_))
                    off += w_
                for k_, (cb, off, w_) in enumerate(ytiles):
                    bi = k_ % 8
                    w = bk_acq(bi, [gbc, b2t, cstg])
                    mm(pbk[bi][:, 0:w_],
                       b2row[b2i][0:1, ch * 512 + cb * 128:ch * 512 + (cb + 1) * 128],
                       growb[0:1, off:off + w_], True, False, waits=w)
                last = {}
                for hbg in range(NHB // 2):
                    b2w, s2 = m2s.load(
                        din["l1_w2"]
                        [((e_i * 2 + ch) * (NHB // 2) + hbg) * 128:
                         ((e_i * 2 + ch) * (NHB // 2) + hbg + 1) * 128, :]
                        .rearrange("p (g c) -> p g c", c=512))
                    slab_last = None
                    for g2 in range(2):
                        hb = hbg * 2 + g2
                        for k_, (cb, off, w_) in enumerate(ytiles):
                            bi = k_ % 8
                            isend = (hb == NHB - 1)
                            wv = [m2s.ld[s2], h_tok[hb]] if k_ == 0 else ()
                            t = mm(pbk[bi][:, 0:w_],
                                   b2w[:, g2, cb * 128:(cb + 1) * 128],
                                   h_sp[hb][:, off:off + w_], False, isend,
                                   waits=wv,
                                   inc="TE" if isend or (g2 == 1 and
                                                         k_ == len(ytiles) - 1)
                                   else None)
                            if isend:
                                last[k_] = t
                            if g2 == 1 and k_ == len(ytiles) - 1:
                                slab_last = t
                    m2s.release(s2, slab_last)
                yev = {}
                for k_, (cb, off, w_) in enumerate(ytiles):
                    bi = k_ % 8
                    ev = vec(lambda e, cb=cb, off=off, w_=w_, bi=bi: e.tensor_copy(
                        y_fm[cb][:, off:off + w_], pbk[bi][:, 0:w_]),
                        waits=[last[k_]] + (yfm_free[cb]
                                            if isinstance(yfm_free[cb], list)
                                            else [yfm_free[cb]]))
                    bk_read(bi, ev)
                    yev.setdefault(cb, []).append(ev)
                iA = vec(lambda e, ch=ch: e.tensor_scalar(
                    idxA2[:], idxAu[:], 2.0, float(ch), ALU.mult, ALU.add),
                    waits=[idx_toks["A"], iax_free[0]])
                iB = vec(lambda e, ch=ch: e.tensor_scalar(
                    idxB2[:], idxBu[:], 2.0, float(ch), ALU.mult, ALU.add),
                    waits=[idx_toks["B"], iax_free[1]])
                cnv_t = None
                for sc in range(CAPC):
                    yi = sc % 2
                    tcp = []
                    for cb in range(4):
                        cc = vec(lambda e, cb=cb, sc=sc: e.tensor_copy(
                            cnv[:], y_fm[cb][:, sc * 128:(sc + 1) * 128]),
                            waits=yev[cb] + [cnv_t])
                        w = bk_acq(7, [cc])
                        t1 = P.emit("tensor", lambda e: e.transpose(
                            pbk[7][:, 0:128], cnv[:], ident[:]), waits=w, inc="TE")
                        cnv_t = t1
                        c2 = vec(lambda e, cb=cb, yi=yi: e.tensor_copy(
                            ytm[yi][:, cb * 128:(cb + 1) * 128], pbk[7][:, 0:128]),
                            waits=([t1] + ytm_free[yi])
                            if isinstance(ytm_free[yi], list)
                            else [t1, ytm_free[yi]])
                        bk_read(7, c2)
                        tcp.append(c2)
                    col = e_i * CAPC + sc
                    s1_ = P.emit("gpsimd", lambda e, yi=yi, col=col, ch=ch:
                                 e.indirect_dma_start(
                                     y1_dram.rearrange("t (h c) -> (t h) c", h=2),
                                     IOA(idxA2[:, col:col + 1], 0),
                                     ytm[yi][:], None,
                                     bounds_check=bndreg(e, 2 * TPC - 1), oob_is_err=False),
                                 waits=tcp + [iA, ysc_tok], inc="yS", amt=16)
                    s2_ = P.emit("gpsimd", lambda e, yi=yi, col=col, ch=ch:
                                 e.indirect_dma_start(
                                     y2_dram.rearrange("t (h c) -> (t h) c", h=2),
                                     IOA(idxB2[:, col:col + 1], 0),
                                     ytm[yi][:], None,
                                     bounds_check=bndreg(e, 2 * TPC - 1), oob_is_err=False),
                                 waits=[iB, s1_], inc="yS", amt=16)
                    ysc_tok = s2_
                    ytm_free[yi] = [s1_, s2_]
                    iax_free = [s1_, s2_]
                for cb in range(4):
                    yfm_free[cb] = yev[cb][-1] if False else yev[cb]
                for cb in range(4):
                    yfm_free[cb] = [ysc_tok]
            b2r_free[b2i] = ("TE", P.cnt["TE"])
        ys_all = ("yS", P.cnt["yS"])
        cmb_prev = [None, None]
        cnv_t = None
        for tbl in range(NTB):
            xi = tbl % 2
            xw = [ys_all] + ([cmb_prev[xi]] if cmb_prev[xi] else [])
            if not isinstance(xe_free[xi], list):
                xw.append(xe_free[xi])
            l1 = dma("gpsimd", xe_g[xi][:], y1_dram[tbl * 128:(tbl + 1) * 128, :],
                     waits=xw, sem=f"yc{xi}")
            l2 = dma("gpsimd", tm_st[xi][:], y2_dram[tbl * 128:(tbl + 1) * 128, :],
                     waits=[ys_all] + ([cmb_prev[xi]] if cmb_prev[xi] else []),
                     sem=f"yd{xi}")
            for fc in range(FCH):
                c1 = vec(lambda e, xi=xi, fc=fc: e.tensor_copy(
                    cnv[:], xe_g[xi][:, fc * 128:(fc + 1) * 128]),
                    waits=[l1, cnv_t])
                w = bk_acq(4 + (fc % 2), [c1])
                t1 = P.emit("tensor", lambda e, fc=fc: e.transpose(
                    pbk[4 + (fc % 2)][:, 0:128], cnv[:], ident[:]),
                    waits=w, inc="TE")
                c1b = vec(lambda e, xi=xi, fc=fc: e.tensor_copy(
                    cnv[:], tm_st[xi][:, fc * 128:(fc + 1) * 128]),
                    waits=[l2, t1])
                cnv_t = c1b
                w2 = bk_acq(6 + (fc % 2), [c1b])
                t2 = P.emit("tensor", lambda e, fc=fc: e.transpose(
                    pbk[6 + (fc % 2)][:, 0:128], cnv[:], ident[:]),
                    waits=w2, inc="TE")
                a1 = vec(lambda e, fc=fc, tbl=tbl: e.tensor_add(
                    xacc[fc][:, tbl * 128:(tbl + 1) * 128],
                    pbk[4 + (fc % 2)][:, 0:128],
                    xacc[fc][:, tbl * 128:(tbl + 1) * 128]),
                    waits=[t1, st["xacc"][fc]])
                bk_read(4 + (fc % 2), a1)
                a2 = vec(lambda e, fc=fc, tbl=tbl: e.tensor_add(
                    xacc[fc][:, tbl * 128:(tbl + 1) * 128],
                    pbk[6 + (fc % 2)][:, 0:128],
                    xacc[fc][:, tbl * 128:(tbl + 1) * 128]),
                    waits=[t2, a1])
                bk_read(6 + (fc % 2), a2)
                st["xacc"][fc] = a2
                cmb_prev[xi] = a2
        return

    def emit_attn(lid, ck, cv, xn_dead):
        l0 = (lid == 0)
        pre = f"l{lid}_"
        for r in range(4):
            dma("gpsimd", kn4_sb[0:1, r * 8:(r + 1) * 8],
                k_gath[r * (D + 1) + D:r * (D + 1) + D + 1, 0:8],
                waits=[ck], sem="K4")
        kn_toks = [("K4", P.cnt["K4"])]
        t1 = vec(lambda e: e.reduce_max(
            msc[0:1, 0:8],
            kn4_sb.ap()[0:1, 0:32].rearrange("a (r h) -> a h r", r=4), AX.X),
            waits=kn_toks)
        t2 = vec(lambda e: e.tensor_mul(msc[0:1, 0:8], msc[0:1, 0:8],
                                        qn_sb[0:1, 0:8]), waits=[t1])
        t3 = vec(lambda e: e.reduce_max(msc[0:1, 0:1], msc[0:1, 0:8], AX.X),
                 waits=[t2])
        t4 = act(lambda e: e.activation(msc[0:1, 0:1], msc[0:1, 0:1], AF.Ln),
                 waits=[t3])
        t5 = act(lambda e: e.activation(msc[0:1, 0:1], msc[0:1, 0:1], AF.Exp,
                                        scale=0.5), waits=[t4])
        t6 = vec(lambda e: e.tensor_scalar(negM1[0:1, 0:1], msc[0:1, 0:1],
                                           -0.1275, -1.0, ALU.mult, ALU.add),
                 waits=[t5])
        w = bk_acq(6, [t6, cst])
        t7 = mm(pbk[6][:, 0:1], ones_r1[0:1, :], negM1[0:1, 0:1], True, True, w,
                inc="TE")
        negM_tok = vec(lambda e: e.tensor_copy(negM[:], pbk[6][:, 0:1]), waits=[t7])
        bk_read(6, negM_tok)

        unpark = []
        for hp in range(8):
            w = [st["q2_free"][hp], st["h_free"]]
            unpark.append(dma("sync", q2[hp][:], q_park[hp * 128:(hp + 1) * 128, :],
                              waits=w, sem=f"up{hp}"))
        st["qpark_free"] = unpark[-1]

        def jmm(n=1):
            for _ in range(n):
                mm(pbk[5][:, 0:QW], masks_sb[:, 0:128], masks_sb[:, 0:QW],
                   True, True, waits=[cst])

        os_ = st["ostream"]
        gate = st["h_free"]
        os_.gate(gate)
        o_last = None
        kg_last = None
        vg_last = None
        rec_chain = gate
        recb_chain = gate
        attn_gate = gate
        slot_exp = [gate, gate]
        slot_av = [gate, gate]
        for quad in range(4):
            k_toks = {}
            for hpi in range(2):
                hp = quad * 2 + hpi
                for r in range(4):
                    ki = hpi * 4 + r
                    w = [ck] + ([st["B_free"][ki]] if st["B_free"][ki] else [])
                    if quad == 0:
                        w += xn_dead
                    t = dma("sync", kwin[ki][:],
                            k_gath[r * (D + 1) + hp * 128:
                                   r * (D + 1) + (hp + 1) * 128, :], waits=w,
                            sem=f"kw{ki}")
                    k_toks[(hpi, r)] = t
                    kg_last = t
            v_toks = {}
            for r in range(4):
                w = [cv, attn_gate]
                t = dma("sync", vw[r][:],
                        v_gath.rearrange("(r q t p) c -> p r q t c", r=4, q=4, p=128)
                        [:, r, quad, :, :], waits=w, sem=f"vq{r}")
                v_toks[r] = t
                vg_last = t
            oz_toks = []
            z_free = [attn_gate, attn_gate]
            for hpi in range(2):
                hp = quad * 2 + hpi
                hp_qk_last = None
                for hi in range(2):
                    hloc = 2 * hpi + hi
                    rlo, rhi = hi * 64, (hi + 1) * 64
                    for qb in range(NQB):
                        diag_kb = qb * QW // 128
                        if l0:
                            kl = [(r, kb) for r in range(4)
                                  for kb in range(diag_kb + 1)]
                        else:
                            kl = [(r, kb) for r in range(4) for kb in range(KTPR)]
                        pairs = [kl[i:i + 2] for i in range(0, len(kl), 2)]
                        np_ = len(pairs)
                        qk_tok = [None] * np_

                        def emit_qk(p_i):
                            pr = pairs[p_i]
                            slot = p_i % 2
                            for j, (r, kb) in enumerate(pr):
                                ww = [k_toks[(hpi, r)]]
                                if j == 0:
                                    ww += [slot_exp[slot], unpark[hp]]
                                qk_tok[p_i] = mm(
                                    pq[slot][:, j * QW:(j + 1) * QW],
                                    kwin[hpi * 4 + r][rlo:rhi,
                                                      kb * 128:(kb + 1) * 128],
                                    q2[hp][rlo:rhi, qb * QW:(qb + 1) * QW],
                                    True, True, waits=ww, inc="TE")

                        emit_qk(0)
                        zlast_t = None
                        for i in range(np_):
                            jmm(1)
                            if i + 1 < np_:
                                emit_qk(i + 1)
                            pr = pairs[i]
                            slot = i % 2
                            W = len(pr) * QW
                            et = act(lambda e, slot=slot, W=W: e.activation(
                                p_buf[slot][:, 0:W], pq[slot][:, 0:W], AF.Exp,
                                bias=negM[:, 0:1], scale=0.125),
                                waits=[qk_tok[i], negM_tok, slot_av[slot]])
                            slot_exp[slot] = et
                            half_tok = [et] * len(pr)
                            for j, (r, kb) in enumerate(pr):
                                if l0 and kb == diag_kb:
                                    half_tok[j] = vec(
                                        lambda e, slot=slot, j=j, r=r:
                                        e.tensor_mul(
                                            p_buf[slot][:, j * QW:(j + 1) * QW],
                                            p_buf[slot][:, j * QW:(j + 1) * QW],
                                            masks_sb[:, r * QW:(r + 1) * QW]),
                                        waits=[et, cst])
                            for j, (r, kb) in enumerate(pr):
                                first = (i == 0 and j == 0)
                                last = (i == np_ - 1 and j == len(pr) - 1)
                                w2 = [half_tok[j], v_toks[r]]
                                if first:
                                    w2 = bk_acq(4, w2)
                                zt = mm(pbk[4][0:65, 0:QW],
                                        vw[r][:, kb, hloc * 65:(hloc + 1) * 65],
                                        p_buf[slot][:, j * QW:(j + 1) * QW],
                                        start=first, stop=last,
                                        waits=w2, inc="TE")
                                slot_av[slot] = zt
                                if last:
                                    zlast_t = zt
                        hp_qk_last = qk_tok[np_ - 1]
                        jmm(6)
                        e1 = act(lambda e: e.activation(
                            recrow[0:1, 0:QW], pbk[4][64:65, 0:QW], AF.Ln),
                            waits=[zlast_t, rec_chain])
                        e2 = act(lambda e: e.activation(
                            recrow[0:1, 0:QW], recrow[0:1, 0:QW], AF.Exp,
                            scale=-1.0), waits=[e1])
                        w = bk_acq(6, [e2])
                        e3 = mm(pbk[6][0:64, 0:QW], ones_r1[0:1, 0:64],
                                recrow[0:1, 0:QW], True, True, w, inc="TE")
                        rec_chain = e3
                        e4 = vec(lambda e: e.tensor_copy(recb[:, 0:QW],
                                                         pbk[6][0:64, 0:QW]),
                                 waits=[e3, recb_chain])
                        bk_read(6, e4)
                        e5 = vec(lambda e, hpi=hpi, hi=hi, qb=qb: e.tensor_mul(
                            z_sb[hpi][hi * 64:(hi + 1) * 64,
                                      qb * QW:(qb + 1) * QW],
                            pbk[4][0:64, 0:QW], recb[:, 0:QW]),
                            waits=[e4, zlast_t, z_free[hpi]])
                        bk_read(4, e5)
                        recb_chain = e5
                        oz_toks.append(e5)
                st["q2_free"][hp] = hp_qk_last
            buf0, s0 = os_.load(din[pre + "wo"]
                                [(quad * 2) * 128:(quad * 2 + 1) * 128, :])
            buf1, s1 = os_.load(din[pre + "wo"]
                                [(quad * 2 + 1) * 128:(quad * 2 + 2) * 128, :])
            for ob in range(FCH):
                for tbl in range(NCB):
                    bo_i = 6 + tbl % 2
                    col0 = tbl * CB
                    w = bk_acq(bo_i, [os_.ld[s0], os_.ld[s1]] + oz_toks)
                    mm(pbk[bo_i][:, 0:CB], buf0[:, ob * 128:(ob + 1) * 128],
                       z_sb[0][:, col0:col0 + CB], True, False, w)
                    t = mm(pbk[bo_i][:, 0:CB], buf1[:, ob * 128:(ob + 1) * 128],
                           z_sb[1][:, col0:col0 + CB], False, True, (), inc="TE")
                    if (not l0) and quad == 0:
                        at = vec(lambda e, ob=ob, col0=col0, bo_i=bo_i:
                                 e.scalar_tensor_tensor(
                                     xacc[ob][:, col0:col0 + CB],
                                     pbk[bo_i][:, 0:CB], bo_sb[:, ob:ob + 1],
                                     xacc[ob][:, col0:col0 + CB],
                                     ALU.add, ALU.add),
                                 waits=[t, st["xacc"][ob], cst])
                    else:
                        at = vec(lambda e, ob=ob, col0=col0, bo_i=bo_i:
                                 e.tensor_add(xacc[ob][:, col0:col0 + CB],
                                              pbk[bo_i][:, 0:CB],
                                              xacc[ob][:, col0:col0 + CB]),
                                 waits=[t, st["xacc"][ob]])
                    bk_read(bo_i, at)
                    st["xacc"][ob] = at
                    o_last = at
            os_.release(s0, o_last)
            os_.release(s1, o_last)
            for ki in range(8):
                st["B_free"][ki] = o_last
            attn_gate = o_last
        for s_i in range(2):
            P.bank_r[2 * s_i] = slot_exp[s_i]
            P.bank_r[2 * s_i + 1] = slot_exp[s_i]
        st["kgath_free"] = kg_last
        st["vgath_free"] = vg_last
        return o_last

    for lid in layers:
        xn_tok = emit_ln(lid)
        ck, cv = emit_qkv(lid, xn_tok)
        xn_tm_toks = []
        if lid == 1 and cfg.sparse_moe:
            stga_last = [t for fl in st["stagerQ"].free for t in fl] \
                if st["stagerQ"] else []
            cnv_tok = ([st["arena_free"]] if st["arena_free"] else []) \
                + st["contrib_done"] + tmax(*stga_last)
            tm_free = [None, None]
            for tb in range(NTB):
                si = tb % 2
                cp_toks = []
                for fc in range(FCH):
                    c1 = vec(lambda e, fc=fc, tb=tb: e.tensor_copy(
                        cnv[:], xn[fc][:, tb * 128:(tb + 1) * 128]),
                        waits=xn_tok + (cnv_tok if isinstance(cnv_tok, list)
                                        else [cnv_tok]))
                    w = bk_acq(7, [c1])
                    t1 = P.emit("tensor", lambda e: e.transpose(
                        pbk[7][:, 0:128], cnv[:], ident[:]), waits=w, inc="TE")
                    cnv_tok = [t1]
                    c2 = vec(lambda e, fc=fc, si=si: e.tensor_copy(
                        tm_st[si][:, fc * 128:(fc + 1) * 128], pbk[7][:, 0:128]),
                        waits=[t1, tm_free[si]] if tm_free[si] else [t1])
                    bk_read(7, c2)
                    cp_toks.append(c2)
                dt_ = dma("sync", xn_tm[tb * 128:(tb + 1) * 128, :], tm_st[si][:],
                          waits=cp_toks, sem=f"tm{si}")
                tm_free[si] = dt_
                xn_tm_toks.append(dt_)
        qkv_gate = [ck, cv]
        if lid == 0:
            emit_ffn_l0(xn_tok, st["contrib_done"])
        elif cfg.sparse_moe:
            emit_moe_sparse(xn_tok, st["contrib_done"], xn_tm_toks)
        else:
            emit_moe(xn_tok, qkv_gate)
        lastacc = max(st["xacc"], key=lambda t: t[1])
        st["h_free"] = lastacc
        if f"mid{lid}" in taps:
            for fc in range(FCH):
                dma("sync", taps[f"mid{lid}"][fc * 128:(fc + 1) * 128, :],
                    xacc[fc][:], waits=[st["xacc"][fc]], sem="OUT")
            ttok = ("OUT", P.cnt["OUT"])
            st["xacc"] = [ttok] * FCH
        xn_dead = [lastacc]
        o_last = emit_attn(lid, ck, cv, xn_dead)
        st["arena_free"] = o_last
        if f"x{lid}" in taps:
            for fc in range(FCH):
                dma("sync", taps[f"x{lid}"][fc * 128:(fc + 1) * 128, :], xacc[fc][:],
                    waits=[st["xacc"][fc]], sem="OUT")

    for fc in range(FCH):
        dma("sync", out_fm[fc * 128:(fc + 1) * 128, :], xacc[fc][:],
            waits=[st["xacc"][fc]], sem="OUT")
    final_v = P.cnt["OUT"]
    P.emit("sync", lambda e: e.nop(), waits=[("OUT", final_v)])

    import contextlib
    ctx = contextlib.ExitStack()
    sems = {s: ctx.enter_context(nc.semaphore(s)) for s in sorted(P.sems)}
    runners = P.replay_into(sems)
    with nc.Block() as blk:
        blk.tensor(lambda e: runners["tensor"](e))
        blk.vector(lambda e: runners["vector"](e))
        blk.scalar(lambda e: runners["scalar"](e))
        blk.sync(lambda e: runners["sync"](e))
        blk.gpsimd(lambda e: runners["gpsimd"](e))
    ctx.close()
    return nc


_BUILD_CACHE = {}


def kernel(**inputs):
    cfg = CFG()
    in_maps, idxs = host_prepare(cfg, inputs)
    key = (cfg.S, cfg.FH, cfg.MH, cfg.layers)
    if key not in _BUILD_CACHE:
        _BUILD_CACHE[key] = build_program(cfg, cfg.layers)
    nc = _BUILD_CACHE[key]
    res = run_bass_kernel_spmd(nc, in_maps, core_ids=list(range(8)))
    return host_assemble(cfg, res.results, idxs)
